# version 06
# baseline (speedup 1.0000x reference)
import contextlib
import numpy as np

import concourse.bass as bass
import concourse.mybir as mybir
from concourse.bass_utils import run_bass_kernel_spmd

F32 = mybir.dt.float32
BF16 = mybir.dt.bfloat16
I32 = mybir.dt.int32
U8 = mybir.dt.uint8
ALU = mybir.AluOpType
ACT = mybir.ActivationFunctionType
AX = mybir.AxisListType

C2 = 256.0 / (2.0 * np.log(2.0))

FQ_THR = [(16.0, ALU.is_gt), (48.0, ALU.is_ge), (80.0, ALU.is_gt), (112.0, ALU.is_ge),
          (144.0, ALU.is_gt), (176.0, ALU.is_ge), (208.0, ALU.is_gt), (240.0, ALU.is_ge)]
COMP_THR = [(16.0, ALU.is_gt), (48.0, ALU.is_ge), (80.0, ALU.is_gt), (112.0, ALU.is_ge),
            (144.5, ALU.is_gt), (175.5, ALU.is_ge), (208.5, ALU.is_gt), (239.5, ALU.is_ge)]


def ap(tensor, offset, dims):
    return bass.AP(tensor, offset, [list(d) for d in dims])


class Builder:
    ENGINES = ("sync", "scalar", "vector", "tensor", "gpsimd")

    def __init__(self, nc, es):
        self.nc = nc
        self.ops = {e: [] for e in self.ENGINES}
        self.sems = {}
        self.counts = {}
        for e in self.ENGINES:
            self.sems[e] = es.enter_context(nc.semaphore(f"s_{e}"))
            self.counts[e] = 0

    def op(self, eng, fn, waits=(), inc_k=1):
        wl = [(w, self.counts[w]) for w in waits if self.counts[w] > 0]
        self.ops[eng].append((fn, wl, inc_k))
        self.counts[eng] += inc_k

    def dma(self, eng, out_ap, in_ap, waits=(), **kw):
        self.op(eng, (lambda E, o=out_ap, i=in_ap, kw=kw: E.dma_start(o, i, **kw)),
                waits=waits, inc_k=16)

    def finalize(self, block):
        sems = self.sems

        def mk(ename):
            ops = self.ops[ename]

            def emit(E):
                for fn, wl, inck in ops:
                    for (w, v) in wl:
                        E.wait_ge(sems[w], v)
                    inst = fn(E)
                    if inck:
                        inst.then_inc(sems[ename], inck)
            return emit
        block.sync(mk("sync"))
        block.scalar(mk("scalar"))
        block.vector(mk("vector"))
        block.tensor(mk("tensor"))
        block.gpsimd(mk("gpsimd"))


def emit_quant(b, t, y_ap, out_ap, width, mode, waits_sc=(), waits_v=(), out_waits=()):
    W = width
    sq, lg = t["Asq"], t["Alg"]
    u, i256, eint, e2int = t["Au"], t["Ai256"], t["Aeint"], t["Ae2int"]
    shS, shI, stepf, invf = t["AshS"], t["AshI"], t["Astepf"], t["Ainvf"]
    q, L = t["Aq"], t["AL"]
    b.op("scalar", lambda E: E.activation(sq[:, :W], y_ap, ACT.Square),
         waits=tuple(waits_sc) + ("vector",))
    b.op("scalar", lambda E: E.activation(lg[:, :W], sq[:, :W], ACT.Ln,
                                          bias=t["tinyb"][:, :], scale=1.0))
    if mode == "rm":
        b.op("vector", lambda E: E.tensor_scalar(u[:, :W], lg[:, :W], C2, 20480.0,
                                                 ALU.mult, ALU.add),
             waits=("scalar",) + tuple(waits_v))
        b.op("vector", lambda E: E.tensor_single_scalar(u[:, :W], u[:, :W], 12288.0, ALU.max))
        b.op("vector", lambda E: E.tensor_copy(i256[:, :W], u[:, :W]))
        b.op("vector", lambda E: E.tensor_single_scalar(eint[:, :W], i256[:, :W], 8,
                                                        ALU.arith_shift_right))
        b.op("vector", lambda E: E.tensor_single_scalar(eint[:, :W], eint[:, :W], -64, ALU.add))
        b.op("vector", lambda E: E.tensor_single_scalar(e2int[:, :W], eint[:, :W], 16, ALU.add))
        b.op("vector", lambda E: E.tensor_single_scalar(shS[:, :W], e2int[:, :W], 1,
                                                        ALU.arith_shift_right))
        b.op("vector", lambda E: E.tensor_tensor(shI[:, :W], e2int[:, :W], shS[:, :W],
                                                 ALU.subtract))
        b.op("vector", lambda E: E.tensor_tensor(shS[:, :W], t["onei"][:, :W], shS[:, :W],
                                                 ALU.logical_shift_left))
        b.op("vector", lambda E: E.tensor_tensor(shI[:, :W], t["onei"][:, :W], shI[:, :W],
                                                 ALU.logical_shift_left))
        b.op("vector", lambda E: E.tensor_copy(stepf[:, :W], shS[:, :W]))
        b.op("vector", lambda E: E.tensor_copy(q[:, :W], shI[:, :W]))
        b.op("vector", lambda E: E.tensor_tensor(stepf[:, :W], stepf[:, :W], q[:, :W],
                                                 ALU.mult))
        b.op("vector", lambda E: E.tensor_tensor(e2int[:, :W], t["thirty1"][:, :W], eint[:, :W],
                                                 ALU.subtract))
        b.op("vector", lambda E: E.tensor_single_scalar(shS[:, :W], e2int[:, :W], 1,
                                                        ALU.arith_shift_right))
        b.op("vector", lambda E: E.tensor_tensor(shI[:, :W], e2int[:, :W], shS[:, :W],
                                                 ALU.subtract))
        b.op("vector", lambda E: E.tensor_tensor(shS[:, :W], t["onei"][:, :W], shS[:, :W],
                                                 ALU.logical_shift_left))
        b.op("vector", lambda E: E.tensor_tensor(shI[:, :W], t["onei"][:, :W], shI[:, :W],
                                                 ALU.logical_shift_left))
        b.op("vector", lambda E: E.tensor_copy(invf[:, :W], shS[:, :W]))
        b.op("vector", lambda E: E.tensor_copy(q[:, :W], shI[:, :W]))
        b.op("vector", lambda E: E.tensor_tensor(invf[:, :W], invf[:, :W], q[:, :W],
                                                 ALU.mult))
    else:
        b.op("vector", lambda E: E.tensor_scalar(u[:, :W], lg[:, :W], C2, 4096.0,
                                                 ALU.mult, ALU.add),
             waits=("scalar",) + tuple(waits_v))
        b.op("vector", lambda E: E.tensor_single_scalar(u[:, :W], u[:, :W], 512.0, ALU.max))
        b.op("vector", lambda E: E.tensor_copy(i256[:, :W], u[:, :W]))
        b.op("vector", lambda E: E.tensor_single_scalar(eint[:, :W], i256[:, :W], 8,
                                                        ALU.arith_shift_right))
        b.op("vector", lambda E: E.tensor_tensor(shS[:, :W], t["onei"][:, :W], eint[:, :W],
                                                 ALU.logical_shift_left))
        b.op("vector", lambda E: E.tensor_tensor(e2int[:, :W], eint[:, :W], t["thirty1"][:, :W],
                                                 ALU.bitwise_xor))
        b.op("vector", lambda E: E.tensor_tensor(shI[:, :W], t["onei"][:, :W], e2int[:, :W],
                                                 ALU.logical_shift_left))
        b.op("vector", lambda E: E.tensor_copy(stepf[:, :W], shS[:, :W]))
        b.op("vector", lambda E: E.tensor_copy(invf[:, :W], shI[:, :W]))
    b.op("vector", lambda E: E.scalar_tensor_tensor(q[:, :W], invf[:, :W], 2.0 ** -8, y_ap,
                                                    ALU.mult, ALU.mult))
    if mode == "comp":
        (t0, o0) = COMP_THR[0]
        b.op("vector", lambda E: E.tensor_single_scalar(L[:, :W], q[:, :W], t0, o0))
        for (tv, ov) in COMP_THR[1:]:
            b.op("vector", lambda E, tv=tv, ov=ov: E.scalar_tensor_tensor(
                L[:, :W], q[:, :W], tv, L[:, :W], ov, ALU.add))
        b.op("vector", lambda E: E.scalar_tensor_tensor(
            out_ap, L[:, :W], 2.0 ** -18, stepf[:, :W], ALU.mult, ALU.mult), waits=out_waits)
        return
    nq, aq, sg = t["Anq"], t["Aaq"], t["Asg"]
    b.op("vector", lambda E: E.tensor_single_scalar(nq[:, :W], q[:, :W], -1.0, ALU.mult))
    b.op("vector", lambda E: E.tensor_tensor(aq[:, :W], q[:, :W], nq[:, :W], ALU.max))
    if mode == "fq":
        (t0, o0) = FQ_THR[0]
        b.op("vector", lambda E: E.tensor_single_scalar(L[:, :W], aq[:, :W], t0, o0))
        for (tv, ov) in FQ_THR[1:]:
            b.op("vector", lambda E, tv=tv, ov=ov: E.scalar_tensor_tensor(
                L[:, :W], aq[:, :W], tv, L[:, :W], ov, ALU.add))
        res, scl = L, 2.0 ** -18
    else:
        r, fl, b1, t1, tie = t["Ar"], t["Afl"], t["Ab1"], t["At1"], t["Atie"]
        b.op("vector", lambda E: E.tensor_single_scalar(r[:, :W], aq[:, :W], -128.0, ALU.add))
        for w in (64.0, 32.0, 16.0, 8.0, 4.0, 2.0):
            b.op("vector", lambda E, w=w: E.tensor_single_scalar(t1[:, :W], r[:, :W], w, ALU.is_ge))
            b.op("vector", lambda E, w=w: E.scalar_tensor_tensor(
                r[:, :W], t1[:, :W], -w, r[:, :W], ALU.mult, ALU.add))
        b.op("vector", lambda E: E.tensor_single_scalar(b1[:, :W], r[:, :W], 1.0, ALU.is_ge))
        b.op("vector", lambda E: E.scalar_tensor_tensor(
            r[:, :W], b1[:, :W], -1.0, r[:, :W], ALU.mult, ALU.add))
        b.op("vector", lambda E: E.tensor_tensor(fl[:, :W], aq[:, :W], r[:, :W], ALU.subtract))
        b.op("vector", lambda E: E.tensor_single_scalar(t1[:, :W], r[:, :W], 0.5, ALU.is_gt))
        b.op("vector", lambda E: E.tensor_tensor(fl[:, :W], fl[:, :W], t1[:, :W], ALU.add))
        b.op("vector", lambda E: E.tensor_single_scalar(tie[:, :W], r[:, :W], 0.5, ALU.is_equal))
        b.op("vector", lambda E: E.tensor_tensor(tie[:, :W], tie[:, :W], b1[:, :W], ALU.mult))
        b.op("vector", lambda E: E.tensor_tensor(fl[:, :W], fl[:, :W], tie[:, :W], ALU.add))
        b.op("vector", lambda E: E.tensor_single_scalar(t1[:, :W], aq[:, :W], 1.0, ALU.is_ge))
        b.op("vector", lambda E: E.tensor_tensor(fl[:, :W], fl[:, :W], t1[:, :W], ALU.mult))
        res, scl = fl, 2.0 ** -39
    b.op("vector", lambda E: E.tensor_scalar(sg[:, :W], q[:, :W], 0.0, 2.0, ALU.is_ge, ALU.mult))
    b.op("vector", lambda E: E.tensor_single_scalar(sg[:, :W], sg[:, :W], -1.0, ALU.add))
    b.op("vector", lambda E, res=res: E.tensor_tensor(res[:, :W], res[:, :W], sg[:, :W], ALU.mult))
    b.op("vector", lambda E, res=res, scl=scl: E.scalar_tensor_tensor(
        out_ap, res[:, :W], scl, stepf[:, :W], ALU.mult, ALU.mult), waits=out_waits)


def build_nc(L=4096, debug=False, skip_b=False, skip_a=False, astage=99.0):
    NCH = L // 128
    NG = L // 512
    nc = bass.Bass("TRN2", target_bir_lowering=False, detect_race_conditions=False)

    x_h = nc.dram_tensor("x", [L, 1024], F32, kind="ExternalInput")
    wsel_h = nc.dram_tensor("wsel", [1024, 96], F32, kind="ExternalInput")
    dtw_h = nc.dram_tensor("dtw", [32, 512], F32, kind="ExternalInput")
    dtb_h = nc.dram_tensor("dtb", [128, 4], F32, kind="ExternalInput")
    wk_h = nc.dram_tensor("wk", [1024, 512], F32, kind="ExternalInput")
    wv_h = nc.dram_tensor("wv", [1024, 512], F32, kind="ExternalInput")
    wr_h = nc.dram_tensor("wr", [1024, 512], F32, kind="ExternalInput")
    wg_h = nc.dram_tensor("wg", [1024, 512], F32, kind="ExternalInput")
    wo_h = nc.dram_tensor("wo", [512, 1024], F32, kind="ExternalInput")
    bd_h = nc.dram_tensor("bd", [128, 4], F32, kind="ExternalInput")
    tf_h = nc.dram_tensor("tf", [128, 4], F32, kind="ExternalInput")
    w1_h = nc.dram_tensor("w1", [4096, 1024], F32, kind="ExternalInput")
    b1_h = nc.dram_tensor("b1", [128, 32], F32, kind="ExternalInput")
    w2_h = nc.dram_tensor("w2", [1024, 4096], F32, kind="ExternalInput")
    b2_h = nc.dram_tensor("b2", [128, 8], F32, kind="ExternalInput")
    maskt_h = nc.dram_tensor("maskt", [1024, L], BF16, kind="ExternalInput")
    identf_h = nc.dram_tensor("identf", [128, 128], F32, kind="ExternalInput")
    identb_h = nc.dram_tensor("identb", [128, 128], BF16, kind="ExternalInput")
    utm_h = nc.dram_tensor("utm", [128, 128], U8, kind="ExternalInput")
    out_h = nc.dram_tensor("out", [1024, L], F32, kind="ExternalOutput")
    xnq_s = nc.dram_tensor("xnq_s", [1024, L], BF16)
    y2_s = nc.dram_tensor("y2_s", [1024, L], F32)
    y2_s1 = nc.dram_tensor("y2_s1", [1024, L], F32)

    dbg = {}

    def mkdbg(name, shape, dt=F32):
        if name not in dbg:
            dbg[name] = nc.dram_tensor("dbg_" + name, list(shape), dt, kind="ExternalOutput")
        return dbg[name]

    es = contextlib.ExitStack()
    with es:
        b = Builder(nc, es)
        t = {}

        def dbg_dump(name, src_ap, shape, dt=F32, when=True):
            if debug and when:
                h = mkdbg(name, shape, dt)
                b.dma("gpsimd", h[:, :], src_ap, waits=("vector", "scalar", "tensor"))
                b.op("vector", lambda E: E.memset(t["zero"][:1, :1], 0.0), waits=("gpsimd",))

        def sb(name, shape, dt=F32, stack=es):
            t[name] = stack.enter_context(nc.sbuf_tensor(name, list(shape), dt))
            return t[name]

        def psum(name, shape, dt=F32, stack=es):
            t[name] = stack.enter_context(nc.psum_tensor(name, list(shape), dt))
            return t[name]

        sb("onei", [128, 512], I32)
        sb("sidentf", [128, 128]); sb("sidentb", [128, 128], BF16)
        sb("sutm", [128, 128], U8)
        sb("zero", [128, 128], F32)
        sb("sdtb", [128, 4]); sb("sbd", [128, 4]); sb("bdneg", [128, 4]); sb("stf", [128, 4])
        sb("b1r", [128, 32]); sb("b2r", [128, 8])
        sb("b1qT", [32, 128], BF16); sb("b2qT", [8, 128], BF16)
        sb("wstage", [128, 1024], F32)
        sb("ys", [128, 4 * L], BF16)
        sb("Asq", [128, 1024])
        for nm in ("Alg", "Au", "Astepf", "Ainvf", "Aq", "AL", "Anq", "Aaq",
                   "Asg", "Ar", "Afl", "Ab1", "At1", "Atie"):
            sb(nm, [128, 512])
        sb("Ai256", [128, 512], I32); sb("Aeint", [128, 512], I32)
        sb("thirty1", [128, 512], I32)
        sb("minus1f", [128, 512], F32)
        sb("Ae2int", [128, 512], I32); sb("AshS", [128, 512], I32); sb("AshI", [128, 512], I32)
        sb("eps5", [128, 1], F32)
        sb("tinyb", [128, 1], F32)


        with nc.Block() as block:
            esA = contextlib.ExitStack()
            with esA:
                sb("wkb", [128, 4096], BF16, esA); sb("wvb", [128, 4096], BF16, esA)
                sb("wrb", [128, 4096], BF16, esA); sb("wgb", [128, 4096], BF16, esA)
                sb("wselb", [128, 768], BF16, esA); sb("dtwb", [32, 512], BF16, esA)
                sb("xbuf", [128, 1024], F32, esA)
                sb("xc", [128, 1024], F32, esA)
                sb("red", [128, 1], F32, esA); sb("mu", [128, 1], F32, esA)
                sb("var", [128, 1], F32, esA); sb("rstd", [128, 1], F32, esA)
                sb("musq", [128, 1], F32, esA)
                sb("xn", [128, 1024], F32, esA)
                sb("xnfmb", [128, 1024], BF16, esA)
                sb("xnqb", [128, 1024], BF16, esA)
                sb("kev", [128, 512], F32, esA)
                sb("Pfull", [128, 512], F32, esA)
                sb("ell", [128, 512], F32, esA)
                sb("ez", [128, 512], F32, esA)
                sb("delta", [128, 512], F32, esA)
                sb("Pmid", [128, 32], F32, esA); sb("Pb64", [128, 4], F32, esA)
                sb("Pb32", [128, 8], F32, esA); sb("Pb16", [128, 16], F32, esA)
                sb("e1arg", [128, 512], F32, esA)
                sb("s64", [128, 512], F32, esA); sb("s32", [128, 512], F32, esA)
                sb("s16", [128, 512], F32, esA); sb("farg", [128, 512], F32, esA)
                sb("expE", [128, 512], F32, esA); sb("expF", [128, 512], F32, esA)
                sb("expE1", [128, 512], F32, esA); sb("expE2", [128, 512], F32, esA)
                sb("xq64", [128, 256], F32, esA); sb("xk64", [128, 256], F32, esA)
                sb("xq32", [128, 256], F32, esA); sb("xk32", [128, 256], F32, esA)
                sb("xq16", [128, 256], F32, esA); sb("xk16", [128, 256], F32, esA)
                sb("qtb", [128, 512], BF16, esA); sb("khf", [128, 512], F32, esA)
                sb("qtbB", [64, 512], BF16, esA)
                sb("ke1B", [64, 512], BF16, esA); sb("ke2B", [64, 512], BF16, esA)
                sb("kq64B", [64, 256], BF16, esA); sb("kk64B", [64, 256], BF16, esA)
                sb("kq32B", [64, 256], BF16, esA); sb("kk32B", [64, 256], BF16, esA)
                sb("kq16B", [64, 256], BF16, esA); sb("kk16B", [64, 256], BF16, esA)
                sb("SbB", [64, 256], BF16, esA)
                sb("khtm", [128, 512], BF16, esA)
                sb("ke1", [128, 512], BF16, esA); sb("ke2", [128, 512], BF16, esA)
                sb("kq64b", [128, 256], BF16, esA); sb("kk64b", [128, 256], BF16, esA)
                sb("kq32b", [128, 256], BF16, esA); sb("kk32b", [128, 256], BF16, esA)
                sb("kq16b", [128, 256], BF16, esA); sb("kk16b", [128, 256], BF16, esA)
                sb("vpb", [128, 512], BF16, esA); sb("vrawf", [128, 512], F32, esA)
                sb("vfm", [128, 512], F32, esA)
                sb("btok", [128, 1], F32, esA)
                sb("sigr", [128, 512], F32, esA); sb("sigg", [128, 512], F32, esA)
                sb("rg", [128, 512], F32, esA)
                sb("Amb", [128, 1024], BF16, esA)
                sb("kv", [128, 512], F32, esA); sb("bonus", [128, 512], F32, esA)
                sb("dtrawb", [32, 128], BF16, esA)
                sb("Stile", [128, 256], F32, esA)
                sb("Sb", [128, 256], BF16, esA)
                sb("yst", [128, 512], F32, esA)
                psum("pA", [128, 512], F32, esA)
                psum("pB", [128, 512], F32, esA)
                psum("pD", [128, 512], F32, esA)
                psum("pE", [128, 512], F32, esA)
                psum("pF", [128, 512], F32, esA)
                psum("pG", [128, 512], F32, esA)
                psum("pH", [128, 512], F32, esA)
                

                b.dma("sync", t["sidentf"][:, :], identf_h[:, :])
                b.dma("sync", t["sidentb"][:, :], identb_h[:, :])
                b.dma("sync", t["sutm"][:, :], utm_h[:, :])
                b.dma("sync", t["sdtb"][:, :], dtb_h[:, :])
                b.dma("sync", t["sbd"][:, :], bd_h[:, :])
                b.dma("sync", t["stf"][:, :], tf_h[:, :])
                b.dma("sync", t["b1r"][:, :], b1_h[:, :])
                b.dma("sync", t["b2r"][:, :], b2_h[:, :])
                b.dma("sync", t["wstage"][:32, :512], dtw_h[:, :])
                b.op("vector", lambda E: E.memset(t["onei"][:, :], 1), waits=("sync",))
                b.op("vector", lambda E: E.memset(t["thirty1"][:, :], 31))
                b.op("vector", lambda E: E.memset(t["minus1f"][:, :], -1.0))
                b.op("vector", lambda E: E.memset(t["eps5"][:, :], 1e-5))
                b.op("vector", lambda E: E.memset(t["tinyb"][:, :], 2.0 ** -120))
                b.op("vector", lambda E: E.memset(t["zero"][:, :], 0.0))
                b.op("vector", lambda E: E.memset(t["Stile"][:, :], 0.0))
                b.op("vector", lambda E: E.tensor_single_scalar(t["bdneg"][:, :], t["sbd"][:, :], -1.0, ALU.mult))
                b.op("vector", lambda E: E.tensor_copy(t["dtwb"][:, :], t["wstage"][:32, :512]))
                for (wh, wb) in ((wk_h, "wkb"), (wv_h, "wvb"), (wr_h, "wrb"), (wg_h, "wgb")):
                    for fc in range(8):
                        b.dma("sync", t["wstage"][:, :512], wh[128 * fc:128 * (fc + 1), :],
                              waits=("vector",))
                        b.op("vector", lambda E, wb=wb, fc=fc: E.tensor_copy(
                            t[wb][:, 512 * fc:512 * (fc + 1)], t["wstage"][:, :512]),
                            waits=("sync",))
                for fc in range(8):
                    b.dma("sync", t["wstage"][:, :96], wsel_h[128 * fc:128 * (fc + 1), :],
                          waits=("vector",))
                    b.op("vector", lambda E, fc=fc: E.tensor_copy(
                        t["wselb"][:, 96 * fc:96 * (fc + 1)], t["wstage"][:, :96]),
                        waits=("sync",))
                b.op("vector", lambda E: E.tensor_copy(
                    t["At1"][:, :512], ap(t["b1r"], 0, [(32, 128), (0, 16), (1, 32)])),
                    waits=("sync",))
                b.op("vector", lambda E: E.tensor_copy(
                    t["Atie"][:, :512], ap(t["b2r"], 0, [(8, 128), (0, 64), (1, 8)])))
                emit_quant(b, t, t["At1"][:, :512], t["Ar"][:, :512], 512, "fq")
                emit_quant(b, t, t["Atie"][:, :512], t["Afl"][:, :512], 512, "fq")
                b.op("tensor", lambda E: E.transpose(t["pF"][:32, :128], t["Ar"][:, :32],
                                                     t["sidentf"][:, :]), waits=("vector",))
                b.op("vector", lambda E: E.tensor_copy(t["b1qT"][:, :], t["pF"][:32, :128]),
                     waits=("tensor",))
                b.op("tensor", lambda E: E.transpose(t["pF"][:8, :128], t["Afl"][:, :8],
                                                     t["sidentf"][:, :]), waits=("vector",))
                b.op("vector", lambda E: E.tensor_copy(t["b2qT"][:, :], t["pF"][:8, :128]),
                     waits=("tensor",))

                for n in range(NCH if not skip_a else 0):
                    tok = slice(128 * n, 128 * (n + 1))
                    b.dma("sync", t["xbuf"][:, :], x_h[tok, :], waits=("vector",))
                    b.op("vector", lambda E: E.reduce_sum(t["red"][:, :], t["xbuf"][:, :], AX.X),
                         waits=("sync",))
                    b.op("vector", lambda E: E.tensor_tensor(t["Asq"][:, :1024], t["xbuf"][:, :],
                                                             t["xbuf"][:, :], ALU.mult))
                    b.op("vector", lambda E: E.tensor_single_scalar(t["mu"][:, :], t["red"][:, :],
                                                                    1.0 / 1024.0, ALU.mult))
                    b.op("vector", lambda E: E.reduce_sum(t["red"][:, :], t["Asq"][:, :1024], AX.X))
                    b.op("vector", lambda E: E.tensor_tensor(t["xc"][:, :], t["xbuf"][:, :],
                         ap(t["mu"], 0, [(1, 128), (0, 1024)]), ALU.subtract))
                    b.op("scalar", lambda E: E.activation(t["musq"][:, :], t["mu"][:, :],
                                                          ACT.Square), waits=("vector",))
                    b.op("vector", lambda E: E.scalar_tensor_tensor(t["var"][:, :], t["red"][:, :],
                         1.0 / 1024.0, t["musq"][:, :], ALU.mult, ALU.subtract),
                         waits=("scalar",))
                    b.op("scalar", lambda E: E.activation(t["var"][:, :], t["var"][:, :], ACT.Ln,
                                                          bias=t["eps5"][:, :], scale=1.0),
                         waits=("vector",))
                    b.op("scalar", lambda E: E.activation(t["Alg"][:, :512], t["xbuf"][:, :512],
                                                          ACT.Copy))
                    b.op("scalar", lambda E: E.activation(t["rstd"][:, :], t["var"][:, :], ACT.Exp,
                                                          scale=-0.5))
                    b.op("vector", lambda E: E.scalar_tensor_tensor(t["xn"][:, :], t["xc"][:, :],
                         t["rstd"][:, :], t["xc"][:, :], ALU.mult, ALU.bypass),
                         waits=("scalar",))
                    for rnd in range(2):
                        for j in range(4):
                            fc = 4 * rnd + j
                            b.op("tensor", lambda E, fc=fc, j=j: E.transpose(
                                t["pF"][:, 128 * j:128 * (j + 1)],
                                t["xn"][:, 128 * fc:128 * (fc + 1)], t["sidentf"][:, :]),
                                waits=("vector", "scalar"))
                        b.op("vector", lambda E, rnd=rnd: E.tensor_copy(
                            t["xnfmb"][:, 512 * rnd:512 * (rnd + 1)], t["pF"][:, :]),
                            waits=("tensor",))
                        emit_quant(b, t, t["pF"][:, :], t["xnqb"][:, 512 * rnd:512 * (rnd + 1)],
                                   512, "fq", waits_sc=("tensor",))
                    dbg_dump("xn", t["xn"][:, :], (128, 1024), when=(n == 0))
                    dbg_dump("xnfmb", t["xnfmb"][:, :], (128, 1024), BF16, when=(n == 0))
                    dbg_dump("xnqb", t["xnqb"][:, :], (128, 1024), BF16, when=(n == 0))
                    for fc in range(8):
                        b.dma("gpsimd", xnq_s[128 * fc:128 * (fc + 1), tok],
                              t["xnqb"][:, 128 * fc:128 * (fc + 1)], waits=("vector",))
                    if astage < 2:
                        continue
                    for fc in range(8):
                        b.op("tensor", lambda E, fc=fc: E.matmul(
                            t["pA"][:, :512], t["xnfmb"][:, 128 * fc:128 * (fc + 1)],
                            t["wvb"][:, 512 * fc:512 * (fc + 1)],
                            start=(fc == 0), stop=(fc == 7), skip_group_check=True),
                            waits=("vector", "scalar") if fc == 0 else ())
                    for fc in range(8):
                        b.op("tensor", lambda E, fc=fc: E.matmul(
                            t["pB"][:, :96], t["xnfmb"][:, 128 * fc:128 * (fc + 1)],
                            t["wselb"][:, 96 * fc:96 * (fc + 1)],
                            start=(fc == 0), stop=(fc == 7), skip_group_check=True))
                    for (wb, bank) in (("wkb", "pD"), ("wrb", "pE")):
                        for dc in range(4):
                            for fc in range(8):
                                b.op("tensor", lambda E, wb=wb, bank=bank, dc=dc, fc=fc: E.matmul(
                                    t[bank][:, 128 * dc:128 * (dc + 1)],
                                    t[wb][:, 512 * fc + 128 * dc:512 * fc + 128 * (dc + 1)],
                                    t["xnfmb"][:, 128 * fc:128 * (fc + 1)],
                                    start=(fc == 0), stop=(fc == 7), skip_group_check=True))
                    b.op("vector", lambda E: E.reduce_sum(t["red"][:, :], t["pB"][:, 32:96], AX.X),
                         waits=("tensor",))
                    b.op("vector", lambda E: E.tensor_copy(t["vrawf"][:, :], t["pA"][:, :512]))
                    b.op("vector", lambda E: E.tensor_single_scalar(t["btok"][:, :], t["red"][:, :],
                                                                    1.0 / 64.0, ALU.mult))
                    b.op("vector", lambda E: E.tensor_copy(t["kev"][:, :], t["pD"][:, :512]))
                    b.op("vector", lambda E: E.scalar_tensor_tensor(t["vpb"][:, :], t["pA"][:, :512],
                         t["btok"][:, :], t["minus1f"][:, :], ALU.mult, ALU.bypass))
                    b.op("vector", lambda E: E.tensor_copy(t["xc"][:, :32], t["pB"][:, 0:32]))
                    b.op("tensor", lambda E: E.transpose(t["pF"][:32, :128], t["xc"][:, :32],
                                                         t["sidentf"][:, :]),
                         waits=("vector", "scalar"))
                    b.op("vector", lambda E: E.tensor_copy(t["dtrawb"][:, :], t["pF"][:32, :128]),
                         waits=("tensor",))
                    if astage < 3:
                        continue
                    b.op("scalar", lambda E: E.activation(t["sigr"][:, :], t["pE"][:, :512],
                                                          ACT.Exp, scale=-1.0), waits=("vector",))
                    b.op("vector", lambda E: E.tensor_single_scalar(t["sigr"][:, :], t["sigr"][:, :],
                                                                    1.0, ALU.add), waits=("scalar",))
                    b.op("scalar", lambda E: E.activation(t["sigr"][:, :], t["sigr"][:, :], ACT.Ln),
                         waits=("vector",))
                    b.op("scalar", lambda E: E.activation(t["sigr"][:, :], t["sigr"][:, :], ACT.Exp,
                                                          scale=-1.0))
                    for dc in range(4):
                        for fc in range(8):
                            b.op("tensor", lambda E, dc=dc, fc=fc: E.matmul(
                                t["pE"][:, 128 * dc:128 * (dc + 1)],
                                t["wgb"][:, 512 * fc + 128 * dc:512 * fc + 128 * (dc + 1)],
                                t["xnfmb"][:, 128 * fc:128 * (fc + 1)],
                                start=(fc == 0), stop=(fc == 7), skip_group_check=True),
                                waits=("scalar",) if (dc == 0 and fc == 0) else ())
                    b.op("scalar", lambda E: E.activation(t["sigg"][:, :], t["pE"][:, :512],
                                                          ACT.Exp, scale=-1.0), waits=("tensor",))
                    b.op("vector", lambda E: E.tensor_single_scalar(t["sigg"][:, :], t["sigg"][:, :],
                                                                    1.0, ALU.add), waits=("scalar",))
                    b.op("scalar", lambda E: E.activation(t["sigg"][:, :], t["sigg"][:, :], ACT.Ln),
                         waits=("vector",))
                    b.op("scalar", lambda E: E.activation(t["sigg"][:, :], t["sigg"][:, :], ACT.Exp,
                                                          scale=-1.0))
                    b.op("vector", lambda E: E.tensor_tensor(t["rg"][:, :], t["sigr"][:, :],
                                                             t["sigg"][:, :], ALU.mult),
                         waits=("scalar",))
                    for dc in range(4):
                        b.op("tensor", lambda E, dc=dc: E.matmul(
                            t["pB"][:, 128 * dc:128 * (dc + 1)],
                            t["dtwb"][:, 128 * dc:128 * (dc + 1)], t["dtrawb"][:, :],
                            start=True, stop=True, skip_group_check=True), waits=("vector",) if dc == 0 else ())
                    for c4 in range(4):
                        b.op("scalar", lambda E, c4=c4: E.activation(
                            t["ez"][:, 128 * c4:128 * (c4 + 1)], t["pB"][:, 128 * c4:128 * (c4 + 1)],
                            ACT.Exp, bias=t["sdtb"][:, c4:c4 + 1], scale=1.0),
                            waits=("tensor", "vector") if c4 == 0 else ("tensor",))
                    b.op("vector", lambda E: E.tensor_single_scalar(t["ez"][:, :], t["ez"][:, :],
                                                                    1.0, ALU.add), waits=("scalar",))
                    b.op("scalar", lambda E: E.activation(t["delta"][:, :], t["ez"][:, :], ACT.Ln),
                         waits=("vector",))
                    for c4 in range(4):
                        b.op("vector", lambda E, c4=c4: E.scalar_tensor_tensor(
                            t["ell"][:, 128 * c4:128 * (c4 + 1)],
                            t["delta"][:, 128 * c4:128 * (c4 + 1)],
                            t["bdneg"][:, c4:c4 + 1],
                            t["delta"][:, 128 * c4:128 * (c4 + 1)], ALU.mult, ALU.bypass),
                            waits=("scalar",) if c4 == 0 else ())
                    for c4 in range(4):
                        b.op("vector", lambda E, c4=c4: E.tensor_tensor_scan(
                            t["Pfull"][:, 128 * c4:128 * (c4 + 1)],
                            t["ell"][:, 128 * c4:128 * (c4 + 1)],
                            t["ell"][:, 128 * c4:128 * (c4 + 1)], 0.0, ALU.add, ALU.bypass))
                    for c4 in range(3):
                        b.op("vector", lambda E, c4=c4: E.scalar_tensor_tensor(
                            t["farg"][:, 128 * c4:128 * (c4 + 1)],
                            t["Pfull"][:, 128 * c4:128 * (c4 + 1)],
                            t["Pfull"][:, 128 * c4 + 127:128 * c4 + 128],
                            t["minus1f"][:, 128 * c4:128 * (c4 + 1)], ALU.subtract, ALU.mult))
                    b.op("vector", lambda E: E.tensor_copy(t["Pmid"][:, :],
                         ap(t["Pfull"], 8, [(512, 128), (16, 32)])))
                    b.op("vector", lambda E: E.tensor_copy(t["Pb64"][:, :],
                         ap(t["Pfull"], 63, [(512, 128), (128, 4)])))
                    b.op("vector", lambda E: E.tensor_copy(t["Pb32"][:, :],
                         ap(t["Pfull"], 31, [(512, 128), (64, 8)])))
                    b.op("vector", lambda E: E.tensor_copy(t["Pb16"][:, :],
                         ap(t["Pfull"], 15, [(512, 128), (32, 16)])))
                    b.op("vector", lambda E: E.tensor_copy(t["Asq"][:, :512],
                                                           t["Pfull"][:, :]))
                    b.op("vector", lambda E: E.scalar_tensor_tensor(
                        t["farg"][:, 384:512], t["Pfull"][:, 384:512],
                        t["Pfull"][:, 511:512],
                        t["minus1f"][:, 384:512], ALU.subtract, ALU.mult))
                    b.op("vector", lambda E: E.tensor_tensor(t["e1arg"][:, :], t["Pfull"][:, :],
                         ap(t["Pmid"], 0, [(32, 128), (1, 32), (0, 16)]), ALU.subtract))
                    b.op("vector", lambda E: E.tensor_tensor(t["s64"][:, :], t["Pfull"][:, :],
                         ap(t["Pb64"], 0, [(4, 128), (1, 4), (0, 128)]), ALU.subtract))
                    b.op("vector", lambda E: E.tensor_tensor(t["s32"][:, :], t["Pfull"][:, :],
                         ap(t["Pb32"], 0, [(8, 128), (1, 8), (0, 64)]), ALU.subtract))
                    b.op("vector", lambda E: E.tensor_tensor(t["s16"][:, :], t["Pfull"][:, :],
                         ap(t["Pb16"], 0, [(16, 128), (1, 16), (0, 32)]), ALU.subtract))
                    if astage < 4:
                        continue
                    b.op("scalar", lambda E: E.activation(t["expE"][:, :], t["Pfull"][:, :], ACT.Exp),
                         waits=("vector",))
                    b.op("scalar", lambda E: E.activation(t["expF"][:, :], t["farg"][:, :], ACT.Exp))
                    b.op("scalar", lambda E: E.activation(t["expE1"][:, :], t["e1arg"][:, :], ACT.Exp))
                    b.op("scalar", lambda E: E.activation(t["expE2"][:, :], t["e1arg"][:, :], ACT.Exp,
                                                          scale=-1.0))
                    b.op("scalar", lambda E: E.activation(t["xq64"][:, :],
                         ap(t["s64"], 64, [(512, 128), (128, 4), (1, 64)]), ACT.Exp))
                    b.op("scalar", lambda E: E.activation(t["xk64"][:, :],
                         ap(t["s64"], 0, [(512, 128), (128, 4), (1, 64)]), ACT.Exp, scale=-1.0))
                    b.op("scalar", lambda E: E.activation(t["xq32"][:, :],
                         ap(t["s32"], 32, [(512, 128), (64, 8), (1, 32)]), ACT.Exp))
                    b.op("scalar", lambda E: E.activation(t["xk32"][:, :],
                         ap(t["s32"], 0, [(512, 128), (64, 8), (1, 32)]), ACT.Exp, scale=-1.0))
                    b.op("scalar", lambda E: E.activation(t["xq16"][:, :],
                         ap(t["s16"], 16, [(512, 128), (32, 16), (1, 16)]), ACT.Exp))
                    b.op("scalar", lambda E: E.activation(t["xk16"][:, :],
                         ap(t["s16"], 0, [(512, 128), (32, 16), (1, 16)]), ACT.Exp, scale=-1.0))
                    b.op("vector", lambda E: E.tensor_tensor(t["qtb"][:, :], t["kev"][:, :],
                         t["expE"][:, :], ALU.mult), waits=("scalar",))
                    b.op("vector", lambda E: E.tensor_tensor(t["khf"][:, :], t["kev"][:, :],
                         t["expF"][:, :], ALU.mult))
                    b.op("vector", lambda E: E.tensor_tensor(t["ke1"][:, :], t["kev"][:, :],
                         t["expE1"][:, :], ALU.mult))
                    b.op("vector", lambda E: E.tensor_tensor(t["ke2"][:, :], t["kev"][:, :],
                         t["expE2"][:, :], ALU.mult))
                    b.op("vector", lambda E: E.tensor_tensor(t["kq64b"][:, :],
                         ap(t["kev"], 64, [(512, 128), (128, 4), (1, 64)]), t["xq64"][:, :], ALU.mult))
                    b.op("vector", lambda E: E.tensor_tensor(t["kk64b"][:, :],
                         ap(t["kev"], 0, [(512, 128), (128, 4), (1, 64)]), t["xk64"][:, :], ALU.mult))
                    b.op("vector", lambda E: E.tensor_tensor(t["kq32b"][:, :],
                         ap(t["kev"], 32, [(512, 128), (64, 8), (1, 32)]), t["xq32"][:, :], ALU.mult))
                    b.op("vector", lambda E: E.tensor_tensor(t["kk32b"][:, :],
                         ap(t["kev"], 0, [(512, 128), (64, 8), (1, 32)]), t["xk32"][:, :], ALU.mult))
                    b.op("vector", lambda E: E.tensor_tensor(t["kq16b"][:, :],
                         ap(t["kev"], 16, [(512, 128), (32, 16), (1, 16)]), t["xq16"][:, :], ALU.mult))
                    b.op("vector", lambda E: E.tensor_tensor(t["kk16b"][:, :],
                         ap(t["kev"], 0, [(512, 128), (32, 16), (1, 16)]), t["xk16"][:, :], ALU.mult))
                    dbg_dump("kev", t["kev"][:, :], (128, 512), when=(n == 0))
                    dbg_dump("vpb", t["vpb"][:, :], (128, 512), BF16, when=(n == 0))
                    dbg_dump("Pfull", t["Pfull"][:, :], (128, 512), when=(n == 0))
                    dbg_dump("expE", t["expE"][:, :], (128, 512), when=(n == 0))
                    dbg_dump("qtb", t["qtb"][:, :], (128, 512), BF16, when=(n == 0))
                    dbg_dump("rg", t["rg"][:, :], (128, 512), when=(n == 0))
                    dbg_dump("btok", t["btok"][:, :], (128, 1), when=(n == 0))
                    if astage < 5:
                        continue
                    b.op("vector", lambda E: E.tensor_copy(t["Sb"][:, :], t["Stile"][:, :]))
                    for (srcn, dstn) in (("qtb", "qtbB"), ("ke1", "ke1B"), ("ke2", "ke2B"),
                                         ("kq64b", "kq64B"), ("kk64b", "kk64B"),
                                         ("kq32b", "kq32B"), ("kk32b", "kk32B"),
                                         ("kq16b", "kq16B"), ("kk16b", "kk16B"),
                                         ("Sb", "SbB")):
                        b.dma("gpsimd", t[dstn][:, :], t[srcn][64:128, :], waits=("vector",))
                    if astage < 5.2:
                        continue
                    for p in range(4):
                        b.op("tensor", lambda E, p=p: E.transpose(
                            t["pF"][:, 128 * p:128 * (p + 1)],
                            t["khf"][:, 128 * p:128 * (p + 1)], t["sidentf"][:, :]),
                            waits=("vector", "scalar") if p == 0 else ())
                    if astage < 5.3:
                        continue
                    b.op("vector", lambda E: E.tensor_copy(t["khtm"][:, :], t["pF"][:, :512]),
                         waits=("tensor",))
                    if astage < 5.4:
                        continue
                    for p in range(4):
                        b.op("tensor", lambda E, p=p: E.transpose(
                            t["pF"][:, 128 * p:128 * (p + 1)],
                            t["vrawf"][:, 128 * p:128 * (p + 1)], t["sidentf"][:, :]),
                            waits=("vector",) if p == 0 else ())
                    if astage < 5.5:
                        continue
                    b.op("vector", lambda E: E.tensor_copy(t["vfm"][:, :], t["pF"][:, :512]),
                         waits=("tensor",))
                    if astage < 5.6:
                        continue
                    for rnd in range(2):
                        import os as _os3
                        for j in range(4):
                            h = 4 * rnd + j
                            p, hh = h // 2, h % 2
                            if _os3.environ.get("NOTP"):
                                tp = None
                            else:
                                tp = (64 * hh, 0)
                            if _os3.environ.get("BASE0"):
                                hh = 0
                                tp = None if _os3.environ.get("NOTP") else (0, 0)
                            ke2s = (t["ke2"][0:64, 128 * p:128 * (p + 1)] if hh == 0
                                    else t["ke2B"][:, 128 * p:128 * (p + 1)])
                            ke1s = (t["ke1"][0:64, 128 * p:128 * (p + 1)] if hh == 0
                                    else t["ke1B"][:, 128 * p:128 * (p + 1)])
                            b.op("tensor", lambda E, j=j, ke2s=ke2s, ke1s=ke1s: E.matmul(
                                t["pG"][:, 128 * j:128 * (j + 1)], ke2s, ke1s,
                                start=(j == 0), stop=False, skip_group_check=True),
                                waits=("vector", "gpsimd") if j == 0 else ())
                        for j in range(4 if astage >= 5.7 else 0):
                            h = 4 * rnd + j
                            p, hh = h // 2, h % 2
                            kkس = (t["kk64b"][0:64, 64 * p:64 * (p + 1)] if hh == 0
                                   else t["kk64B"][:, 64 * p:64 * (p + 1)])
                            kqس = (t["kq64b"][0:64, 64 * p:64 * (p + 1)] if hh == 0
                                   else t["kq64B"][:, 64 * p:64 * (p + 1)])
                            b.op("tensor", lambda E, j=j, a=kkس, c=kqس: E.matmul(
                                t["pG"][0:64, 128 * j + 64:128 * (j + 1)], a, c,
                                start=True, stop=False, skip_group_check=True))
                            for q2 in range(2 if astage >= 5.8 else 0):
                                a32 = (t["kk32b"][0:64, 32 * (2 * p + q2):32 * (2 * p + q2) + 32] if hh == 0
                                       else t["kk32B"][:, 32 * (2 * p + q2):32 * (2 * p + q2) + 32])
                                c32_ = (t["kq32b"][0:64, 32 * (2 * p + q2):32 * (2 * p + q2) + 32] if hh == 0
                                        else t["kq32B"][:, 32 * (2 * p + q2):32 * (2 * p + q2) + 32])
                                b.op("tensor", lambda E, j=j, q2=q2, a=a32, c=c32_: E.matmul(
                                    t["pG"][64 * q2:64 * q2 + 32,
                                            128 * j + 64 * q2 + 32:128 * j + 64 * q2 + 64],
                                    a, c, start=True, stop=False, skip_group_check=True,
                                    tile_position=(0, 64 * q2)))
                            for q in range(4 if astage >= 5.9 else 0):
                                a16 = (t["kk16b"][0:64, 16 * (4 * p + q):16 * (4 * p + q) + 16] if hh == 0
                                       else t["kk16B"][:, 16 * (4 * p + q):16 * (4 * p + q) + 16])
                                c16_ = (t["kq16b"][0:64, 16 * (4 * p + q):16 * (4 * p + q) + 16] if hh == 0
                                        else t["kq16B"][:, 16 * (4 * p + q):16 * (4 * p + q) + 16])
                                b.op("tensor", lambda E, j=j, q=q, a=a16, c=c16_: E.matmul(
                                    t["pG"][32 * q:32 * q + 16,
                                            128 * j + 32 * q + 16:128 * j + 32 * q + 32],
                                    a, c, start=True, stop=(j == 3 and q == 3), skip_group_check=True,
                                    tile_position=(0, 32 * q)))
                        for j in range(4):
                            b.op("vector", lambda E, j=j, rnd=rnd: E.select(
                                t["Amb"][:, 512 * rnd + 128 * j:512 * rnd + 128 * (j + 1)],
                                t["sutm"][:, :],
                                t["pG"][:, 128 * j:128 * (j + 1)], t["zero"][:, :]),
                                waits=("tensor",) if j == 0 else ())
                    if astage < 6:
                        continue
                    for h in range(8):
                        p, hh = h // 2, h % 2
                        Sbs = (t["Sb"][0:64, 64 * p:64 * (p + 1)] if hh == 0
                               else t["SbB"][:, 64 * p:64 * (p + 1)])
                        qts = (t["qtb"][0:64, 128 * p:128 * (p + 1)] if hh == 0
                               else t["qtbB"][:, 128 * p:128 * (p + 1)])
                        b.op("tensor", lambda E, p=p, hh=hh, a=Sbs, c=qts: E.matmul(
                            t["pH"][64 * hh:64 * (hh + 1), 128 * p:128 * (p + 1)],
                            a, c, start=True, stop=False, skip_group_check=True,
                            tile_position=(0, 64 * hh)),
                            waits=("vector", "gpsimd") if h == 0 else ())
                        b.op("tensor", lambda E, p=p, hh=hh, h=h: E.matmul(
                            t["pH"][64 * hh:64 * (hh + 1), 128 * p:128 * (p + 1)],
                            t["vpb"][:, 64 * h:64 * (h + 1)],
                            t["Amb"][:, 128 * h:128 * (h + 1)],
                            start=False, stop=True, skip_group_check=True, tile_position=(0, 64 * hh)))
                    b.op("vector", lambda E: E.tensor_tensor(t["kv"][:, :], t["kev"][:, :],
                         t["vfm"][:, :], ALU.mult))
                    b.op("vector", lambda E: E.tensor_tensor(t["bonus"][:, :], t["kv"][:, :],
                         ap(t["stf"], 0, [(4, 128), (1, 4), (0, 128)]), ALU.mult))
                    b.op("vector", lambda E: E.tensor_tensor(t["yst"][:, :], t["pH"][:, :512],
                         t["bonus"][:, :], ALU.add), waits=("tensor",))
                    b.op("vector", lambda E, n=n: E.tensor_tensor(
                        ap(t["ys"], 128 * n, [(4 * L, 128), (L, 4), (1, 128)]),
                        t["yst"][:, :], t["rg"][:, :], ALU.mult))
                    dbg_dump("Amb", t["Amb"][:, :], (128, 1024), BF16, when=(n == 0))
                    dbg_dump("ke1", t["ke1"][:, :], (128, 512), BF16, when=(n == 0))
                    dbg_dump("ke2", t["ke2"][:, :], (128, 512), BF16, when=(n == 0))
                    dbg_dump("kq64b", t["kq64b"][:, :], (128, 256), BF16, when=(n == 0))
                    dbg_dump("kk64b", t["kk64b"][:, :], (128, 256), BF16, when=(n == 0))
                    dbg_dump("yst", t["yst"][:, :], (128, 512), when=(n == 0))
                    if astage < 7:
                        continue
                    for h in range(8):
                        p, hh = h // 2, h % 2
                        b.op("tensor", lambda E, p=p, hh=hh, h=h: E.matmul(
                            t["pG"][64 * hh:64 * (hh + 1), 64 * p:64 * (p + 1)],
                            t["khtm"][:, 128 * p + 64 * hh:128 * p + 64 * (hh + 1)],
                            t["vpb"][:, 64 * h:64 * (h + 1)],
                            start=True, stop=True, skip_group_check=True, tile_position=(0, 64 * hh)),
                            waits=("vector",) if h == 0 else ())
                    for p in range(4):
                        b.op("vector", lambda E, p=p: E.scalar_tensor_tensor(
                            t["Stile"][:, 64 * p:64 * (p + 1)],
                            t["Stile"][:, 64 * p:64 * (p + 1)],
                            t["expE"][:, 128 * p + 127:128 * p + 128],
                            t["pG"][:, 64 * p:64 * (p + 1)], ALU.mult, ALU.add),
                            waits=("tensor",) if p == 0 else ())

            if debug:
                b.op("vector", lambda E: E.tensor_copy(t["Asq"][:, :256], t["Stile"][:, :]))
                dbg_dump("Sfinal", t["Asq"][:, :256], (128, 256))
            if skip_b:
                for g in range(NG):
                    for dc in range(8):
                        b.dma("gpsimd", out_h[128 * dc:128 * (dc + 1), 512 * g:512 * (g + 1)],
                              t["Asq"][:, :512], waits=("vector",))
                b.op("gpsimd", lambda E: E.memset(t["zero"][:1, :1], 0.0),
                     waits=("gpsimd",), inc_k=0)
                b.finalize(block)
                return nc
            esB = contextlib.ExitStack()
            with esB:
                esB1 = contextlib.ExitStack()
                esB1.__enter__()
                sb("w1qb", [128, 16384], BF16, esB1)
                sb("w2qb", [128, 16384], BF16, esB1)
                sb("xnqg", [128, 4096], BF16, esB1)
                sb("hq", [128, 8192], BF16, esB1)
                sb("y2g", [128, 4096], F32, esB1)
                psum("qA", [128, 512], F32, esB1)
                psum("qB", [128, 512], F32, esB1)
                psum("qC", [128, 512], F32, esB1)
                psum("qD", [128, 512], F32, esB1)
                sb("b1row", [1, 4096], BF16, esB1); sb("b2row", [1, 1024], BF16, esB1)
                sb("onesbf", [1, 512], BF16, esB1)
                b.op("vector", lambda E: E.memset(t["onesbf"][:, :], 1.0))
                b.dma("gpsimd", ap(t["b1row"], 0, [(4096, 1), (1, 4096)]),
                      ap(t["b1qT"], 0, [(128, 32), (1, 128)]), waits=("vector",))
                b.dma("gpsimd", ap(t["b2row"], 0, [(1024, 1), (1, 1024)]),
                      ap(t["b2qT"], 0, [(128, 8), (1, 128)]), waits=("vector",))

                for half in range(2):
                    for nt in range(16):
                        row = 2048 * half + 128 * nt
                        b.dma("sync", t["wstage"][:, :1024], w1_h[row:row + 128, :],
                              waits=("vector", "tensor", "scalar"))
                        for rnd in range(2):
                            for j in range(4):
                                fc = 4 * rnd + j
                                b.op("tensor", lambda E, fc=fc, j=j: E.transpose(
                                    t["qD"][:, 128 * j:128 * (j + 1)],
                                    t["wstage"][:, 128 * fc:128 * (fc + 1)], t["sidentf"][:, :]),
                                    waits=("sync", "vector", "scalar"))
                            emit_quant(
                                b, t, t["qD"][:, :512],
                                ap(t["w1qb"], 2048 * 4 * rnd + 128 * nt,
                                   [(16384, 128), (2048, 4), (1, 128)]),
                                512, "fq", waits_sc=("tensor",))
                    for dc in range(8):
                        for qrt in range(2):
                            col = 2048 * half + 1024 * qrt
                            b.dma("sync", t["wstage"][:, :1024],
                                  w2_h[128 * dc:128 * (dc + 1), col:col + 1024],
                                  waits=("vector", "tensor", "scalar"))
                            for rnd in range(2):
                                for j in range(4):
                                    nn = 4 * rnd + j
                                    b.op("tensor", lambda E, nn=nn, j=j: E.transpose(
                                        t["qD"][:, 128 * j:128 * (j + 1)],
                                        t["wstage"][:, 128 * nn:128 * (nn + 1)], t["sidentf"][:, :]),
                                        waits=("sync", "vector", "scalar"))
                                emit_quant(
                                    b, t, t["qD"][:, :512],
                                    ap(t["w2qb"], 1024 * (8 * qrt + 4 * rnd) + 128 * dc,
                                       [(16384, 128), (1024, 4), (1, 128)]),
                                    512, "fq", waits_sc=("tensor",))
                    for g in range(NG):
                        gcol = slice(512 * g, 512 * (g + 1))
                        for fc in range(8):
                            b.dma("sync", t["xnqg"][:, 512 * fc:512 * (fc + 1)],
                                  xnq_s[128 * fc:128 * (fc + 1), gcol],
                                  waits=("tensor", "vector", "gpsimd"))
                        if debug and half == 0 and g == 0:
                            pass
                        for pair in range(8):
                            for s_, bank in ((0, "qA"), (1, "qB")):
                                nch = 2 * pair + s_
                                for fc in range(8):
                                    b.op("tensor", lambda E, bank=bank, nch=nch, fc=fc: E.matmul(
                                        t[bank][:, :512],
                                        t["w1qb"][:, 2048 * fc + 128 * nch:2048 * fc + 128 * (nch + 1)],
                                        t["xnqg"][:, 512 * fc:512 * (fc + 1)],
                                        start=(fc == 0), stop=False, skip_group_check=True),
                                        waits=("sync", "vector", "scalar") if fc == 0 else ())
                                ng = 16 * half + nch
                                b.op("tensor", lambda E, bank=bank, ng=ng: E.matmul(
                                    t[bank][:, :512],
                                    t["b1row"][0:1, 128 * ng:128 * (ng + 1)],
                                    t["onesbf"][:, :512], start=False, stop=True, skip_group_check=True),
                                    waits=("gpsimd",) if ng == 0 else ())
                            for s_, bank in ((0, "qA"), (1, "qB")):
                                nch = 2 * pair + s_
                                emit_quant(b, t, t[bank][:, :512],
                                           t["hq"][:, 512 * nch:512 * (nch + 1)], 512, "comp",
                                           waits_sc=("tensor",) if s_ == 0 else ())
                        dbg_dump("hq", t["hq"][:, :], (128, 8192), BF16,
                                 when=(half == 0 and g == 0))
                        for dc in range(8):
                            for ncq in range(16):
                                b.op("tensor", lambda E, dc=dc, ncq=ncq: E.matmul(
                                    t["qC"][:, :512],
                                    t["w2qb"][:, 1024 * ncq + 128 * dc:1024 * ncq + 128 * (dc + 1)],
                                    t["hq"][:, 512 * ncq:512 * (ncq + 1)],
                                    start=(ncq == 0), stop=(half == 1 and ncq == 15), skip_group_check=True),
                                    waits=("vector",) if ncq == 0 else ())
                            if half == 0:
                                b.op("tensor", lambda E, dc=dc: E.matmul(
                                    t["qC"][:, :512], t["b2row"][0:1, 128 * dc:128 * (dc + 1)],
                                    t["onesbf"][:, :512], start=False, stop=True, skip_group_check=True))
                            b.op("vector", lambda E, dc=dc: E.tensor_copy(
                                t["y2g"][:, 512 * dc:512 * (dc + 1)], t["qC"][:, :512]),
                                waits=("tensor",))
                        dbg_dump("y2half0", t["y2g"][:, :], (128, 4096),
                                 when=(half == 0 and g == 0))
                        dbg_dump("y2half1", t["y2g"][:, :], (128, 4096),
                                 when=(half == 1 and g == 0))
                        for dc in range(8):
                            tgt = y2_s if half == 0 else y2_s1
                            b.dma("gpsimd", tgt[128 * dc:128 * (dc + 1), gcol],
                                  t["y2g"][:, 512 * dc:512 * (dc + 1)], waits=("vector",))

                esB1.__exit__(None, None, None)
                sb("wob", [128, 4096], BF16, esB)
                sb("y2f", [128, 4096], F32, esB)
                sb("maskd", [128, 512], BF16, esB)
                sb("xg", [128, 4096], F32, esB)
                sb("outt", [128, 512], F32, esB)
                psum("pJ", [128, 512], F32, esB)
                psum("pK", [128, 512], F32, esB)
                for vc in range(4):
                    b.dma("sync", t["wstage"][:, :1024], wo_h[128 * vc:128 * (vc + 1), :],
                          waits=("vector", "tensor"))
                    b.op("vector", lambda E, vc=vc: E.tensor_copy(
                        t["wob"][:, 1024 * vc:1024 * (vc + 1)], t["wstage"][:, :1024]),
                        waits=("sync",))
                for g in range(NG):
                    gcol = slice(512 * g, 512 * (g + 1))
                    for dc in range(8):
                        b.dma("sync", t["y2f"][:, 512 * dc:512 * (dc + 1)],
                              y2_s[128 * dc:128 * (dc + 1), gcol],
                              waits=("vector", "scalar", "gpsimd"))
                        b.dma("sync", t["xg"][:, 512 * dc:512 * (dc + 1)],
                              y2_s1[128 * dc:128 * (dc + 1), gcol])
                    for dc in range(8):
                        b.op("vector", lambda E, dc=dc: E.tensor_tensor(
                            t["y2f"][:, 512 * dc:512 * (dc + 1)],
                            t["y2f"][:, 512 * dc:512 * (dc + 1)],
                            t["xg"][:, 512 * dc:512 * (dc + 1)], ALU.add),
                            waits=("sync",) if dc == 0 else ())
                    for tt_ in range(4):
                        b.dma("sync", t["xg"][:, 1024 * tt_:1024 * (tt_ + 1)],
                              x_h[512 * g + 128 * tt_:512 * g + 128 * (tt_ + 1), :],
                              waits=("tensor", "vector"))
                    dbg_dump("y2fin", t["y2f"][:, :], (128, 4096), when=(g == 0))
                    for dc in range(8):
                        emit_quant(b, t, t["y2f"][:, 512 * dc:512 * (dc + 1)],
                                   t["y2f"][:, 512 * dc:512 * (dc + 1)], 512, "rm",
                                   waits_sc=("sync",) if dc == 0 else ())
                    dbg_dump("y2rm", t["y2f"][:, :], (128, 4096), when=(g == 0))
                    for dc in range(8):
                        for vc in range(4):
                            b.op("tensor", lambda E, dc=dc, vc=vc, g=g: E.matmul(
                                t["pJ"][:, :512],
                                t["wob"][:, 1024 * vc + 128 * dc:1024 * vc + 128 * (dc + 1)],
                                ap(t["ys"], L * vc + 512 * g, [(4 * L, 128), (1, 512)]),
                                start=(vc == 0), stop=(vc == 3), skip_group_check=True),
                                waits=("vector", "sync") if vc == 0 else ())
                        for tt_ in range(4):
                            b.op("tensor", lambda E, tt_=tt_, dc=dc: E.transpose(
                                t["pK"][:, 128 * tt_:128 * (tt_ + 1)],
                                t["xg"][:, 1024 * tt_ + 128 * dc:1024 * tt_ + 128 * (dc + 1)],
                                t["sidentf"][:, :]),
                                waits=("vector",) if tt_ == 0 else ())
                        b.dma("sync", t["maskd"][:, :512],
                              maskt_h[128 * dc:128 * (dc + 1), gcol], waits=("vector",))
                        b.op("vector", lambda E, dc=dc: E.tensor_tensor(
                            t["outt"][:, :512], t["y2f"][:, 512 * dc:512 * (dc + 1)],
                            t["maskd"][:, :512], ALU.mult),
                            waits=("tensor", "sync"))
                        if debug and dc == 0:
                            b.op("vector", lambda E: E.tensor_copy(t["Asq"][:, :512], t["pJ"][:, :512]))
                            dbg_dump("att0", t["Asq"][:, :512], (128, 512), when=True)
                        b.op("vector", lambda E: E.tensor_tensor(
                            t["outt"][:, :512], t["outt"][:, :512], t["pJ"][:, :512], ALU.add))
                        b.op("vector", lambda E: E.tensor_tensor(
                            t["outt"][:, :512], t["outt"][:, :512], t["pK"][:, :512], ALU.add))
                        dbg_dump("outt0", t["outt"][:, :512], (128, 512),
                                 when=(g == 0 and dc == 0))
                        if debug and g == 0 and dc == 0:
                            b.op("vector", lambda E: E.tensor_copy(t["At1"][:, :512], t["y2f"][:, :512]))
                            dbg_dump("y2r0", t["At1"][:, :512], (128, 512), when=True)
                            b.op("vector", lambda E: E.tensor_copy(t["At1"][:, :512], t["pK"][:, :512]))
                            dbg_dump("xfm0", t["At1"][:, :512], (128, 512), when=True)
                            b.op("vector", lambda E: E.tensor_copy(t["At1"][:, :512], t["maskd"][:, :512]))
                            dbg_dump("msk0", t["At1"][:, :512], (128, 512), when=True)
                        b.dma("gpsimd", out_h[128 * dc:128 * (dc + 1), gcol],
                              t["outt"][:, :512], waits=("vector",))
                b.op("gpsimd", lambda E: E.memset(t["zero"][:1, :1], 0.0),
                     waits=("gpsimd",), inc_k=0)

            b.finalize(block)
    return nc


_NC_CACHE = {}


def _bf16():
    import ml_dtypes
    return ml_dtypes.bfloat16


def kernel(L0_attn_x, L0_attn_W_sel, L0_attn_dt_W, L0_attn_dt_b, L0_attn_Wk,
           L0_attn_Wv, L0_attn_Wr, L0_attn_Wg, L0_attn_Wo, L0_attn_base_decay,
           L0_attn_time_first, L0_ffn_W1, L0_ffn_b1, L0_ffn_W2, L0_ffn_b2):
    import jax
    B, L, D = L0_attn_x.shape
    if L not in _NC_CACHE:
        _NC_CACHE[L] = build_nc(L)
    nc = _NC_CACHE[L]
    jax.config.update("jax_threefry_partitionable", True)
    with jax.default_device(jax.devices("cpu")[0]):
        keep = np.asarray(jax.random.bernoulli(
            jax.random.key(42, impl="threefry2x32"), 0.8, (B, L, D)))
    maskt = np.where(keep, np.float32(1.25), np.float32(0.0)).astype(_bf16())
    maskt = np.ascontiguousarray(np.swapaxes(maskt, 1, 2))
    f32 = np.float32
    common = {
        "wsel": np.ascontiguousarray(L0_attn_W_sel[:, :96], dtype=f32),
        "dtw": np.ascontiguousarray(L0_attn_dt_W, dtype=f32),
        "dtb": np.ascontiguousarray(np.asarray(L0_attn_dt_b, dtype=f32).reshape(4, 128).T),
        "wk": np.ascontiguousarray(L0_attn_Wk, dtype=f32),
        "wv": np.ascontiguousarray(L0_attn_Wv, dtype=f32),
        "wr": np.ascontiguousarray(L0_attn_Wr, dtype=f32),
        "wg": np.ascontiguousarray(L0_attn_Wg, dtype=f32),
        "wo": np.ascontiguousarray(L0_attn_Wo, dtype=f32),
        "bd": np.ascontiguousarray(np.asarray(L0_attn_base_decay, dtype=f32).reshape(512).reshape(4, 128).T),
        "tf": np.ascontiguousarray(np.asarray(L0_attn_time_first, dtype=f32).reshape(512).reshape(4, 128).T),
        "w1": np.ascontiguousarray(L0_ffn_W1, dtype=f32),
        "b1": np.ascontiguousarray(np.asarray(L0_ffn_b1, dtype=f32).reshape(32, 128).T),
        "w2": np.ascontiguousarray(L0_ffn_W2, dtype=f32),
        "b2": np.ascontiguousarray(np.asarray(L0_ffn_b2, dtype=f32).reshape(8, 128).T),
        "identf": np.eye(128, dtype=f32),
        "identb": np.eye(128, dtype=f32).astype(_bf16()),
        "utm": np.triu(np.ones((128, 128), dtype=np.uint8)),
    }
    in_maps = []
    for bi in range(B):
        m = dict(common)
        m["x"] = np.ascontiguousarray(L0_attn_x[bi], dtype=f32)
        m["maskt"] = np.ascontiguousarray(maskt[bi])
        in_maps.append(m)
    res = run_bass_kernel_spmd(nc, in_maps, core_ids=list(range(B)))
    outs = [np.ascontiguousarray(np.asarray(r["out"]).T) for r in res.results]
    return np.stack(outs).astype(np.float32)


# version 08
# speedup vs baseline: 2.7380x; 2.7380x over previous
import contextlib
import numpy as np

import concourse.bass as bass
import concourse.mybir as mybir
from concourse.bass_utils import run_bass_kernel_spmd

F32 = mybir.dt.float32
FP8 = mybir.dt.float8e5
BF16 = mybir.dt.bfloat16
I32 = mybir.dt.int32
U8 = mybir.dt.uint8
ALU = mybir.AluOpType
ACT = mybir.ActivationFunctionType
AX = mybir.AxisListType

C2 = 256.0 / (2.0 * np.log(2.0))

FQ_THR = [(16.0, ALU.is_gt), (48.0, ALU.is_ge), (80.0, ALU.is_gt), (112.0, ALU.is_ge),
          (144.0, ALU.is_gt), (176.0, ALU.is_ge), (208.0, ALU.is_gt), (240.0, ALU.is_ge)]
COMP_THR = [(16.0, ALU.is_gt), (48.0, ALU.is_ge), (80.0, ALU.is_gt), (112.0, ALU.is_ge),
            (144.5, ALU.is_gt), (175.5, ALU.is_ge), (208.5, ALU.is_gt), (239.5, ALU.is_ge)]


def ap(tensor, offset, dims):
    return bass.AP(tensor, offset, [list(d) for d in dims])


class Builder:
    ENGINES = ("sync", "scalar", "vector", "tensor", "gpsimd")

    def __init__(self, nc, es):
        self.nc = nc
        self.ops = {e: [] for e in self.ENGINES}
        self.sems = {}
        self.counts = {}
        for e in self.ENGINES:
            self.sems[e] = es.enter_context(nc.semaphore(f"s_{e}"))
            self.counts[e] = 0

    def op(self, eng, fn, waits=(), waits_abs=(), inc_k=1):
        wl = [(w, self.counts[w]) for w in waits if self.counts[w] > 0]
        wl += [(w, c) for (w, c) in waits_abs if c > 0]
        self.ops[eng].append((fn, wl, inc_k))
        self.counts[eng] += inc_k

    def mark(self, eng):
        return self.counts[eng]

    def dma(self, eng, out_ap, in_ap, waits=(), waits_abs=(), **kw):
        self.op(eng, (lambda E, o=out_ap, i=in_ap, kw=kw: E.dma_start(o, i, **kw)),
                waits=waits, waits_abs=waits_abs, inc_k=16)

    def finalize(self, block):
        sems = self.sems

        def mk(ename):
            ops = self.ops[ename]

            def emit(E):
                for fn, wl, inck in ops:
                    for (w, v) in wl:
                        E.wait_ge(sems[w], v)
                    inst = fn(E)
                    if inck:
                        inst.then_inc(sems[ename], inck)
            return emit
        block.sync(mk("sync"))
        block.scalar(mk("scalar"))
        block.vector(mk("vector"))
        block.tensor(mk("tensor"))
        block.gpsimd(mk("gpsimd"))


def emit_quant(b, t, y_ap, out_ap, width, mode, waits_sc=(), waits_v=(), out_waits=(),
               evac=None, out_waits_abs=(), waits_sc_abs=()):
    W = width
    sq, lg = t["Asq"], t["Alg"]
    u, i256, eint, e2int = t["Au"], t["Ai256"], t["Aeint"], t["Ae2int"]
    shS, shI, stepf, invf = t["AshS"], t["AshI"], t["Astepf"], t["Ainvf"]
    q, L = t["Aq"], t["AL"]
    b.op("scalar", lambda E, y=y_ap: E.activation(sq[:, :W], y, ACT.Square),
         waits=tuple(waits_sc) + ("vector",), waits_abs=waits_sc_abs)
    if evac is not None:
        b.op("scalar", lambda E, y=y_ap: E.activation(evac, y, ACT.Copy))
        y_ap = evac
    free_mark = b.mark("scalar")
    b.op("scalar", lambda E: E.activation(lg[:, :W], sq[:, :W], ACT.Ln,
                                          bias=t["tinyb"][:, :], scale=1.0))
    if mode == "rm":
        b.op("vector", lambda E: E.tensor_scalar(u[:, :W], lg[:, :W], C2, 20480.0,
                                                 ALU.mult, ALU.add),
             waits=("scalar",) + tuple(waits_v))
        b.op("vector", lambda E: E.tensor_single_scalar(u[:, :W], u[:, :W], 12288.0, ALU.max))
        b.op("vector", lambda E: E.tensor_copy(i256[:, :W], u[:, :W]))
        b.op("vector", lambda E: E.tensor_single_scalar(eint[:, :W], i256[:, :W], 8,
                                                        ALU.arith_shift_right))
        b.op("vector", lambda E: E.tensor_single_scalar(eint[:, :W], eint[:, :W], -64, ALU.add))
        b.op("vector", lambda E: E.tensor_single_scalar(e2int[:, :W], eint[:, :W], 16, ALU.add))
        b.op("vector", lambda E: E.tensor_single_scalar(shS[:, :W], e2int[:, :W], 1,
                                                        ALU.arith_shift_right))
        b.op("vector", lambda E: E.tensor_tensor(shI[:, :W], e2int[:, :W], shS[:, :W],
                                                 ALU.subtract))
        b.op("vector", lambda E: E.tensor_tensor(shS[:, :W], t["onei"][:, :W], shS[:, :W],
                                                 ALU.logical_shift_left))
        b.op("vector", lambda E: E.tensor_tensor(shI[:, :W], t["onei"][:, :W], shI[:, :W],
                                                 ALU.logical_shift_left))
        b.op("vector", lambda E: E.tensor_copy(stepf[:, :W], shS[:, :W]))
        b.op("vector", lambda E: E.tensor_copy(q[:, :W], shI[:, :W]))
        b.op("vector", lambda E: E.tensor_tensor(stepf[:, :W], stepf[:, :W], q[:, :W],
                                                 ALU.mult))
        b.op("vector", lambda E: E.tensor_tensor(e2int[:, :W], t["thirty1"][:, :W], eint[:, :W],
                                                 ALU.subtract))
        b.op("vector", lambda E: E.tensor_single_scalar(shS[:, :W], e2int[:, :W], 1,
                                                        ALU.arith_shift_right))
        b.op("vector", lambda E: E.tensor_tensor(shI[:, :W], e2int[:, :W], shS[:, :W],
                                                 ALU.subtract))
        b.op("vector", lambda E: E.tensor_tensor(shS[:, :W], t["onei"][:, :W], shS[:, :W],
                                                 ALU.logical_shift_left))
        b.op("vector", lambda E: E.tensor_tensor(shI[:, :W], t["onei"][:, :W], shI[:, :W],
                                                 ALU.logical_shift_left))
        b.op("vector", lambda E: E.tensor_copy(invf[:, :W], shS[:, :W]))
        b.op("vector", lambda E: E.tensor_copy(q[:, :W], shI[:, :W]))
        b.op("vector", lambda E: E.tensor_tensor(invf[:, :W], invf[:, :W], q[:, :W],
                                                 ALU.mult))
    else:
        b.op("vector", lambda E: E.tensor_scalar(u[:, :W], lg[:, :W], C2, 4096.0,
                                                 ALU.mult, ALU.add),
             waits=("scalar",) + tuple(waits_v))
        b.op("vector", lambda E: E.tensor_single_scalar(u[:, :W], u[:, :W], 512.0, ALU.max))
        b.op("vector", lambda E: E.tensor_copy(i256[:, :W], u[:, :W]))
        b.op("vector", lambda E: E.tensor_single_scalar(eint[:, :W], i256[:, :W], 8,
                                                        ALU.arith_shift_right))
        b.op("vector", lambda E: E.tensor_tensor(shS[:, :W], t["onei"][:, :W], eint[:, :W],
                                                 ALU.logical_shift_left))
        b.op("vector", lambda E: E.tensor_tensor(e2int[:, :W], eint[:, :W], t["thirty1"][:, :W],
                                                 ALU.bitwise_xor))
        b.op("vector", lambda E: E.tensor_tensor(shI[:, :W], t["onei"][:, :W], e2int[:, :W],
                                                 ALU.logical_shift_left))
        b.op("vector", lambda E: E.tensor_copy(stepf[:, :W], shS[:, :W]))
        b.op("vector", lambda E: E.tensor_copy(invf[:, :W], shI[:, :W]))
    b.op("vector", lambda E, y=y_ap: E.scalar_tensor_tensor(q[:, :W], invf[:, :W], 2.0 ** -8, y,
                                                    ALU.mult, ALU.mult))
    if mode == "comp":
        (t0, o0) = COMP_THR[0]
        b.op("vector", lambda E: E.tensor_single_scalar(L[:, :W], q[:, :W], t0, o0))
        for (tv, ov) in COMP_THR[1:]:
            b.op("vector", lambda E, tv=tv, ov=ov: E.scalar_tensor_tensor(
                L[:, :W], q[:, :W], tv, L[:, :W], ov, ALU.add))
        b.op("vector", lambda E: E.scalar_tensor_tensor(
            out_ap, L[:, :W], 2.0 ** -18, stepf[:, :W], ALU.mult, ALU.mult),
            waits=out_waits, waits_abs=out_waits_abs)
        return free_mark
    nq, aq, sg = t["Anq"], t["Aaq"], t["Asg"]
    b.op("vector", lambda E: E.tensor_single_scalar(nq[:, :W], q[:, :W], -1.0, ALU.mult))
    b.op("vector", lambda E: E.tensor_tensor(aq[:, :W], q[:, :W], nq[:, :W], ALU.max))
    if mode == "fq":
        (t0, o0) = FQ_THR[0]
        b.op("vector", lambda E: E.tensor_single_scalar(L[:, :W], aq[:, :W], t0, o0))
        for (tv, ov) in FQ_THR[1:]:
            b.op("vector", lambda E, tv=tv, ov=ov: E.scalar_tensor_tensor(
                L[:, :W], aq[:, :W], tv, L[:, :W], ov, ALU.add))
        res, scl = L, 2.0 ** -18
    else:
        r, fl, b1, t1, tie = t["Ar"], t["Afl"], t["Ab1"], t["At1"], t["Atie"]
        b.op("vector", lambda E: E.tensor_single_scalar(r[:, :W], aq[:, :W], -128.0, ALU.add))
        for w in (64.0, 32.0, 16.0, 8.0, 4.0, 2.0):
            b.op("vector", lambda E, w=w: E.tensor_single_scalar(t1[:, :W], r[:, :W], w, ALU.is_ge))
            b.op("vector", lambda E, w=w: E.scalar_tensor_tensor(
                r[:, :W], t1[:, :W], -w, r[:, :W], ALU.mult, ALU.add))
        b.op("vector", lambda E: E.tensor_single_scalar(b1[:, :W], r[:, :W], 1.0, ALU.is_ge))
        b.op("vector", lambda E: E.scalar_tensor_tensor(
            r[:, :W], b1[:, :W], -1.0, r[:, :W], ALU.mult, ALU.add))
        b.op("vector", lambda E: E.tensor_tensor(fl[:, :W], aq[:, :W], r[:, :W], ALU.subtract))
        b.op("vector", lambda E: E.tensor_single_scalar(t1[:, :W], r[:, :W], 0.5, ALU.is_gt))
        b.op("vector", lambda E: E.tensor_tensor(fl[:, :W], fl[:, :W], t1[:, :W], ALU.add))
        b.op("vector", lambda E: E.tensor_single_scalar(tie[:, :W], r[:, :W], 0.5, ALU.is_equal))
        b.op("vector", lambda E: E.tensor_tensor(tie[:, :W], tie[:, :W], b1[:, :W], ALU.mult))
        b.op("vector", lambda E: E.tensor_tensor(fl[:, :W], fl[:, :W], tie[:, :W], ALU.add))
        b.op("vector", lambda E: E.tensor_single_scalar(t1[:, :W], aq[:, :W], 1.0, ALU.is_ge))
        b.op("vector", lambda E: E.tensor_tensor(fl[:, :W], fl[:, :W], t1[:, :W], ALU.mult))
        res, scl = fl, 2.0 ** -39
    b.op("vector", lambda E: E.tensor_scalar(sg[:, :W], q[:, :W], 0.0, 2.0, ALU.is_ge, ALU.mult))
    b.op("vector", lambda E: E.tensor_single_scalar(sg[:, :W], sg[:, :W], -1.0, ALU.add))
    b.op("vector", lambda E, res=res: E.tensor_tensor(res[:, :W], res[:, :W], sg[:, :W], ALU.mult))
    b.op("vector", lambda E, res=res, scl=scl: E.scalar_tensor_tensor(
        out_ap, res[:, :W], scl, stepf[:, :W], ALU.mult, ALU.mult),
        waits=out_waits, waits_abs=out_waits_abs)
    return free_mark


def build_nc(L=4096, debug=False, skip_b=False, skip_a=False, astage=99.0):
    NCH = L // 128
    NG = L // 512
    nc = bass.Bass("TRN2", target_bir_lowering=False, detect_race_conditions=False)

    x_h = nc.dram_tensor("x", [L, 1024], F32, kind="ExternalInput")
    wsel_h = nc.dram_tensor("wsel", [1024, 96], F32, kind="ExternalInput")
    dtw_h = nc.dram_tensor("dtw", [32, 512], F32, kind="ExternalInput")
    dtb_h = nc.dram_tensor("dtb", [128, 4], F32, kind="ExternalInput")
    wk_h = nc.dram_tensor("wk", [1024, 512], F32, kind="ExternalInput")
    wv_h = nc.dram_tensor("wv", [1024, 512], F32, kind="ExternalInput")
    wr_h = nc.dram_tensor("wr", [1024, 512], F32, kind="ExternalInput")
    wg_h = nc.dram_tensor("wg", [1024, 512], F32, kind="ExternalInput")
    wo_h = nc.dram_tensor("wo", [512, 1024], F32, kind="ExternalInput")
    bd_h = nc.dram_tensor("bd", [128, 4], F32, kind="ExternalInput")
    tf_h = nc.dram_tensor("tf", [128, 4], F32, kind="ExternalInput")
    w1_h = nc.dram_tensor("w1", [4096, 1024], F32, kind="ExternalInput")
    b1_h = nc.dram_tensor("b1", [128, 32], F32, kind="ExternalInput")
    w2_h = nc.dram_tensor("w2", [1024, 4096], F32, kind="ExternalInput")
    b2_h = nc.dram_tensor("b2", [128, 8], F32, kind="ExternalInput")
    maskt_h = nc.dram_tensor("maskt", [1024, L], BF16, kind="ExternalInput")
    identf_h = nc.dram_tensor("identf", [128, 128], F32, kind="ExternalInput")
    identb_h = nc.dram_tensor("identb", [128, 128], BF16, kind="ExternalInput")
    utm_h = nc.dram_tensor("utm", [128, 128], U8, kind="ExternalInput")
    out_h = nc.dram_tensor("out", [1024, L], F32, kind="ExternalOutput")
    xnq_s = nc.dram_tensor("xnq_s", [1024, L], BF16)
    y2_s = nc.dram_tensor("y2_s", [1024, L], F32)
    y2_s1 = nc.dram_tensor("y2_s1", [1024, L], F32)

    dbg = {}

    def mkdbg(name, shape, dt=F32):
        if name not in dbg:
            dbg[name] = nc.dram_tensor("dbg_" + name, list(shape), dt, kind="ExternalOutput")
        return dbg[name]

    es = contextlib.ExitStack()
    with es:
        b = Builder(nc, es)
        t = {}

        def dbg_dump(name, src_ap, shape, dt=F32, when=True):
            if debug and when:
                h = mkdbg(name, shape, dt)
                b.dma("gpsimd", h[:, :], src_ap, waits=("vector", "scalar", "tensor"))
                b.op("vector", lambda E: E.memset(t["zero"][:1, :1], 0.0), waits=("gpsimd",))

        def sb(name, shape, dt=F32, stack=es):
            t[name] = stack.enter_context(nc.sbuf_tensor(name, list(shape), dt))
            return t[name]

        def psum(name, shape, dt=F32, stack=es):
            t[name] = stack.enter_context(nc.psum_tensor(name, list(shape), dt))
            return t[name]

        sb("sidentf", [128, 128]); sb("sidentb", [128, 128], BF16)
        sb("sutm", [128, 128], U8)
        sb("zero", [128, 128], F32)
        sb("sdtb", [128, 4]); sb("sbd", [128, 4]); sb("bdneg", [128, 4]); sb("stf", [128, 4])
        sb("b1r", [128, 32]); sb("b2r", [128, 8])
        sb("b1qT", [32, 128], BF16); sb("b2qT", [8, 128], BF16)
        sb("wstage", [128, 1024], F32)
        sb("ys", [128, 4 * L], BF16)
        sb("Asq", [128, 1024])
        for nm in ("Alg", "Ar", "Afl", "At1", "Atie"):
            sb(nm, [128, 512])
        sb("minus1f", [128, 512], F32)
        sb("q8", [128, 512], FP8)
        sb("qb16", [128, 512], BF16)
        sb("eps5", [128, 1], F32)
        sb("one1", [128, 1], F32)


        with nc.Block() as block:
            esA = contextlib.ExitStack()
            with esA:
                sb("wkb", [128, 4096], BF16, esA); sb("wvb", [128, 4096], BF16, esA)
                sb("wrb", [128, 4096], BF16, esA); sb("wgb", [128, 4096], BF16, esA)
                sb("wselb", [128, 768], BF16, esA); sb("dtwb", [32, 512], BF16, esA)
                sb("xbuf0", [128, 1024], F32, esA); sb("xbuf1", [128, 1024], F32, esA)
                sb("xc", [128, 1024], F32, esA)
                sb("red", [128, 1], F32, esA); sb("mu", [128, 1], F32, esA)
                sb("var", [128, 1], F32, esA); sb("rstd", [128, 1], F32, esA)
                sb("musq", [128, 1], F32, esA)
                sb("dtev", [128, 32], F32, esA)
                sb("xn0", [128, 1024], F32, esA); sb("xn1", [128, 1024], F32, esA)
                sb("xnfmb0", [128, 1024], BF16, esA); sb("xnfmb1", [128, 1024], BF16, esA)
                sb("xnqb0", [128, 1024], FP8, esA); sb("xnqb1", [128, 1024], FP8, esA)
                sb("kev", [128, 512], F32, esA)
                sb("Pfull", [128, 512], F32, esA)
                sb("ell", [128, 512], F32, esA)
                sb("ez", [128, 512], F32, esA)
                sb("delta", [128, 512], F32, esA)
                sb("Pmid", [128, 32], F32, esA); sb("Pb64", [128, 4], F32, esA)
                sb("Pb32", [128, 8], F32, esA); sb("Pb16", [128, 16], F32, esA)
                sb("e1arg", [128, 512], F32, esA)
                sb("s64", [128, 512], F32, esA); sb("s32", [128, 512], F32, esA)
                sb("s16", [128, 512], F32, esA); sb("farg", [128, 512], F32, esA)
                sb("expE", [128, 512], F32, esA); sb("expF", [128, 512], F32, esA)
                sb("expE1", [128, 512], F32, esA); sb("expE2", [128, 512], F32, esA)
                sb("xq64", [128, 256], F32, esA); sb("xk64", [128, 256], F32, esA)
                sb("xq32", [128, 256], F32, esA); sb("xk32", [128, 256], F32, esA)
                sb("xq16", [128, 256], F32, esA); sb("xk16", [128, 256], F32, esA)
                sb("qtb", [128, 512], BF16, esA); sb("khf", [128, 512], F32, esA)
                sb("qtbB", [64, 512], BF16, esA)
                sb("ke1B", [64, 512], BF16, esA); sb("ke2B", [64, 512], BF16, esA)
                sb("kq64B", [64, 256], BF16, esA); sb("kk64B", [64, 256], BF16, esA)
                sb("kq32B", [64, 256], BF16, esA); sb("kk32B", [64, 256], BF16, esA)
                sb("kq16B", [64, 256], BF16, esA); sb("kk16B", [64, 256], BF16, esA)
                sb("SbB", [64, 256], BF16, esA)
                sb("khtm", [128, 512], BF16, esA)
                sb("ke1", [128, 512], BF16, esA); sb("ke2", [128, 512], BF16, esA)
                sb("kq64b", [128, 256], BF16, esA); sb("kk64b", [128, 256], BF16, esA)
                sb("kq32b", [128, 256], BF16, esA); sb("kk32b", [128, 256], BF16, esA)
                sb("kq16b", [128, 256], BF16, esA); sb("kk16b", [128, 256], BF16, esA)
                sb("vpb", [128, 512], BF16, esA); sb("vrawf", [128, 512], F32, esA)
                sb("vfm", [128, 512], F32, esA)
                sb("btok", [128, 1], F32, esA)
                sb("sigr", [128, 512], F32, esA); sb("sigg", [128, 512], F32, esA)
                sb("rg", [128, 512], F32, esA)
                sb("Amb", [128, 1024], BF16, esA)
                sb("kv", [128, 512], F32, esA); sb("bonus", [128, 512], F32, esA)
                sb("dtrawb", [32, 128], BF16, esA)
                sb("Stile", [128, 256], F32, esA)
                sb("Sb", [128, 256], BF16, esA)
                sb("yst", [128, 512], F32, esA)
                psum("pA", [128, 512], F32, esA)
                psum("pB", [128, 512], F32, esA)
                psum("pD", [128, 512], F32, esA)
                psum("pE", [128, 512], F32, esA)
                psum("pF", [128, 512], F32, esA)
                psum("pG", [128, 512], F32, esA)
                psum("pH", [128, 512], F32, esA)
                psum("pL", [128, 512], F32, esA)
                

                b.dma("sync", t["sidentf"][:, :], identf_h[:, :])
                b.dma("sync", t["sidentb"][:, :], identb_h[:, :])
                b.dma("sync", t["sutm"][:, :], utm_h[:, :])
                b.dma("sync", t["sdtb"][:, :], dtb_h[:, :])
                b.dma("sync", t["sbd"][:, :], bd_h[:, :])
                b.dma("sync", t["stf"][:, :], tf_h[:, :])
                b.dma("sync", t["b1r"][:, :], b1_h[:, :])
                b.dma("sync", t["b2r"][:, :], b2_h[:, :])
                b.dma("sync", t["wstage"][:32, :512], dtw_h[:, :])
                b.op("vector", lambda E: E.memset(t["zero"][:, :], 0.0), waits=("sync",))
                b.op("vector", lambda E: E.memset(t["minus1f"][:, :], -1.0))
                b.op("vector", lambda E: E.memset(t["eps5"][:, :], 1e-5))
                b.op("vector", lambda E: E.memset(t["one1"][:, :], 1.0))
                b.op("vector", lambda E: E.memset(t["Stile"][:, :], 0.0))
                b.op("vector", lambda E: E.tensor_single_scalar(t["bdneg"][:, :], t["sbd"][:, :], -1.0, ALU.mult))
                b.op("vector", lambda E: E.tensor_copy(t["dtwb"][:, :], t["wstage"][:32, :512]))
                for (wh, wb) in ((wk_h, "wkb"), (wv_h, "wvb"), (wr_h, "wrb"), (wg_h, "wgb")):
                    for fc in range(8):
                        b.dma("sync", t["wstage"][:, :512], wh[128 * fc:128 * (fc + 1), :],
                              waits=("vector",))
                        b.op("vector", lambda E, wb=wb, fc=fc: E.tensor_copy(
                            t[wb][:, 512 * fc:512 * (fc + 1)], t["wstage"][:, :512]),
                            waits=("sync",))
                for fc in range(8):
                    b.dma("sync", t["wstage"][:, :96], wsel_h[128 * fc:128 * (fc + 1), :],
                          waits=("vector",))
                    b.op("vector", lambda E, fc=fc: E.tensor_copy(
                        t["wselb"][:, 96 * fc:96 * (fc + 1)], t["wstage"][:, :96]),
                        waits=("sync",))
                b.op("vector", lambda E: E.tensor_copy(
                    t["At1"][:, :512], ap(t["b1r"], 0, [(32, 128), (0, 16), (1, 32)])),
                    waits=("sync",))
                b.op("vector", lambda E: E.tensor_copy(
                    t["Atie"][:, :512], ap(t["b2r"], 0, [(8, 128), (0, 64), (1, 8)])))
                b.op("vector", lambda E: E.tensor_copy(t["q8"][:, :512], t["At1"][:, :512]))
                b.op("vector", lambda E: E.tensor_copy(t["Ar"][:, :512], t["q8"][:, :512]))
                b.op("vector", lambda E: E.tensor_copy(t["q8"][:, :512], t["Atie"][:, :512]))
                b.op("vector", lambda E: E.tensor_copy(t["Afl"][:, :512], t["q8"][:, :512]))
                b.op("tensor", lambda E: E.transpose(t["pF"][:32, :128], t["Ar"][:, :32],
                                                     t["sidentf"][:, :]), waits=("vector",))
                b.op("vector", lambda E: E.tensor_copy(t["b1qT"][:, :], t["pF"][:32, :128]),
                     waits=("tensor",))
                b.op("tensor", lambda E: E.transpose(t["pF"][:8, :128], t["Afl"][:, :8],
                                                     t["sidentf"][:, :]), waits=("vector",))
                b.op("vector", lambda E: E.tensor_copy(t["b2qT"][:, :], t["pF"][:8, :128]),
                     waits=("tensor",))

                xbuf_free = {0: 0, 1: 0}
                fm_free = {0: 0, 1: 0}
                qb_free = {0: 0, 1: 0}
                ln_done = {}

                def emit_ln(n):
                    bs = n % 2
                    tok = slice(128 * n, 128 * (n + 1))
                    xb, xnn = t[f"xbuf{bs}"], t[f"xn{bs}"]
                    fm, qb = t[f"xnfmb{bs}"], t[f"xnqb{bs}"]
                    b.dma("sync", xb[:, :], x_h[tok, :],
                          waits_abs=[("vector", xbuf_free[bs])])
                    b.op("vector", lambda E: E.reduce_sum(t["red"][:, :], xb[:, :], AX.X),
                         waits=("sync",))
                    b.op("vector", lambda E: E.tensor_tensor(t["Asq"][:, :1024], xb[:, :],
                                                             xb[:, :], ALU.mult))
                    b.op("vector", lambda E: E.tensor_single_scalar(t["mu"][:, :], t["red"][:, :],
                                                                    1.0 / 1024.0, ALU.mult))
                    b.op("vector", lambda E: E.reduce_sum(t["red"][:, :], t["Asq"][:, :1024], AX.X))
                    b.op("vector", lambda E: E.tensor_tensor(t["xc"][:, :], xb[:, :],
                         ap(t["mu"], 0, [(1, 128), (0, 1024)]), ALU.subtract))
                    xbuf_free[bs] = b.mark("vector")
                    b.op("scalar", lambda E: E.activation(t["musq"][:, :], t["mu"][:, :],
                                                          ACT.Square), waits=("vector",))
                    b.op("vector", lambda E: E.scalar_tensor_tensor(t["var"][:, :], t["red"][:, :],
                         1.0 / 1024.0, t["musq"][:, :], ALU.mult, ALU.subtract),
                         waits=("scalar",))
                    b.op("scalar", lambda E: E.activation(t["var"][:, :], t["var"][:, :], ACT.Ln,
                                                          bias=t["eps5"][:, :], scale=1.0),
                         waits=("vector",))
                    b.op("scalar", lambda E: E.activation(t["Alg"][:, :512], t["Asq"][:, :512],
                                                          ACT.Copy))
                    b.op("scalar", lambda E: E.activation(t["rstd"][:, :], t["var"][:, :], ACT.Exp,
                                                          scale=-0.5))
                    b.op("vector", lambda E: E.scalar_tensor_tensor(xnn[:, :], t["xc"][:, :],
                         t["rstd"][:, :], t["xc"][:, :], ALU.mult, ALU.bypass),
                         waits=("scalar",))
                    for rnd in range(2):
                        for j in range(4):
                            fc = 4 * rnd + j
                            b.op("tensor", lambda E, fc=fc, j=j: E.transpose(
                                t["pL"][:, 128 * j:128 * (j + 1)],
                                xnn[:, 128 * fc:128 * (fc + 1)], t["sidentf"][:, :]),
                                waits=("vector", "scalar") if j == 0 else (),
                                waits_abs=[("tensor", fm_free[bs])] if rnd == 0 and j == 0 else [])
                        b.op("scalar", lambda E, rnd=rnd: E.activation(
                            fm[:, 512 * rnd:512 * (rnd + 1)], t["pL"][:, :], ACT.Copy),
                            waits=("tensor",))
                        b.op("scalar", lambda E: E.activation(t["q8"][:, :512], t["pL"][:, :],
                                                              ACT.Copy))
                        b.op("vector", lambda E, rnd=rnd: E.tensor_copy(
                            qb[:, 512 * rnd:512 * (rnd + 1)], t["q8"][:, :512]),
                            waits=("scalar",),
                            waits_abs=[("gpsimd", qb_free[bs])] if rnd == 0 else [])
                    dbg_dump("xn", xnn[:, :], (128, 1024), when=(n == 0))
                    dbg_dump("xnfmb", fm[:, :], (128, 1024), BF16, when=(n == 0))
                    for fc in range(8):
                        b.dma("gpsimd", xnq_s[128 * fc:128 * (fc + 1), tok],
                              qb[:, 128 * fc:128 * (fc + 1)], waits=("vector",))
                    qb_free[bs] = b.mark("gpsimd")
                    ln_done[n] = {"v": b.mark("vector"), "s": b.mark("scalar")}

                if not skip_a:
                    emit_ln(0)
                for n in range(NCH if not skip_a else 0):
                    bs = n % 2
                    tok = slice(128 * n, 128 * (n + 1))
                    if n + 1 < NCH:
                        emit_ln(n + 1)
                    fmb = t[f"xnfmb{bs}"]
                    if astage < 2:
                        continue
                    for fc in range(8):
                        b.op("tensor", lambda E, fc=fc, fmb=fmb: E.matmul(
                            t["pA"][:, :512], fmb[:, 128 * fc:128 * (fc + 1)],
                            t["wvb"][:, 512 * fc:512 * (fc + 1)],
                            start=(fc == 0), stop=(fc == 7), skip_group_check=True),
                            waits_abs=([("vector", ln_done[n]["v"]),
                                        ("scalar", ln_done[n]["s"])] if fc == 0 else []))
                    for fc in range(8):
                        b.op("tensor", lambda E, fc=fc, fmb=fmb: E.matmul(
                            t["pB"][:, :96], fmb[:, 128 * fc:128 * (fc + 1)],
                            t["wselb"][:, 96 * fc:96 * (fc + 1)],
                            start=(fc == 0), stop=(fc == 7), skip_group_check=True))
                    for (wb, bank) in (("wkb", "pD"), ("wrb", "pE")):
                        for dc in range(4):
                            for fc in range(8):
                                b.op("tensor", lambda E, wb=wb, bank=bank, dc=dc, fc=fc, fmb=fmb: E.matmul(
                                    t[bank][:, 128 * dc:128 * (dc + 1)],
                                    t[wb][:, 512 * fc + 128 * dc:512 * fc + 128 * (dc + 1)],
                                    fmb[:, 128 * fc:128 * (fc + 1)],
                                    start=(fc == 0), stop=(fc == 7), skip_group_check=True))
                    b.op("vector", lambda E: E.reduce_sum(t["red"][:, :], t["pB"][:, 32:96], AX.X),
                         waits=("tensor",))
                    b.op("vector", lambda E: E.tensor_copy(t["vrawf"][:, :], t["pA"][:, :512]))
                    b.op("vector", lambda E: E.tensor_single_scalar(t["btok"][:, :], t["red"][:, :],
                                                                    1.0 / 64.0, ALU.mult))
                    b.op("vector", lambda E: E.tensor_copy(t["kev"][:, :], t["pD"][:, :512]))
                    b.op("vector", lambda E: E.scalar_tensor_tensor(t["vpb"][:, :], t["pA"][:, :512],
                         t["btok"][:, :], t["minus1f"][:, :], ALU.mult, ALU.bypass))
                    b.op("vector", lambda E: E.tensor_copy(t["dtev"][:, :32], t["pB"][:, 0:32]))
                    b.op("tensor", lambda E: E.transpose(t["pF"][:32, :128], t["dtev"][:, :32],
                                                         t["sidentf"][:, :]),
                         waits=("vector", "scalar"))
                    b.op("vector", lambda E: E.tensor_copy(t["dtrawb"][:, :], t["pF"][:32, :128]),
                         waits=("tensor",))
                    if astage < 3:
                        continue
                    b.op("scalar", lambda E: E.activation(t["sigr"][:, :], t["pE"][:, :512],
                                                          ACT.Exp, scale=-1.0), waits=("vector",))
                    b.op("scalar", lambda E: E.activation(t["sigr"][:, :], t["sigr"][:, :], ACT.Ln,
                                                          bias=t["one1"][:, :], scale=1.0))
                    b.op("scalar", lambda E: E.activation(t["sigr"][:, :], t["sigr"][:, :], ACT.Exp,
                                                          scale=-1.0))
                    for dc in range(4):
                        for fc in range(8):
                            b.op("tensor", lambda E, dc=dc, fc=fc, fmb=fmb: E.matmul(
                                t["pE"][:, 128 * dc:128 * (dc + 1)],
                                t["wgb"][:, 512 * fc + 128 * dc:512 * fc + 128 * (dc + 1)],
                                fmb[:, 128 * fc:128 * (fc + 1)],
                                start=(fc == 0), stop=(fc == 7), skip_group_check=True),
                                waits=("scalar",) if (dc == 0 and fc == 0) else ())
                    fm_free[bs] = b.mark("tensor")
                    b.op("scalar", lambda E: E.activation(t["sigg"][:, :], t["pE"][:, :512],
                                                          ACT.Exp, scale=-1.0), waits=("tensor",))
                    b.op("scalar", lambda E: E.activation(t["sigg"][:, :], t["sigg"][:, :], ACT.Ln,
                                                          bias=t["one1"][:, :], scale=1.0))
                    b.op("scalar", lambda E: E.activation(t["sigg"][:, :], t["sigg"][:, :], ACT.Exp,
                                                          scale=-1.0))
                    b.op("vector", lambda E: E.tensor_tensor(t["rg"][:, :], t["sigr"][:, :],
                                                             t["sigg"][:, :], ALU.mult),
                         waits=("scalar",))
                    for dc in range(4):
                        b.op("tensor", lambda E, dc=dc: E.matmul(
                            t["pB"][:, 128 * dc:128 * (dc + 1)],
                            t["dtwb"][:, 128 * dc:128 * (dc + 1)], t["dtrawb"][:, :],
                            start=True, stop=True, skip_group_check=True), waits=("vector",) if dc == 0 else ())
                    for c4 in range(4):
                        b.op("scalar", lambda E, c4=c4: E.activation(
                            t["ez"][:, 128 * c4:128 * (c4 + 1)], t["pB"][:, 128 * c4:128 * (c4 + 1)],
                            ACT.Exp, bias=t["sdtb"][:, c4:c4 + 1], scale=1.0),
                            waits=("tensor", "vector") if c4 == 0 else ("tensor",))
                    b.op("scalar", lambda E: E.activation(t["delta"][:, :], t["ez"][:, :], ACT.Ln,
                                                          bias=t["one1"][:, :], scale=1.0))
                    for c4 in range(4):
                        b.op("vector", lambda E, c4=c4: E.scalar_tensor_tensor(
                            t["ell"][:, 128 * c4:128 * (c4 + 1)],
                            t["delta"][:, 128 * c4:128 * (c4 + 1)],
                            t["bdneg"][:, c4:c4 + 1],
                            t["delta"][:, 128 * c4:128 * (c4 + 1)], ALU.mult, ALU.bypass),
                            waits=("scalar",) if c4 == 0 else ())
                    for c4 in range(4):
                        b.op("vector", lambda E, c4=c4: E.tensor_tensor_scan(
                            t["Pfull"][:, 128 * c4:128 * (c4 + 1)],
                            t["ell"][:, 128 * c4:128 * (c4 + 1)],
                            t["ell"][:, 128 * c4:128 * (c4 + 1)], 0.0, ALU.add, ALU.bypass))
                    for c4 in range(3):
                        b.op("vector", lambda E, c4=c4: E.scalar_tensor_tensor(
                            t["farg"][:, 128 * c4:128 * (c4 + 1)],
                            t["Pfull"][:, 128 * c4:128 * (c4 + 1)],
                            t["Pfull"][:, 128 * c4 + 127:128 * c4 + 128],
                            t["minus1f"][:, 128 * c4:128 * (c4 + 1)], ALU.subtract, ALU.mult))
                    b.op("vector", lambda E: E.tensor_copy(t["Pmid"][:, :],
                         ap(t["Pfull"], 8, [(512, 128), (16, 32)])))
                    b.op("vector", lambda E: E.tensor_copy(t["Pb64"][:, :],
                         ap(t["Pfull"], 63, [(512, 128), (128, 4)])))
                    b.op("vector", lambda E: E.tensor_copy(t["Pb32"][:, :],
                         ap(t["Pfull"], 31, [(512, 128), (64, 8)])))
                    b.op("vector", lambda E: E.tensor_copy(t["Pb16"][:, :],
                         ap(t["Pfull"], 15, [(512, 128), (32, 16)])))
                    b.op("vector", lambda E: E.tensor_copy(t["Asq"][:, :512],
                                                           t["Pfull"][:, :]))
                    b.op("vector", lambda E: E.scalar_tensor_tensor(
                        t["farg"][:, 384:512], t["Pfull"][:, 384:512],
                        t["Pfull"][:, 511:512],
                        t["minus1f"][:, 384:512], ALU.subtract, ALU.mult))
                    b.op("vector", lambda E: E.tensor_tensor(t["e1arg"][:, :], t["Pfull"][:, :],
                         ap(t["Pmid"], 0, [(32, 128), (1, 32), (0, 16)]), ALU.subtract))
                    b.op("vector", lambda E: E.tensor_tensor(t["s64"][:, :], t["Pfull"][:, :],
                         ap(t["Pb64"], 0, [(4, 128), (1, 4), (0, 128)]), ALU.subtract))
                    b.op("vector", lambda E: E.tensor_tensor(t["s32"][:, :], t["Pfull"][:, :],
                         ap(t["Pb32"], 0, [(8, 128), (1, 8), (0, 64)]), ALU.subtract))
                    b.op("vector", lambda E: E.tensor_tensor(t["s16"][:, :], t["Pfull"][:, :],
                         ap(t["Pb16"], 0, [(16, 128), (1, 16), (0, 32)]), ALU.subtract))
                    if astage < 4:
                        continue
                    b.op("scalar", lambda E: E.activation(t["expE"][:, :], t["Pfull"][:, :], ACT.Exp),
                         waits=("vector",))
                    b.op("scalar", lambda E: E.activation(t["expF"][:, :], t["farg"][:, :], ACT.Exp))
                    b.op("scalar", lambda E: E.activation(t["expE1"][:, :], t["e1arg"][:, :], ACT.Exp))
                    b.op("scalar", lambda E: E.activation(t["expE2"][:, :], t["e1arg"][:, :], ACT.Exp,
                                                          scale=-1.0))
                    b.op("scalar", lambda E: E.activation(t["xq64"][:, :],
                         ap(t["s64"], 64, [(512, 128), (128, 4), (1, 64)]), ACT.Exp))
                    b.op("scalar", lambda E: E.activation(t["xk64"][:, :],
                         ap(t["s64"], 0, [(512, 128), (128, 4), (1, 64)]), ACT.Exp, scale=-1.0))
                    b.op("scalar", lambda E: E.activation(t["xq32"][:, :],
                         ap(t["s32"], 32, [(512, 128), (64, 8), (1, 32)]), ACT.Exp))
                    b.op("scalar", lambda E: E.activation(t["xk32"][:, :],
                         ap(t["s32"], 0, [(512, 128), (64, 8), (1, 32)]), ACT.Exp, scale=-1.0))
                    b.op("scalar", lambda E: E.activation(t["xq16"][:, :],
                         ap(t["s16"], 16, [(512, 128), (32, 16), (1, 16)]), ACT.Exp))
                    b.op("scalar", lambda E: E.activation(t["xk16"][:, :],
                         ap(t["s16"], 0, [(512, 128), (32, 16), (1, 16)]), ACT.Exp, scale=-1.0))
                    b.op("vector", lambda E: E.tensor_tensor(t["qtb"][:, :], t["kev"][:, :],
                         t["expE"][:, :], ALU.mult), waits=("scalar",))
                    b.op("vector", lambda E: E.tensor_tensor(t["khf"][:, :], t["kev"][:, :],
                         t["expF"][:, :], ALU.mult))
                    b.op("vector", lambda E: E.tensor_tensor(t["ke1"][:, :], t["kev"][:, :],
                         t["expE1"][:, :], ALU.mult))
                    b.op("vector", lambda E: E.tensor_tensor(t["ke2"][:, :], t["kev"][:, :],
                         t["expE2"][:, :], ALU.mult))
                    b.op("vector", lambda E: E.tensor_tensor(t["kq64b"][:, :],
                         ap(t["kev"], 64, [(512, 128), (128, 4), (1, 64)]), t["xq64"][:, :], ALU.mult))
                    b.op("vector", lambda E: E.tensor_tensor(t["kk64b"][:, :],
                         ap(t["kev"], 0, [(512, 128), (128, 4), (1, 64)]), t["xk64"][:, :], ALU.mult))
                    b.op("vector", lambda E: E.tensor_tensor(t["kq32b"][:, :],
                         ap(t["kev"], 32, [(512, 128), (64, 8), (1, 32)]), t["xq32"][:, :], ALU.mult))
                    b.op("vector", lambda E: E.tensor_tensor(t["kk32b"][:, :],
                         ap(t["kev"], 0, [(512, 128), (64, 8), (1, 32)]), t["xk32"][:, :], ALU.mult))
                    b.op("vector", lambda E: E.tensor_tensor(t["kq16b"][:, :],
                         ap(t["kev"], 16, [(512, 128), (32, 16), (1, 16)]), t["xq16"][:, :], ALU.mult))
                    b.op("vector", lambda E: E.tensor_tensor(t["kk16b"][:, :],
                         ap(t["kev"], 0, [(512, 128), (32, 16), (1, 16)]), t["xk16"][:, :], ALU.mult))
                    dbg_dump("kev", t["kev"][:, :], (128, 512), when=(n == 0))
                    dbg_dump("vpb", t["vpb"][:, :], (128, 512), BF16, when=(n == 0))
                    dbg_dump("Pfull", t["Pfull"][:, :], (128, 512), when=(n == 0))
                    dbg_dump("expE", t["expE"][:, :], (128, 512), when=(n == 0))
                    dbg_dump("qtb", t["qtb"][:, :], (128, 512), BF16, when=(n == 0))
                    dbg_dump("rg", t["rg"][:, :], (128, 512), when=(n == 0))
                    dbg_dump("btok", t["btok"][:, :], (128, 1), when=(n == 0))
                    if astage < 5:
                        continue
                    b.op("vector", lambda E: E.tensor_copy(t["Sb"][:, :], t["Stile"][:, :]))
                    for (srcn, dstn) in (("qtb", "qtbB"), ("ke1", "ke1B"), ("ke2", "ke2B"),
                                         ("kq64b", "kq64B"), ("kk64b", "kk64B"),
                                         ("kq32b", "kq32B"), ("kk32b", "kk32B"),
                                         ("kq16b", "kq16B"), ("kk16b", "kk16B"),
                                         ("Sb", "SbB")):
                        b.dma("gpsimd", t[dstn][:, :], t[srcn][64:128, :], waits=("vector",))
                    if astage < 5.2:
                        continue
                    for p in range(4):
                        b.op("tensor", lambda E, p=p: E.transpose(
                            t["pF"][:, 128 * p:128 * (p + 1)],
                            t["khf"][:, 128 * p:128 * (p + 1)], t["sidentf"][:, :]),
                            waits=("vector", "scalar") if p == 0 else ())
                    if astage < 5.3:
                        continue
                    b.op("vector", lambda E: E.tensor_copy(t["khtm"][:, :], t["pF"][:, :512]),
                         waits=("tensor",))
                    if astage < 5.4:
                        continue
                    for p in range(4):
                        b.op("tensor", lambda E, p=p: E.transpose(
                            t["pF"][:, 128 * p:128 * (p + 1)],
                            t["vrawf"][:, 128 * p:128 * (p + 1)], t["sidentf"][:, :]),
                            waits=("vector",) if p == 0 else ())
                    if astage < 5.5:
                        continue
                    b.op("vector", lambda E: E.tensor_copy(t["vfm"][:, :], t["pF"][:, :512]),
                         waits=("tensor",))
                    if astage < 5.6:
                        continue
                    for rnd in range(2):
                        import os as _os3
                        for j in range(4):
                            h = 4 * rnd + j
                            p, hh = h // 2, h % 2
                            if _os3.environ.get("NOTP"):
                                tp = None
                            else:
                                tp = (64 * hh, 0)
                            if _os3.environ.get("BASE0"):
                                hh = 0
                                tp = None if _os3.environ.get("NOTP") else (0, 0)
                            ke2s = (t["ke2"][0:64, 128 * p:128 * (p + 1)] if hh == 0
                                    else t["ke2B"][:, 128 * p:128 * (p + 1)])
                            ke1s = (t["ke1"][0:64, 128 * p:128 * (p + 1)] if hh == 0
                                    else t["ke1B"][:, 128 * p:128 * (p + 1)])
                            b.op("tensor", lambda E, j=j, ke2s=ke2s, ke1s=ke1s: E.matmul(
                                t["pG"][:, 128 * j:128 * (j + 1)], ke2s, ke1s,
                                start=(j == 0), stop=False, skip_group_check=True),
                                waits=("vector", "gpsimd") if j == 0 else ())
                        for j in range(4 if astage >= 5.7 else 0):
                            h = 4 * rnd + j
                            p, hh = h // 2, h % 2
                            kkس = (t["kk64b"][0:64, 64 * p:64 * (p + 1)] if hh == 0
                                   else t["kk64B"][:, 64 * p:64 * (p + 1)])
                            kqس = (t["kq64b"][0:64, 64 * p:64 * (p + 1)] if hh == 0
                                   else t["kq64B"][:, 64 * p:64 * (p + 1)])
                            b.op("tensor", lambda E, j=j, a=kkس, c=kqس: E.matmul(
                                t["pG"][0:64, 128 * j + 64:128 * (j + 1)], a, c,
                                start=True, stop=False, skip_group_check=True))
                            for q2 in range(2 if astage >= 5.8 else 0):
                                a32 = (t["kk32b"][0:64, 32 * (2 * p + q2):32 * (2 * p + q2) + 32] if hh == 0
                                       else t["kk32B"][:, 32 * (2 * p + q2):32 * (2 * p + q2) + 32])
                                c32_ = (t["kq32b"][0:64, 32 * (2 * p + q2):32 * (2 * p + q2) + 32] if hh == 0
                                        else t["kq32B"][:, 32 * (2 * p + q2):32 * (2 * p + q2) + 32])
                                b.op("tensor", lambda E, j=j, q2=q2, a=a32, c=c32_: E.matmul(
                                    t["pG"][64 * q2:64 * q2 + 32,
                                            128 * j + 64 * q2 + 32:128 * j + 64 * q2 + 64],
                                    a, c, start=True, stop=False, skip_group_check=True,
                                    tile_position=(0, 64 * q2)))
                            for q in range(4 if astage >= 5.9 else 0):
                                a16 = (t["kk16b"][0:64, 16 * (4 * p + q):16 * (4 * p + q) + 16] if hh == 0
                                       else t["kk16B"][:, 16 * (4 * p + q):16 * (4 * p + q) + 16])
                                c16_ = (t["kq16b"][0:64, 16 * (4 * p + q):16 * (4 * p + q) + 16] if hh == 0
                                        else t["kq16B"][:, 16 * (4 * p + q):16 * (4 * p + q) + 16])
                                b.op("tensor", lambda E, j=j, q=q, a=a16, c=c16_: E.matmul(
                                    t["pG"][32 * q:32 * q + 16,
                                            128 * j + 32 * q + 16:128 * j + 32 * q + 32],
                                    a, c, start=True, stop=(j == 3 and q == 3), skip_group_check=True,
                                    tile_position=(0, 32 * q)))
                        for j in range(4):
                            b.op("vector", lambda E, j=j, rnd=rnd: E.select(
                                t["Amb"][:, 512 * rnd + 128 * j:512 * rnd + 128 * (j + 1)],
                                t["sutm"][:, :],
                                t["pG"][:, 128 * j:128 * (j + 1)], t["zero"][:, :]),
                                waits=("tensor",) if j == 0 else ())
                    if astage < 6:
                        continue
                    for h in range(8):
                        p, hh = h // 2, h % 2
                        Sbs = (t["Sb"][0:64, 64 * p:64 * (p + 1)] if hh == 0
                               else t["SbB"][:, 64 * p:64 * (p + 1)])
                        qts = (t["qtb"][0:64, 128 * p:128 * (p + 1)] if hh == 0
                               else t["qtbB"][:, 128 * p:128 * (p + 1)])
                        b.op("tensor", lambda E, p=p, hh=hh, a=Sbs, c=qts: E.matmul(
                            t["pH"][64 * hh:64 * (hh + 1), 128 * p:128 * (p + 1)],
                            a, c, start=True, stop=False, skip_group_check=True,
                            tile_position=(0, 64 * hh)),
                            waits=("vector", "gpsimd") if h == 0 else ())
                        b.op("tensor", lambda E, p=p, hh=hh, h=h: E.matmul(
                            t["pH"][64 * hh:64 * (hh + 1), 128 * p:128 * (p + 1)],
                            t["vpb"][:, 64 * h:64 * (h + 1)],
                            t["Amb"][:, 128 * h:128 * (h + 1)],
                            start=False, stop=True, skip_group_check=True, tile_position=(0, 64 * hh)))
                    b.op("vector", lambda E: E.tensor_tensor(t["kv"][:, :], t["kev"][:, :],
                         t["vfm"][:, :], ALU.mult))
                    b.op("vector", lambda E: E.tensor_tensor(t["bonus"][:, :], t["kv"][:, :],
                         ap(t["stf"], 0, [(4, 128), (1, 4), (0, 128)]), ALU.mult))
                    b.op("vector", lambda E: E.tensor_tensor(t["yst"][:, :], t["pH"][:, :512],
                         t["bonus"][:, :], ALU.add), waits=("tensor",))
                    b.op("vector", lambda E, n=n: E.tensor_tensor(
                        ap(t["ys"], 128 * n, [(4 * L, 128), (L, 4), (1, 128)]),
                        t["yst"][:, :], t["rg"][:, :], ALU.mult))
                    dbg_dump("Amb", t["Amb"][:, :], (128, 1024), BF16, when=(n == 0))
                    dbg_dump("ke1", t["ke1"][:, :], (128, 512), BF16, when=(n == 0))
                    dbg_dump("ke2", t["ke2"][:, :], (128, 512), BF16, when=(n == 0))
                    dbg_dump("kq64b", t["kq64b"][:, :], (128, 256), BF16, when=(n == 0))
                    dbg_dump("kk64b", t["kk64b"][:, :], (128, 256), BF16, when=(n == 0))
                    dbg_dump("yst", t["yst"][:, :], (128, 512), when=(n == 0))
                    if astage < 7:
                        continue
                    for h in range(8):
                        p, hh = h // 2, h % 2
                        b.op("tensor", lambda E, p=p, hh=hh, h=h: E.matmul(
                            t["pG"][64 * hh:64 * (hh + 1), 64 * p:64 * (p + 1)],
                            t["khtm"][:, 128 * p + 64 * hh:128 * p + 64 * (hh + 1)],
                            t["vpb"][:, 64 * h:64 * (h + 1)],
                            start=True, stop=True, skip_group_check=True, tile_position=(0, 64 * hh)),
                            waits=("vector",) if h == 0 else ())
                    for p in range(4):
                        b.op("vector", lambda E, p=p: E.scalar_tensor_tensor(
                            t["Stile"][:, 64 * p:64 * (p + 1)],
                            t["Stile"][:, 64 * p:64 * (p + 1)],
                            t["expE"][:, 128 * p + 127:128 * p + 128],
                            t["pG"][:, 64 * p:64 * (p + 1)], ALU.mult, ALU.add),
                            waits=("tensor",) if p == 0 else ())

            if debug:
                b.op("vector", lambda E: E.tensor_copy(t["Asq"][:, :256], t["Stile"][:, :]))
                dbg_dump("Sfinal", t["Asq"][:, :256], (128, 256))
            if skip_b:
                for g in range(NG):
                    for dc in range(8):
                        b.dma("gpsimd", out_h[128 * dc:128 * (dc + 1), 512 * g:512 * (g + 1)],
                              t["Asq"][:, :512], waits=("vector",))
                b.op("gpsimd", lambda E: E.memset(t["zero"][:1, :1], 0.0),
                     waits=("gpsimd",), inc_k=0)
                b.finalize(block)
                return nc
            esB = contextlib.ExitStack()
            with esB:
                esB1 = contextlib.ExitStack()
                esB1.__enter__()
                sb("w1qb", [128, 16384], BF16, esB1)
                sb("w2qb", [128, 16384], BF16, esB1)
                sb("xnqg", [128, 4096], BF16, esB1)
                sb("xnqg2", [128, 4096], BF16, esB1)
                sb("wstage2", [128, 1024], F32, esB1)
                sb("hq", [128, 8192], BF16, esB1)
                sb("y2g", [128, 1024], F32, esB1)
                psum("qA", [128, 512], F32, esB1)
                psum("qB", [128, 512], F32, esB1)
                psum("qC", [128, 512], F32, esB1)
                psum("qD", [128, 512], F32, esB1)
                sb("b1row", [1, 4096], BF16, esB1); sb("b2row", [1, 1024], BF16, esB1)
                sb("onesbf", [1, 512], BF16, esB1)
                b.op("vector", lambda E: E.memset(t["onesbf"][:, :], 1.0))
                b.dma("gpsimd", ap(t["b1row"], 0, [(4096, 1), (1, 4096)]),
                      ap(t["b1qT"], 0, [(128, 32), (1, 128)]), waits=("vector",))
                b.dma("gpsimd", ap(t["b2row"], 0, [(1024, 1), (1, 1024)]),
                      ap(t["b2qT"], 0, [(128, 8), (1, 128)]), waits=("vector",))

                xnq_gp_mark = b.mark("gpsimd")
                y2_done = {}
                for half in range(2):
                    qd_free = 0
                    wq_guard = [("tensor", b.mark("tensor"))] if half == 1 else []
                    first = True
                    wtiles = ([("w1", nt) for nt in range(16)]
                              + [("w2", (dc, qrt)) for dc in range(8) for qrt in range(2)])
                    wbuf_free = {0: 0, 1: 0}
                    ld_done = {}
                    for ti, (kind, idx) in enumerate(wtiles):
                        bi = ti % 2
                        buf = t["wstage"] if bi == 0 else t["wstage2"]
                        q = "sync" if bi == 0 else "gpsimd"
                        if kind == "w1":
                            nt = idx
                            srcap = w1_h[2048 * half + 128 * nt:2048 * half + 128 * (nt + 1), :]
                        else:
                            dc, qrt = idx
                            col = 2048 * half + 1024 * qrt
                            srcap = w2_h[128 * dc:128 * (dc + 1), col:col + 1024]
                        wa = [("tensor", wbuf_free[bi])]
                        if first:
                            wa = [("tensor", b.mark("tensor")), ("scalar", b.mark("scalar")),
                                  ("vector", b.mark("vector"))]
                        b.dma(q, buf[:, :1024], srcap, waits_abs=wa)
                        first = False
                        ld_done[ti] = (q, b.mark(q))
                        for rnd in range(2):
                            for j in range(4):
                                fc = 4 * rnd + j
                                b.op("tensor", lambda E, fc=fc, j=j, buf=buf: E.transpose(
                                    t["qD"][:, 128 * j:128 * (j + 1)],
                                    buf[:, 128 * fc:128 * (fc + 1)], t["sidentf"][:, :]),
                                    waits_abs=([ld_done[ti], ("scalar", qd_free)]
                                               if j == 0 else []))
                            if kind == "w1":
                                out = ap(t["w1qb"], 2048 * 4 * rnd + 128 * nt,
                                         [(16384, 128), (2048, 4), (1, 128)])
                            else:
                                out = ap(t["w2qb"], 1024 * (8 * qrt + 4 * rnd) + 128 * dc,
                                         [(16384, 128), (1024, 4), (1, 128)])
                            b.op("scalar", lambda E: E.activation(t["q8"][:, :512],
                                                                  t["qD"][:, :512], ACT.Copy),
                                 waits=("tensor", "vector"))
                            qd_free = b.mark("scalar")
                            b.op("vector", lambda E, out=out: E.tensor_copy(t["q8" if False else "q8"][:, :512], t["q8"][:, :512]) if False else E.tensor_copy(out, t["q8"][:, :512]),
                                 waits=("scalar",), waits_abs=wq_guard)
                            wq_guard = []
                        wbuf_free[bi] = b.mark("tensor")
                    w_ready = b.mark("vector")
                    bank_free = {"qA": 0, "qB": 0}
                    qc_free = 0
                    if half == 0:
                        y2dma_mark = {0: 0, 1: 0}
                    xnqg_rd = {0: b.mark("tensor"), 1: b.mark("tensor")}
                    pair_mm_mark = {}

                    def xq_of(g):
                        return t["xnqg" if g % 2 == 0 else "xnqg2"]

                    def emit_loads(g):
                        gcol = slice(512 * g, 512 * (g + 1))
                        for fc in range(8):
                            b.dma("sync", xq_of(g)[:, 512 * fc:512 * (fc + 1)],
                                  xnq_s[128 * fc:128 * (fc + 1), gcol],
                                  waits_abs=([("tensor", xnqg_rd[g % 2]),
                                              ("gpsimd", xnq_gp_mark)] if fc == 0 else []))
                        return b.mark("sync")

                    def emit_mm(g, pair, ld_mark):
                        for s_, bank in ((0, "qA"), (1, "qB")):
                            nch = 2 * pair + s_
                            for fc in range(8):
                                b.op("tensor", lambda E, bank=bank, nch=nch, fc=fc, g=g: E.matmul(
                                    t[bank][:, :512],
                                    t["w1qb"][:, 2048 * fc + 128 * nch:2048 * fc + 128 * (nch + 1)],
                                    xq_of(g)[:, 512 * fc:512 * (fc + 1)],
                                    start=(fc == 0), stop=False, skip_group_check=True),
                                    waits_abs=([("sync", ld_mark), ("vector", w_ready),
                                                ("scalar", bank_free[bank])] if fc == 0 else []))
                            ng = 16 * half + nch
                            b.op("tensor", lambda E, bank=bank, ng=ng: E.matmul(
                                t[bank][:, :512],
                                t["b1row"][0:1, 128 * ng:128 * (ng + 1)],
                                t["onesbf"][:, :512], start=False, stop=True, skip_group_check=True),
                                waits=("gpsimd",) if ng == 0 else ())
                        pair_mm_mark[(g, pair)] = b.mark("tensor")

                    ld_marks = {0: emit_loads(0)}
                    emit_mm(0, 0, ld_marks[0])
                    ffn2_end = 0
                    for g in range(NG):
                        if g + 1 < NG and (g + 1) not in ld_marks:
                            ld_marks[g + 1] = emit_loads(g + 1)
                        for pair in range(8):
                            for s_, bank in ((0, "qA"), (1, "qB")):
                                nch = 2 * pair + s_
                                ow = ([("tensor", ffn2_end)]
                                      if (pair == 0 and s_ == 0) else [])
                                b.op("scalar", lambda E, bank=bank: E.activation(
                                    t["qb16"][:, :512], t[bank][:, :512], ACT.Copy),
                                    waits=("vector",),
                                    waits_abs=([("tensor", pair_mm_mark[(g, pair)])]
                                               if s_ == 0 else []))
                                bank_free[bank] = b.mark("scalar")
                                b.op("scalar", lambda E: E.activation(
                                    t["q8"][:, :512], t["qb16"][:, :512], ACT.Relu))
                                b.op("vector", lambda E, nch=nch: E.tensor_copy(
                                    t["hq"][:, 512 * nch:512 * (nch + 1)], t["q8"][:, :512]),
                                    waits=("scalar",), waits_abs=ow)
                            if pair + 1 < 8:
                                emit_mm(g, pair + 1, ld_marks[g])
                            elif g + 1 < NG:
                                emit_mm(g + 1, 0, ld_marks[g + 1])
                        xnqg_rd[g % 2] = b.mark("tensor")
                        dbg_dump("hq", t["hq"][:, :], (128, 8192), BF16,
                                 when=(half == 0 and g == 0))
                        hq_mark = b.mark("vector")
                        for dc in range(8):
                            for ncq in range(16):
                                b.op("tensor", lambda E, dc=dc, ncq=ncq: E.matmul(
                                    t["qC"][:, :512],
                                    t["w2qb"][:, 1024 * ncq + 128 * dc:1024 * ncq + 128 * (dc + 1)],
                                    t["hq"][:, 512 * ncq:512 * (ncq + 1)],
                                    start=(ncq == 0), stop=(half == 1 and ncq == 15), skip_group_check=True),
                                    waits_abs=([("vector", hq_mark), ("scalar", qc_free)]
                                               if ncq == 0 else [])),
                            if half == 0:
                                b.op("tensor", lambda E, dc=dc: E.matmul(
                                    t["qC"][:, :512], t["b2row"][0:1, 128 * dc:128 * (dc + 1)],
                                    t["onesbf"][:, :512], start=False, stop=True, skip_group_check=True))
                            slot = dc % 2
                            b.op("scalar", lambda E, slot=slot: E.activation(
                                t["y2g"][:, 512 * slot:512 * (slot + 1)], t["qC"][:, :512], ACT.Copy),
                                waits=("tensor",),
                                waits_abs=[("gpsimd", y2dma_mark[slot])])
                            qc_free = b.mark("scalar")
                            tgt = y2_s if half == 0 else y2_s1
                            b.dma("gpsimd", tgt[128 * dc:128 * (dc + 1), 512 * g:512 * (g + 1)],
                                  t["y2g"][:, 512 * slot:512 * (slot + 1)], waits=("scalar",))
                            y2dma_mark[slot] = b.mark("gpsimd")
                        if half == 1:
                            y2_done[g] = b.mark("gpsimd")
                        ffn2_end = b.mark("tensor")


                esB1.__exit__(None, None, None)
                sb("wob", [128, 4096], BF16, esB)
                sb("y2f", [128, 4096], F32, esB); sb("y2f2", [128, 4096], F32, esB)
                sb("y2h", [128, 4096], F32, esB); sb("y2h2", [128, 4096], F32, esB)
                sb("xg", [128, 4096], F32, esB); sb("xg2", [128, 4096], F32, esB)
                sb("maskg", [128, 4096], BF16, esB)
                sb("outt", [128, 512], F32, esB)
                psum("pJ", [128, 512], F32, esB)
                psum("pK", [128, 512], F32, esB)
                for vc in range(4):
                    b.dma("sync", t["wstage"][:, :1024], wo_h[128 * vc:128 * (vc + 1), :],
                          waits=("vector", "tensor"))
                    b.op("vector", lambda E, vc=vc: E.tensor_copy(
                        t["wob"][:, 1024 * vc:1024 * (vc + 1)], t["wstage"][:, :1024]),
                        waits=("sync",))

                fb_free = {0: {"v": 0, "t": 0}, 1: {"v": 0, "t": 0}}
                ld_done = {}

                def fin_bufs(g):
                    s = "" if g % 2 == 0 else "2"
                    return (t["y2f" + s], t["y2h" + s], t["xg" + s], t["maskg"])

                def fin_loads(g):
                    y2fb, y2hb, xgb, mkb = fin_bufs(g)
                    fr = fb_free[g % 2]
                    wa = [("vector", fr["v"]), ("tensor", fr["t"]),
                          ("gpsimd", y2_done[g])]
                    b.dma("sync", ap(y2fb, 0, [(4096, 128), (512, 8), (1, 512)]),
                          ap(y2_s, 512 * g, [(L, 128), (128 * L, 8), (1, 512)]),
                          waits_abs=wa)

                    b.dma("scalar", ap(y2hb, 0, [(4096, 128), (512, 8), (1, 512)]),
                          ap(y2_s1, 512 * g, [(L, 128), (128 * L, 8), (1, 512)]),
                          waits_abs=wa)
                    b.dma("scalar", xgb[:, :2048],
                          ap(x_h, 512 * g * 1024, [(1024, 128), (131072, 2), (1, 1024)]))
                    b.dma("gpsimd", xgb[:, 2048:],
                          ap(x_h, 512 * g * 1024 + 262144, [(1024, 128), (131072, 2), (1, 1024)]),
                          waits_abs=wa)
                    ld_done[g] = {"sync": b.mark("sync"), "gpsimd": b.mark("gpsimd"),
                                  "scalar": b.mark("scalar")}

                fin_loads(0)
                mk_rd = 0
                for g in range(NG):
                    if g + 1 < NG:
                        fin_loads(g + 1)
                    y2fb, y2hb, xgb, mkb = fin_bufs(g)
                    b.dma("sync", ap(mkb, 0, [(4096, 128), (512, 8), (1, 512)]),
                          ap(maskt_h, 512 * g, [(L, 128), (128 * L, 8), (1, 512)]),
                          waits_abs=[("vector", mk_rd)])
                    mk_ld = b.mark("sync")
                    gcol = slice(512 * g, 512 * (g + 1))
                    for dc in range(8):
                        b.op("vector", lambda E, dc=dc, y2fb=y2fb, y2hb=y2hb: E.tensor_tensor(
                            y2fb[:, 512 * dc:512 * (dc + 1)],
                            y2fb[:, 512 * dc:512 * (dc + 1)],
                            y2hb[:, 512 * dc:512 * (dc + 1)], ALU.add),
                            waits_abs=([("sync", ld_done[g]["sync"]),
                                        ("scalar", ld_done[g]["scalar"])] if dc == 0 else []))
                    pre_rm_v = b.mark("vector")
                    dbg_dump("y2fin", y2fb[:, :], (128, 4096), when=(g == 0))
                    pj_free = pre_rm_v
                    pk_free = pre_rm_v
                    for dc in range(8):
                        for vc in range(4):
                            b.op("tensor", lambda E, dc=dc, vc=vc, g=g: E.matmul(
                                t["pJ"][:, :512],
                                t["wob"][:, 1024 * vc + 128 * dc:1024 * vc + 128 * (dc + 1)],
                                ap(t["ys"], L * vc + 512 * g, [(4 * L, 128), (1, 512)]),
                                start=(vc == 0), stop=(vc == 3), skip_group_check=True),
                                waits_abs=([("vector", pj_free)] if vc == 0 else []))
                        for tt_ in range(4):
                            b.op("tensor", lambda E, tt_=tt_, dc=dc, xgb=xgb: E.transpose(
                                t["pK"][:, 128 * tt_:128 * (tt_ + 1)],
                                xgb[:, 1024 * tt_ + 128 * dc:1024 * tt_ + 128 * (dc + 1)],
                                t["sidentf"][:, :]),
                                waits_abs=([("vector", pk_free),
                                            ("scalar", ld_done[g]["scalar"]),
                                            ("gpsimd", ld_done[g]["gpsimd"])] if tt_ == 0 else []))
                        mm_mark = b.mark("tensor")
                        b.op("vector", lambda E, dc=dc, y2fb=y2fb: E.tensor_copy(
                            t["qb16"][:, :512], y2fb[:, 512 * dc:512 * (dc + 1)]))
                        b.op("vector", lambda E, mkb=mkb, dc=dc: E.tensor_tensor(
                            t["outt"][:, :512], t["qb16"][:, :512],
                            mkb[:, 512 * dc:512 * (dc + 1)], ALU.mult),
                            waits_abs=[("tensor", mm_mark), ("sync", mk_ld),
                                       ("gpsimd", b.mark("gpsimd"))])
                        if debug and dc == 0:
                            b.op("vector", lambda E: E.tensor_copy(t["Asq"][:, :512], t["pJ"][:, :512]))
                            dbg_dump("att0", t["Asq"][:, :512], (128, 512), when=True)
                        b.op("vector", lambda E: E.tensor_tensor(
                            t["outt"][:, :512], t["outt"][:, :512], t["pJ"][:, :512], ALU.add))
                        pj_free = b.mark("vector")
                        b.op("vector", lambda E: E.tensor_tensor(
                            t["outt"][:, :512], t["outt"][:, :512], t["pK"][:, :512], ALU.add))
                        pk_free = b.mark("vector")
                        dbg_dump("outt0", t["outt"][:, :512], (128, 512),
                                 when=(g == 0 and dc == 0))
                        if debug and g == 0 and dc == 0:
                            b.op("vector", lambda E, y2fb=y2fb: E.tensor_copy(t["At1"][:, :512], y2fb[:, :512]))
                            dbg_dump("y2r0", t["At1"][:, :512], (128, 512), when=True)
                            b.op("vector", lambda E: E.tensor_copy(t["At1"][:, :512], t["pK"][:, :512]))
                            dbg_dump("xfm0", t["At1"][:, :512], (128, 512), when=True)
                            b.op("vector", lambda E, mkb=mkb: E.tensor_copy(t["At1"][:, :512], mkb[:, :512]))
                            dbg_dump("msk0", t["At1"][:, :512], (128, 512), when=True)
                        b.dma("gpsimd", out_h[128 * dc:128 * (dc + 1), gcol],
                              t["outt"][:, :512], waits=("vector",))
                    mk_rd = b.mark("vector")
                    fb_free[g % 2] = {"v": mk_rd, "t": b.mark("tensor")}
                b.op("gpsimd", lambda E: E.memset(t["zero"][:1, :1], 0.0),
                     waits=("gpsimd",), inc_k=0)

            b.finalize(block)
    return nc


_NC_CACHE = {}


def _bf16():
    import ml_dtypes
    return ml_dtypes.bfloat16


def kernel(L0_attn_x, L0_attn_W_sel, L0_attn_dt_W, L0_attn_dt_b, L0_attn_Wk,
           L0_attn_Wv, L0_attn_Wr, L0_attn_Wg, L0_attn_Wo, L0_attn_base_decay,
           L0_attn_time_first, L0_ffn_W1, L0_ffn_b1, L0_ffn_W2, L0_ffn_b2):
    import jax
    B, L, D = L0_attn_x.shape
    if L not in _NC_CACHE:
        _NC_CACHE[L] = build_nc(L)
    nc = _NC_CACHE[L]
    jax.config.update("jax_threefry_partitionable", True)
    with jax.default_device(jax.devices("cpu")[0]):
        keep = np.asarray(jax.random.bernoulli(
            jax.random.key(42, impl="threefry2x32"), 0.8, (B, L, D)))
    maskt = np.where(keep, np.float32(1.25), np.float32(0.0)).astype(_bf16())
    maskt = np.ascontiguousarray(np.swapaxes(maskt, 1, 2))
    f32 = np.float32
    common = {
        "wsel": np.ascontiguousarray(L0_attn_W_sel[:, :96], dtype=f32),
        "dtw": np.ascontiguousarray(L0_attn_dt_W, dtype=f32),
        "dtb": np.ascontiguousarray(np.asarray(L0_attn_dt_b, dtype=f32).reshape(4, 128).T),
        "wk": np.ascontiguousarray(L0_attn_Wk, dtype=f32),
        "wv": np.ascontiguousarray(L0_attn_Wv, dtype=f32),
        "wr": np.ascontiguousarray(L0_attn_Wr, dtype=f32),
        "wg": np.ascontiguousarray(L0_attn_Wg, dtype=f32),
        "wo": np.ascontiguousarray(L0_attn_Wo, dtype=f32),
        "bd": np.ascontiguousarray(np.asarray(L0_attn_base_decay, dtype=f32).reshape(512).reshape(4, 128).T),
        "tf": np.ascontiguousarray(np.asarray(L0_attn_time_first, dtype=f32).reshape(512).reshape(4, 128).T),
        "w1": np.ascontiguousarray(L0_ffn_W1, dtype=f32),
        "b1": np.ascontiguousarray(np.asarray(L0_ffn_b1, dtype=f32).reshape(32, 128).T),
        "w2": np.ascontiguousarray(L0_ffn_W2, dtype=f32),
        "b2": np.ascontiguousarray(np.asarray(L0_ffn_b2, dtype=f32).reshape(8, 128).T),
        "identf": np.eye(128, dtype=f32),
        "identb": np.eye(128, dtype=f32).astype(_bf16()),
        "utm": np.triu(np.ones((128, 128), dtype=np.uint8)),
    }
    in_maps = []
    for bi in range(B):
        m = dict(common)
        m["x"] = np.ascontiguousarray(L0_attn_x[bi], dtype=f32)
        m["maskt"] = np.ascontiguousarray(maskt[bi])
        in_maps.append(m)
    res = run_bass_kernel_spmd(nc, in_maps, core_ids=list(range(B)))
    outs = [np.ascontiguousarray(np.asarray(r["out"]).T) for r in res.results]
    return np.stack(outs).astype(np.float32)


# version 11
# speedup vs baseline: 2.8629x; 1.0456x over previous
import contextlib
import numpy as np

import concourse.bass as bass
import concourse.mybir as mybir
from concourse.bass_utils import run_bass_kernel_spmd

F32 = mybir.dt.float32
FP8 = mybir.dt.float8e5
BF16 = mybir.dt.bfloat16
I32 = mybir.dt.int32
U8 = mybir.dt.uint8
ALU = mybir.AluOpType
ACT = mybir.ActivationFunctionType
AX = mybir.AxisListType

C2 = 256.0 / (2.0 * np.log(2.0))

FQ_THR = [(16.0, ALU.is_gt), (48.0, ALU.is_ge), (80.0, ALU.is_gt), (112.0, ALU.is_ge),
          (144.0, ALU.is_gt), (176.0, ALU.is_ge), (208.0, ALU.is_gt), (240.0, ALU.is_ge)]
COMP_THR = [(16.0, ALU.is_gt), (48.0, ALU.is_ge), (80.0, ALU.is_gt), (112.0, ALU.is_ge),
            (144.5, ALU.is_gt), (175.5, ALU.is_ge), (208.5, ALU.is_gt), (239.5, ALU.is_ge)]


def ap(tensor, offset, dims):
    return bass.AP(tensor, offset, [list(d) for d in dims])


class Builder:
    ENGINES = ("sync", "scalar", "vector", "tensor", "gpsimd")

    def __init__(self, nc, es):
        self.nc = nc
        self.ops = {e: [] for e in self.ENGINES}
        self.sems = {}
        self.counts = {}
        for e in self.ENGINES:
            self.sems[e] = es.enter_context(nc.semaphore(f"s_{e}"))
            self.counts[e] = 0

    def op(self, eng, fn, waits=(), waits_abs=(), inc_k=1):
        wl = [(w, self.counts[w]) for w in waits if self.counts[w] > 0]
        wl += [(w, c) for (w, c) in waits_abs if c > 0]
        self.ops[eng].append((fn, wl, inc_k))
        self.counts[eng] += inc_k

    def mark(self, eng):
        return self.counts[eng]

    def dma(self, eng, out_ap, in_ap, waits=(), waits_abs=(), **kw):
        self.op(eng, (lambda E, o=out_ap, i=in_ap, kw=kw: E.dma_start(o, i, **kw)),
                waits=waits, waits_abs=waits_abs, inc_k=16)

    def finalize(self, block):
        sems = self.sems

        def mk(ename):
            ops = self.ops[ename]

            def emit(E):
                for fn, wl, inck in ops:
                    for (w, v) in wl:
                        E.wait_ge(sems[w], v)
                    inst = fn(E)
                    if inck:
                        inst.then_inc(sems[ename], inck)
            return emit
        block.sync(mk("sync"))
        block.scalar(mk("scalar"))
        block.vector(mk("vector"))
        block.tensor(mk("tensor"))
        block.gpsimd(mk("gpsimd"))


def emit_quant(b, t, y_ap, out_ap, width, mode, waits_sc=(), waits_v=(), out_waits=(),
               evac=None, out_waits_abs=(), waits_sc_abs=()):
    W = width
    sq, lg = t["Asq"], t["Alg"]
    u, i256, eint, e2int = t["Au"], t["Ai256"], t["Aeint"], t["Ae2int"]
    shS, shI, stepf, invf = t["AshS"], t["AshI"], t["Astepf"], t["Ainvf"]
    q, L = t["Aq"], t["AL"]
    b.op("scalar", lambda E, y=y_ap: E.activation(sq[:, :W], y, ACT.Square),
         waits=tuple(waits_sc) + ("vector",), waits_abs=waits_sc_abs)
    if evac is not None:
        b.op("scalar", lambda E, y=y_ap: E.activation(evac, y, ACT.Copy))
        y_ap = evac
    free_mark = b.mark("scalar")
    b.op("scalar", lambda E: E.activation(lg[:, :W], sq[:, :W], ACT.Ln,
                                          bias=t["tinyb"][:, :], scale=1.0))
    if mode == "rm":
        b.op("vector", lambda E: E.tensor_scalar(u[:, :W], lg[:, :W], C2, 20480.0,
                                                 ALU.mult, ALU.add),
             waits=("scalar",) + tuple(waits_v))
        b.op("vector", lambda E: E.tensor_single_scalar(u[:, :W], u[:, :W], 12288.0, ALU.max))
        b.op("vector", lambda E: E.tensor_copy(i256[:, :W], u[:, :W]))
        b.op("vector", lambda E: E.tensor_single_scalar(eint[:, :W], i256[:, :W], 8,
                                                        ALU.arith_shift_right))
        b.op("vector", lambda E: E.tensor_single_scalar(eint[:, :W], eint[:, :W], -64, ALU.add))
        b.op("vector", lambda E: E.tensor_single_scalar(e2int[:, :W], eint[:, :W], 16, ALU.add))
        b.op("vector", lambda E: E.tensor_single_scalar(shS[:, :W], e2int[:, :W], 1,
                                                        ALU.arith_shift_right))
        b.op("vector", lambda E: E.tensor_tensor(shI[:, :W], e2int[:, :W], shS[:, :W],
                                                 ALU.subtract))
        b.op("vector", lambda E: E.tensor_tensor(shS[:, :W], t["onei"][:, :W], shS[:, :W],
                                                 ALU.logical_shift_left))
        b.op("vector", lambda E: E.tensor_tensor(shI[:, :W], t["onei"][:, :W], shI[:, :W],
                                                 ALU.logical_shift_left))
        b.op("vector", lambda E: E.tensor_copy(stepf[:, :W], shS[:, :W]))
        b.op("vector", lambda E: E.tensor_copy(q[:, :W], shI[:, :W]))
        b.op("vector", lambda E: E.tensor_tensor(stepf[:, :W], stepf[:, :W], q[:, :W],
                                                 ALU.mult))
        b.op("vector", lambda E: E.tensor_tensor(e2int[:, :W], t["thirty1"][:, :W], eint[:, :W],
                                                 ALU.subtract))
        b.op("vector", lambda E: E.tensor_single_scalar(shS[:, :W], e2int[:, :W], 1,
                                                        ALU.arith_shift_right))
        b.op("vector", lambda E: E.tensor_tensor(shI[:, :W], e2int[:, :W], shS[:, :W],
                                                 ALU.subtract))
        b.op("vector", lambda E: E.tensor_tensor(shS[:, :W], t["onei"][:, :W], shS[:, :W],
                                                 ALU.logical_shift_left))
        b.op("vector", lambda E: E.tensor_tensor(shI[:, :W], t["onei"][:, :W], shI[:, :W],
                                                 ALU.logical_shift_left))
        b.op("vector", lambda E: E.tensor_copy(invf[:, :W], shS[:, :W]))
        b.op("vector", lambda E: E.tensor_copy(q[:, :W], shI[:, :W]))
        b.op("vector", lambda E: E.tensor_tensor(invf[:, :W], invf[:, :W], q[:, :W],
                                                 ALU.mult))
    else:
        b.op("vector", lambda E: E.tensor_scalar(u[:, :W], lg[:, :W], C2, 4096.0,
                                                 ALU.mult, ALU.add),
             waits=("scalar",) + tuple(waits_v))
        b.op("vector", lambda E: E.tensor_single_scalar(u[:, :W], u[:, :W], 512.0, ALU.max))
        b.op("vector", lambda E: E.tensor_copy(i256[:, :W], u[:, :W]))
        b.op("vector", lambda E: E.tensor_single_scalar(eint[:, :W], i256[:, :W], 8,
                                                        ALU.arith_shift_right))
        b.op("vector", lambda E: E.tensor_tensor(shS[:, :W], t["onei"][:, :W], eint[:, :W],
                                                 ALU.logical_shift_left))
        b.op("vector", lambda E: E.tensor_tensor(e2int[:, :W], eint[:, :W], t["thirty1"][:, :W],
                                                 ALU.bitwise_xor))
        b.op("vector", lambda E: E.tensor_tensor(shI[:, :W], t["onei"][:, :W], e2int[:, :W],
                                                 ALU.logical_shift_left))
        b.op("vector", lambda E: E.tensor_copy(stepf[:, :W], shS[:, :W]))
        b.op("vector", lambda E: E.tensor_copy(invf[:, :W], shI[:, :W]))
    b.op("vector", lambda E, y=y_ap: E.scalar_tensor_tensor(q[:, :W], invf[:, :W], 2.0 ** -8, y,
                                                    ALU.mult, ALU.mult))
    if mode == "comp":
        (t0, o0) = COMP_THR[0]
        b.op("vector", lambda E: E.tensor_single_scalar(L[:, :W], q[:, :W], t0, o0))
        for (tv, ov) in COMP_THR[1:]:
            b.op("vector", lambda E, tv=tv, ov=ov: E.scalar_tensor_tensor(
                L[:, :W], q[:, :W], tv, L[:, :W], ov, ALU.add))
        b.op("vector", lambda E: E.scalar_tensor_tensor(
            out_ap, L[:, :W], 2.0 ** -18, stepf[:, :W], ALU.mult, ALU.mult),
            waits=out_waits, waits_abs=out_waits_abs)
        return free_mark
    nq, aq, sg = t["Anq"], t["Aaq"], t["Asg"]
    b.op("vector", lambda E: E.tensor_single_scalar(nq[:, :W], q[:, :W], -1.0, ALU.mult))
    b.op("vector", lambda E: E.tensor_tensor(aq[:, :W], q[:, :W], nq[:, :W], ALU.max))
    if mode == "fq":
        (t0, o0) = FQ_THR[0]
        b.op("vector", lambda E: E.tensor_single_scalar(L[:, :W], aq[:, :W], t0, o0))
        for (tv, ov) in FQ_THR[1:]:
            b.op("vector", lambda E, tv=tv, ov=ov: E.scalar_tensor_tensor(
                L[:, :W], aq[:, :W], tv, L[:, :W], ov, ALU.add))
        res, scl = L, 2.0 ** -18
    else:
        r, fl, b1, t1, tie = t["Ar"], t["Afl"], t["Ab1"], t["At1"], t["Atie"]
        b.op("vector", lambda E: E.tensor_single_scalar(r[:, :W], aq[:, :W], -128.0, ALU.add))
        for w in (64.0, 32.0, 16.0, 8.0, 4.0, 2.0):
            b.op("vector", lambda E, w=w: E.tensor_single_scalar(t1[:, :W], r[:, :W], w, ALU.is_ge))
            b.op("vector", lambda E, w=w: E.scalar_tensor_tensor(
                r[:, :W], t1[:, :W], -w, r[:, :W], ALU.mult, ALU.add))
        b.op("vector", lambda E: E.tensor_single_scalar(b1[:, :W], r[:, :W], 1.0, ALU.is_ge))
        b.op("vector", lambda E: E.scalar_tensor_tensor(
            r[:, :W], b1[:, :W], -1.0, r[:, :W], ALU.mult, ALU.add))
        b.op("vector", lambda E: E.tensor_tensor(fl[:, :W], aq[:, :W], r[:, :W], ALU.subtract))
        b.op("vector", lambda E: E.tensor_single_scalar(t1[:, :W], r[:, :W], 0.5, ALU.is_gt))
        b.op("vector", lambda E: E.tensor_tensor(fl[:, :W], fl[:, :W], t1[:, :W], ALU.add))
        b.op("vector", lambda E: E.tensor_single_scalar(tie[:, :W], r[:, :W], 0.5, ALU.is_equal))
        b.op("vector", lambda E: E.tensor_tensor(tie[:, :W], tie[:, :W], b1[:, :W], ALU.mult))
        b.op("vector", lambda E: E.tensor_tensor(fl[:, :W], fl[:, :W], tie[:, :W], ALU.add))
        b.op("vector", lambda E: E.tensor_single_scalar(t1[:, :W], aq[:, :W], 1.0, ALU.is_ge))
        b.op("vector", lambda E: E.tensor_tensor(fl[:, :W], fl[:, :W], t1[:, :W], ALU.mult))
        res, scl = fl, 2.0 ** -39
    b.op("vector", lambda E: E.tensor_scalar(sg[:, :W], q[:, :W], 0.0, 2.0, ALU.is_ge, ALU.mult))
    b.op("vector", lambda E: E.tensor_single_scalar(sg[:, :W], sg[:, :W], -1.0, ALU.add))
    b.op("vector", lambda E, res=res: E.tensor_tensor(res[:, :W], res[:, :W], sg[:, :W], ALU.mult))
    b.op("vector", lambda E, res=res, scl=scl: E.scalar_tensor_tensor(
        out_ap, res[:, :W], scl, stepf[:, :W], ALU.mult, ALU.mult),
        waits=out_waits, waits_abs=out_waits_abs)
    return free_mark


def build_nc(L=4096, debug=False, skip_b=False, skip_a=False, astage=99.0):
    NCH = L // 128
    NG = L // 512
    nc = bass.Bass("TRN2", target_bir_lowering=False, detect_race_conditions=False)

    x_h = nc.dram_tensor("x", [L, 1024], F32, kind="ExternalInput")
    wsel_h = nc.dram_tensor("wsel", [1024, 96], F32, kind="ExternalInput")
    dtw_h = nc.dram_tensor("dtw", [32, 512], F32, kind="ExternalInput")
    dtb_h = nc.dram_tensor("dtb", [128, 4], F32, kind="ExternalInput")
    wk_h = nc.dram_tensor("wk", [1024, 512], F32, kind="ExternalInput")
    wv_h = nc.dram_tensor("wv", [1024, 512], F32, kind="ExternalInput")
    wr_h = nc.dram_tensor("wr", [1024, 512], F32, kind="ExternalInput")
    wg_h = nc.dram_tensor("wg", [1024, 512], F32, kind="ExternalInput")
    wo_h = nc.dram_tensor("wo", [512, 1024], F32, kind="ExternalInput")
    bd_h = nc.dram_tensor("bd", [128, 4], F32, kind="ExternalInput")
    tf_h = nc.dram_tensor("tf", [128, 4], F32, kind="ExternalInput")
    w1_h = nc.dram_tensor("w1", [4096, 1024], F32, kind="ExternalInput")
    b1_h = nc.dram_tensor("b1", [128, 32], F32, kind="ExternalInput")
    w2_h = nc.dram_tensor("w2", [1024, 4096], F32, kind="ExternalInput")
    b2_h = nc.dram_tensor("b2", [128, 8], F32, kind="ExternalInput")
    maskt_h = nc.dram_tensor("maskt", [1024, L], BF16, kind="ExternalInput")
    identf_h = nc.dram_tensor("identf", [128, 128], F32, kind="ExternalInput")
    identb_h = nc.dram_tensor("identb", [128, 128], BF16, kind="ExternalInput")
    utm_h = nc.dram_tensor("utm", [128, 128], U8, kind="ExternalInput")
    out_h = nc.dram_tensor("out", [1024, L], F32, kind="ExternalOutput")
    xnq_s = nc.dram_tensor("xnq_s", [1024, L], BF16)
    y2_s = nc.dram_tensor("y2_s", [1024, L], F32)
    y2_s1 = nc.dram_tensor("y2_s1", [1024, L], F32)

    dbg = {}

    def mkdbg(name, shape, dt=F32):
        if name not in dbg:
            dbg[name] = nc.dram_tensor("dbg_" + name, list(shape), dt, kind="ExternalOutput")
        return dbg[name]

    es = contextlib.ExitStack()
    with es:
        b = Builder(nc, es)
        t = {}

        def dbg_dump(name, src_ap, shape, dt=F32, when=True):
            if debug and when:
                h = mkdbg(name, shape, dt)
                b.dma("gpsimd", h[:, :], src_ap, waits=("vector", "scalar", "tensor"))
                b.op("vector", lambda E: E.memset(t["zero"][:1, :1], 0.0), waits=("gpsimd",))

        def sb(name, shape, dt=F32, stack=es):
            t[name] = stack.enter_context(nc.sbuf_tensor(name, list(shape), dt))
            return t[name]

        def psum(name, shape, dt=F32, stack=es):
            t[name] = stack.enter_context(nc.psum_tensor(name, list(shape), dt))
            return t[name]

        sb("sidentf", [128, 128]); sb("sidentb", [128, 128], BF16)
        sb("sutm", [128, 128], U8)
        sb("zero", [128, 128], F32)
        sb("sdtb", [128, 4]); sb("sbd", [128, 4]); sb("bdneg", [128, 4]); sb("stf", [128, 4])
        sb("b1r", [128, 32]); sb("b2r", [128, 8])
        sb("b1qT", [32, 128], BF16); sb("b2qT", [8, 128], BF16)
        sb("wstage", [128, 1024], F32)
        sb("ys", [128, 4 * L], BF16)
        sb("Asq", [128, 1024])
        for nm in ("Alg", "Ar", "Afl", "At1", "Atie"):
            sb(nm, [128, 512])
        sb("minus1f", [128, 512], F32)
        sb("q8", [128, 512], FP8)
        sb("qb16", [128, 512], BF16)
        sb("eps5", [128, 1], F32)
        sb("one1", [128, 1], F32)


        with nc.Block() as block:
            esA = contextlib.ExitStack()
            with esA:
                sb("wkb", [128, 4096], BF16, esA); sb("wvb", [128, 4096], BF16, esA)
                sb("wrb", [128, 4096], BF16, esA); sb("wgb", [128, 4096], BF16, esA)
                sb("wselb", [128, 768], BF16, esA); sb("dtwb", [32, 512], BF16, esA)
                sb("xbuf0", [128, 1024], F32, esA); sb("xbuf1", [128, 1024], F32, esA)
                sb("xc", [128, 1024], F32, esA)
                sb("red", [128, 1], F32, esA); sb("mu", [128, 1], F32, esA)
                sb("var", [128, 1], F32, esA); sb("rstd", [128, 1], F32, esA)
                sb("musq", [128, 1], F32, esA)
                sb("dtev", [128, 32], F32, esA)
                sb("xn0", [128, 1024], F32, esA); sb("xn1", [128, 1024], F32, esA)
                sb("xnfmb0", [128, 1024], BF16, esA); sb("xnfmb1", [128, 1024], BF16, esA)
                sb("xnqb0", [128, 1024], FP8, esA); sb("xnqb1", [128, 1024], FP8, esA)
                sb("kev", [128, 512], F32, esA)
                sb("Pfull", [128, 512], F32, esA)
                sb("ell", [128, 512], F32, esA)
                sb("ez", [128, 512], F32, esA)
                sb("delta", [128, 512], F32, esA)
                sb("Pmid", [128, 32], F32, esA); sb("Pb64", [128, 4], F32, esA)
                sb("Pb32", [128, 8], F32, esA); sb("Pb16", [128, 16], F32, esA)
                sb("e1arg", [128, 512], F32, esA)
                sb("s64", [128, 512], F32, esA); sb("s32", [128, 512], F32, esA)
                sb("s16", [128, 512], F32, esA); sb("farg", [128, 512], F32, esA)
                sb("expE", [128, 512], F32, esA); sb("expF", [128, 512], F32, esA)
                sb("expE1", [128, 512], F32, esA); sb("expE2", [128, 512], F32, esA)
                sb("xq64", [128, 256], F32, esA); sb("xk64", [128, 256], F32, esA)
                sb("xq32", [128, 256], F32, esA); sb("xk32", [128, 256], F32, esA)
                sb("xq16", [128, 256], F32, esA); sb("xk16", [128, 256], F32, esA)
                sb("qtb", [128, 512], BF16, esA); sb("khf", [128, 512], F32, esA)
                sb("qtbB", [64, 512], BF16, esA)
                sb("ke1B", [64, 512], BF16, esA); sb("ke2B", [64, 512], BF16, esA)
                sb("kq64B", [64, 256], BF16, esA); sb("kk64B", [64, 256], BF16, esA)
                sb("kq32B", [64, 256], BF16, esA); sb("kk32B", [64, 256], BF16, esA)
                sb("kq16B", [64, 256], BF16, esA); sb("kk16B", [64, 256], BF16, esA)
                sb("SbB", [64, 256], BF16, esA)
                sb("khtm", [128, 512], BF16, esA)
                sb("ke1", [128, 512], BF16, esA); sb("ke2", [128, 512], BF16, esA)
                sb("kq64b", [128, 256], BF16, esA); sb("kk64b", [128, 256], BF16, esA)
                sb("kq32b", [128, 256], BF16, esA); sb("kk32b", [128, 256], BF16, esA)
                sb("kq16b", [128, 256], BF16, esA); sb("kk16b", [128, 256], BF16, esA)
                sb("vpb", [128, 512], BF16, esA); sb("vrawf", [128, 512], F32, esA)
                sb("vfm", [128, 512], F32, esA)
                sb("btok", [128, 1], F32, esA)
                sb("sigr", [128, 512], F32, esA); sb("sigg", [128, 512], F32, esA)
                sb("rg", [128, 512], F32, esA)
                sb("Amb", [128, 1024], BF16, esA)
                sb("kv", [128, 512], F32, esA); sb("bonus", [128, 512], F32, esA)
                sb("dtrawb", [32, 128], BF16, esA)
                sb("Stile", [128, 256], F32, esA)
                sb("Sb", [128, 256], BF16, esA)
                sb("yst", [128, 512], F32, esA)
                psum("pA", [128, 512], F32, esA)
                psum("pB", [128, 512], F32, esA)
                psum("pD", [128, 512], F32, esA)
                psum("pE", [128, 512], F32, esA)
                psum("pF", [128, 512], F32, esA)
                psum("pG", [128, 512], F32, esA)
                psum("pH", [128, 512], F32, esA)
                psum("pL", [128, 512], F32, esA)
                

                b.dma("sync", t["sidentf"][:, :], identf_h[:, :])
                b.dma("sync", t["sidentb"][:, :], identb_h[:, :])
                b.dma("sync", t["sutm"][:, :], utm_h[:, :])
                b.dma("sync", t["sdtb"][:, :], dtb_h[:, :])
                b.dma("sync", t["sbd"][:, :], bd_h[:, :])
                b.dma("sync", t["stf"][:, :], tf_h[:, :])
                b.dma("sync", t["b1r"][:, :], b1_h[:, :])
                b.dma("sync", t["b2r"][:, :], b2_h[:, :])
                b.dma("sync", t["wstage"][:32, :512], dtw_h[:, :])
                b.op("vector", lambda E: E.memset(t["zero"][:, :], 0.0), waits=("sync",))
                b.op("vector", lambda E: E.memset(t["minus1f"][:, :], -1.0))
                b.op("vector", lambda E: E.memset(t["eps5"][:, :], 1e-5))
                b.op("vector", lambda E: E.memset(t["one1"][:, :], 1.0))
                b.op("vector", lambda E: E.memset(t["Stile"][:, :], 0.0))
                b.op("vector", lambda E: E.tensor_single_scalar(t["bdneg"][:, :], t["sbd"][:, :], -1.0, ALU.mult))
                b.dma("gpsimd", t["dtwb"][:, :], dtw_h[:, :])
                for (wh, wb) in ((wk_h, "wkb"), (wv_h, "wvb"), (wr_h, "wrb"), (wg_h, "wgb")):
                    for fc in range(8):
                        b.dma("gpsimd", t[wb][:, 512 * fc:512 * (fc + 1)],
                              wh[128 * fc:128 * (fc + 1), :])
                for fc in range(8):
                    b.dma("gpsimd", t["wselb"][:, 96 * fc:96 * (fc + 1)],
                          wsel_h[128 * fc:128 * (fc + 1), :])
                wld_mark = b.mark("sync")
                wld_gp_mark = b.mark("gpsimd")
                b.op("vector", lambda E: E.tensor_copy(
                    t["At1"][:, :512], ap(t["b1r"], 0, [(32, 128), (0, 16), (1, 32)])),
                    waits=("sync",))
                b.op("vector", lambda E: E.tensor_copy(
                    t["Atie"][:, :512], ap(t["b2r"], 0, [(8, 128), (0, 64), (1, 8)])))
                b.op("vector", lambda E: E.tensor_copy(t["q8"][:, :512], t["At1"][:, :512]))
                b.op("vector", lambda E: E.tensor_copy(t["Ar"][:, :512], t["q8"][:, :512]))
                b.op("vector", lambda E: E.tensor_copy(t["q8"][:, :512], t["Atie"][:, :512]))
                b.op("vector", lambda E: E.tensor_copy(t["Afl"][:, :512], t["q8"][:, :512]))
                b.op("tensor", lambda E: E.transpose(t["pF"][:32, :128], t["Ar"][:, :32],
                                                     t["sidentf"][:, :]), waits=("vector",))
                b.op("vector", lambda E: E.tensor_copy(t["b1qT"][:, :], t["pF"][:32, :128]),
                     waits=("tensor",))
                b.op("tensor", lambda E: E.transpose(t["pF"][:8, :128], t["Afl"][:, :8],
                                                     t["sidentf"][:, :]), waits=("vector",))
                b.op("vector", lambda E: E.tensor_copy(t["b2qT"][:, :], t["pF"][:8, :128]),
                     waits=("tensor",))

                xbuf_free = {0: 0, 1: 0}
                fm_free = {0: 0, 1: 0}
                qb_free = {0: 0, 1: 0}
                ln_done = {}

                def emit_ln(n):
                    bs = n % 2
                    tok = slice(128 * n, 128 * (n + 1))
                    xb, xnn = t[f"xbuf{bs}"], t[f"xn{bs}"]
                    fm, qb = t[f"xnfmb{bs}"], t[f"xnqb{bs}"]
                    b.dma("sync", xb[:, :], x_h[tok, :],
                          waits_abs=[("vector", xbuf_free[bs])])
                    b.op("vector", lambda E: E.reduce_sum(t["red"][:, :], xb[:, :], AX.X),
                         waits=("sync",))
                    b.op("vector", lambda E: E.tensor_tensor(t["Asq"][:, :1024], xb[:, :],
                                                             xb[:, :], ALU.mult))
                    b.op("vector", lambda E: E.tensor_single_scalar(t["mu"][:, :], t["red"][:, :],
                                                                    1.0 / 1024.0, ALU.mult))
                    b.op("vector", lambda E: E.reduce_sum(t["red"][:, :], t["Asq"][:, :1024], AX.X))
                    b.op("vector", lambda E: E.tensor_tensor(t["xc"][:, :], xb[:, :],
                         ap(t["mu"], 0, [(1, 128), (0, 1024)]), ALU.subtract))
                    xbuf_free[bs] = b.mark("vector")
                    b.op("scalar", lambda E: E.activation(t["musq"][:, :], t["mu"][:, :],
                                                          ACT.Square), waits=("vector",))
                    b.op("vector", lambda E: E.scalar_tensor_tensor(t["var"][:, :], t["red"][:, :],
                         1.0 / 1024.0, t["musq"][:, :], ALU.mult, ALU.subtract),
                         waits=("scalar",))
                    b.op("scalar", lambda E: E.activation(t["var"][:, :], t["var"][:, :], ACT.Ln,
                                                          bias=t["eps5"][:, :], scale=1.0),
                         waits=("vector",))
                    b.op("scalar", lambda E: E.activation(t["Alg"][:, :512], t["Asq"][:, :512],
                                                          ACT.Copy))
                    b.op("scalar", lambda E: E.activation(t["rstd"][:, :], t["var"][:, :], ACT.Exp,
                                                          scale=-0.5))
                    b.op("vector", lambda E: E.scalar_tensor_tensor(xnn[:, :], t["xc"][:, :],
                         t["rstd"][:, :], t["xc"][:, :], ALU.mult, ALU.bypass),
                         waits=("scalar",))
                    for rnd in range(2):
                        for j in range(4):
                            fc = 4 * rnd + j
                            b.op("tensor", lambda E, fc=fc, j=j: E.transpose(
                                t["pL"][:, 128 * j:128 * (j + 1)],
                                xnn[:, 128 * fc:128 * (fc + 1)], t["sidentf"][:, :]),
                                waits=("vector", "scalar") if j == 0 else (),
                                waits_abs=[("tensor", fm_free[bs])] if rnd == 0 and j == 0 else [])
                        b.op("scalar", lambda E, rnd=rnd: E.activation(
                            fm[:, 512 * rnd:512 * (rnd + 1)], t["pL"][:, :], ACT.Copy),
                            waits=("tensor",))
                        b.op("scalar", lambda E: E.activation(t["q8"][:, :512], t["pL"][:, :],
                                                              ACT.Copy))
                        b.op("vector", lambda E, rnd=rnd: E.tensor_copy(
                            qb[:, 512 * rnd:512 * (rnd + 1)], t["q8"][:, :512]),
                            waits=("scalar",),
                            waits_abs=[("gpsimd", qb_free[bs])] if rnd == 0 else [])
                    dbg_dump("xn", xnn[:, :], (128, 1024), when=(n == 0))
                    dbg_dump("xnfmb", fm[:, :], (128, 1024), BF16, when=(n == 0))
                    for fc in range(8):
                        b.dma("gpsimd", xnq_s[128 * fc:128 * (fc + 1), tok],
                              qb[:, 128 * fc:128 * (fc + 1)], waits=("vector",))
                    qb_free[bs] = b.mark("gpsimd")
                    ln_done[n] = {"v": b.mark("vector"), "s": b.mark("scalar")}

                if not skip_a:
                    emit_ln(0)
                for n in range(NCH if not skip_a else 0):
                    bs = n % 2
                    tok = slice(128 * n, 128 * (n + 1))
                    if n + 1 < NCH:
                        emit_ln(n + 1)
                    fmb = t[f"xnfmb{bs}"]
                    if astage < 2:
                        continue
                    for fc in range(8):
                        b.op("tensor", lambda E, fc=fc, fmb=fmb: E.matmul(
                            t["pA"][:, :512], fmb[:, 128 * fc:128 * (fc + 1)],
                            t["wvb"][:, 512 * fc:512 * (fc + 1)],
                            start=(fc == 0), stop=(fc == 7), skip_group_check=True),
                            waits_abs=([("vector", ln_done[n]["v"]),
                                        ("scalar", ln_done[n]["s"]),
                                        ("sync", wld_mark),
                                        ("gpsimd", wld_gp_mark)] if fc == 0 else []))
                    for fc in range(8):
                        b.op("tensor", lambda E, fc=fc, fmb=fmb: E.matmul(
                            t["pB"][:, :96], fmb[:, 128 * fc:128 * (fc + 1)],
                            t["wselb"][:, 96 * fc:96 * (fc + 1)],
                            start=(fc == 0), stop=(fc == 7), skip_group_check=True))
                    for (wb, bank) in (("wkb", "pD"), ("wrb", "pE")):
                        for dc in range(4):
                            for fc in range(8):
                                b.op("tensor", lambda E, wb=wb, bank=bank, dc=dc, fc=fc, fmb=fmb: E.matmul(
                                    t[bank][:, 128 * dc:128 * (dc + 1)],
                                    t[wb][:, 512 * fc + 128 * dc:512 * fc + 128 * (dc + 1)],
                                    fmb[:, 128 * fc:128 * (fc + 1)],
                                    start=(fc == 0), stop=(fc == 7), skip_group_check=True))
                    b.op("vector", lambda E: E.reduce_sum(t["red"][:, :], t["pB"][:, 32:96], AX.X),
                         waits=("tensor",))
                    b.op("vector", lambda E: E.tensor_copy(t["vrawf"][:, :], t["pA"][:, :512]))
                    b.op("vector", lambda E: E.tensor_single_scalar(t["btok"][:, :], t["red"][:, :],
                                                                    1.0 / 64.0, ALU.mult))
                    b.op("vector", lambda E: E.tensor_copy(t["kev"][:, :], t["pD"][:, :512]))
                    b.op("vector", lambda E: E.scalar_tensor_tensor(t["vpb"][:, :], t["pA"][:, :512],
                         t["btok"][:, :], t["minus1f"][:, :], ALU.mult, ALU.bypass))
                    b.op("vector", lambda E: E.tensor_copy(t["dtev"][:, :32], t["pB"][:, 0:32]))
                    b.op("tensor", lambda E: E.transpose(t["pF"][:32, :128], t["dtev"][:, :32],
                                                         t["sidentf"][:, :]),
                         waits=("vector", "scalar"))
                    b.op("vector", lambda E: E.tensor_copy(t["dtrawb"][:, :], t["pF"][:32, :128]),
                         waits=("tensor",))
                    if astage < 3:
                        continue
                    b.op("scalar", lambda E: E.activation(t["sigr"][:, :], t["pE"][:, :512],
                                                          ACT.Exp, scale=-1.0), waits=("vector",))
                    b.op("scalar", lambda E: E.activation(t["sigr"][:, :], t["sigr"][:, :], ACT.Ln,
                                                          bias=t["one1"][:, :], scale=1.0))
                    b.op("scalar", lambda E: E.activation(t["sigr"][:, :], t["sigr"][:, :], ACT.Exp,
                                                          scale=-1.0))
                    for dc in range(4):
                        for fc in range(8):
                            b.op("tensor", lambda E, dc=dc, fc=fc, fmb=fmb: E.matmul(
                                t["pE"][:, 128 * dc:128 * (dc + 1)],
                                t["wgb"][:, 512 * fc + 128 * dc:512 * fc + 128 * (dc + 1)],
                                fmb[:, 128 * fc:128 * (fc + 1)],
                                start=(fc == 0), stop=(fc == 7), skip_group_check=True),
                                waits=("scalar",) if (dc == 0 and fc == 0) else ())
                    fm_free[bs] = b.mark("tensor")
                    b.op("scalar", lambda E: E.activation(t["sigg"][:, :], t["pE"][:, :512],
                                                          ACT.Exp, scale=-1.0), waits=("tensor",))
                    b.op("scalar", lambda E: E.activation(t["sigg"][:, :], t["sigg"][:, :], ACT.Ln,
                                                          bias=t["one1"][:, :], scale=1.0))
                    b.op("scalar", lambda E: E.activation(t["sigg"][:, :], t["sigg"][:, :], ACT.Exp,
                                                          scale=-1.0))
                    b.op("vector", lambda E: E.tensor_tensor(t["rg"][:, :], t["sigr"][:, :],
                                                             t["sigg"][:, :], ALU.mult),
                         waits=("scalar",))
                    for dc in range(4):
                        b.op("tensor", lambda E, dc=dc: E.matmul(
                            t["pB"][:, 128 * dc:128 * (dc + 1)],
                            t["dtwb"][:, 128 * dc:128 * (dc + 1)], t["dtrawb"][:, :],
                            start=True, stop=True, skip_group_check=True), waits=("vector",) if dc == 0 else ())
                    for c4 in range(4):
                        b.op("scalar", lambda E, c4=c4: E.activation(
                            t["ez"][:, 128 * c4:128 * (c4 + 1)], t["pB"][:, 128 * c4:128 * (c4 + 1)],
                            ACT.Exp, bias=t["sdtb"][:, c4:c4 + 1], scale=1.0),
                            waits=("tensor", "vector") if c4 == 0 else ("tensor",))
                    b.op("scalar", lambda E: E.activation(t["delta"][:, :], t["ez"][:, :], ACT.Ln,
                                                          bias=t["one1"][:, :], scale=1.0))
                    for c4 in range(4):
                        b.op("vector", lambda E, c4=c4: E.scalar_tensor_tensor(
                            t["ell"][:, 128 * c4:128 * (c4 + 1)],
                            t["delta"][:, 128 * c4:128 * (c4 + 1)],
                            t["bdneg"][:, c4:c4 + 1],
                            t["delta"][:, 128 * c4:128 * (c4 + 1)], ALU.mult, ALU.bypass),
                            waits=("scalar",) if c4 == 0 else ())
                    for c4 in range(4):
                        b.op("vector", lambda E, c4=c4: E.tensor_tensor_scan(
                            t["Pfull"][:, 128 * c4:128 * (c4 + 1)],
                            t["ell"][:, 128 * c4:128 * (c4 + 1)],
                            t["ell"][:, 128 * c4:128 * (c4 + 1)], 0.0, ALU.add, ALU.bypass))
                    for c4 in range(3):
                        b.op("vector", lambda E, c4=c4: E.scalar_tensor_tensor(
                            t["farg"][:, 128 * c4:128 * (c4 + 1)],
                            t["Pfull"][:, 128 * c4:128 * (c4 + 1)],
                            t["Pfull"][:, 128 * c4 + 127:128 * c4 + 128],
                            t["minus1f"][:, 128 * c4:128 * (c4 + 1)], ALU.subtract, ALU.mult))
                    b.op("vector", lambda E: E.tensor_copy(t["Pmid"][:, :],
                         ap(t["Pfull"], 8, [(512, 128), (16, 32)])))
                    b.op("vector", lambda E: E.tensor_copy(t["Pb64"][:, :],
                         ap(t["Pfull"], 63, [(512, 128), (128, 4)])))
                    b.op("vector", lambda E: E.tensor_copy(t["Pb32"][:, :],
                         ap(t["Pfull"], 31, [(512, 128), (64, 8)])))
                    b.op("vector", lambda E: E.tensor_copy(t["Pb16"][:, :],
                         ap(t["Pfull"], 15, [(512, 128), (32, 16)])))
                    b.op("vector", lambda E: E.tensor_copy(t["Asq"][:, :512],
                                                           t["Pfull"][:, :]))
                    b.op("vector", lambda E: E.scalar_tensor_tensor(
                        t["farg"][:, 384:512], t["Pfull"][:, 384:512],
                        t["Pfull"][:, 511:512],
                        t["minus1f"][:, 384:512], ALU.subtract, ALU.mult))
                    b.op("vector", lambda E: E.tensor_tensor(t["e1arg"][:, :], t["Pfull"][:, :],
                         ap(t["Pmid"], 0, [(32, 128), (1, 32), (0, 16)]), ALU.subtract))
                    b.op("vector", lambda E: E.tensor_tensor(t["s64"][:, :], t["Pfull"][:, :],
                         ap(t["Pb64"], 0, [(4, 128), (1, 4), (0, 128)]), ALU.subtract))
                    b.op("vector", lambda E: E.tensor_tensor(t["s32"][:, :], t["Pfull"][:, :],
                         ap(t["Pb32"], 0, [(8, 128), (1, 8), (0, 64)]), ALU.subtract))
                    b.op("vector", lambda E: E.tensor_tensor(t["s16"][:, :], t["Pfull"][:, :],
                         ap(t["Pb16"], 0, [(16, 128), (1, 16), (0, 32)]), ALU.subtract))
                    if astage < 4:
                        continue
                    b.op("scalar", lambda E: E.activation(t["expE"][:, :], t["Pfull"][:, :], ACT.Exp),
                         waits=("vector",))
                    b.op("scalar", lambda E: E.activation(t["expF"][:, :], t["farg"][:, :], ACT.Exp))
                    b.op("scalar", lambda E: E.activation(t["expE1"][:, :], t["e1arg"][:, :], ACT.Exp))
                    b.op("scalar", lambda E: E.activation(t["expE2"][:, :], t["e1arg"][:, :], ACT.Exp,
                                                          scale=-1.0))
                    b.op("scalar", lambda E: E.activation(t["xq64"][:, :],
                         ap(t["s64"], 64, [(512, 128), (128, 4), (1, 64)]), ACT.Exp))
                    b.op("scalar", lambda E: E.activation(t["xk64"][:, :],
                         ap(t["s64"], 0, [(512, 128), (128, 4), (1, 64)]), ACT.Exp, scale=-1.0))
                    b.op("scalar", lambda E: E.activation(t["xq32"][:, :],
                         ap(t["s32"], 32, [(512, 128), (64, 8), (1, 32)]), ACT.Exp))
                    b.op("scalar", lambda E: E.activation(t["xk32"][:, :],
                         ap(t["s32"], 0, [(512, 128), (64, 8), (1, 32)]), ACT.Exp, scale=-1.0))
                    b.op("scalar", lambda E: E.activation(t["xq16"][:, :],
                         ap(t["s16"], 16, [(512, 128), (32, 16), (1, 16)]), ACT.Exp))
                    b.op("scalar", lambda E: E.activation(t["xk16"][:, :],
                         ap(t["s16"], 0, [(512, 128), (32, 16), (1, 16)]), ACT.Exp, scale=-1.0))
                    b.op("vector", lambda E: E.tensor_tensor(t["qtb"][:, :], t["kev"][:, :],
                         t["expE"][:, :], ALU.mult), waits=("scalar",))
                    b.op("vector", lambda E: E.tensor_tensor(t["khf"][:, :], t["kev"][:, :],
                         t["expF"][:, :], ALU.mult))
                    b.op("vector", lambda E: E.tensor_tensor(t["ke1"][:, :], t["kev"][:, :],
                         t["expE1"][:, :], ALU.mult))
                    b.op("vector", lambda E: E.tensor_tensor(t["ke2"][:, :], t["kev"][:, :],
                         t["expE2"][:, :], ALU.mult))
                    b.op("vector", lambda E: E.tensor_tensor(t["kq64b"][:, :],
                         ap(t["kev"], 64, [(512, 128), (128, 4), (1, 64)]), t["xq64"][:, :], ALU.mult))
                    b.op("vector", lambda E: E.tensor_tensor(t["kk64b"][:, :],
                         ap(t["kev"], 0, [(512, 128), (128, 4), (1, 64)]), t["xk64"][:, :], ALU.mult))
                    b.op("vector", lambda E: E.tensor_tensor(t["kq32b"][:, :],
                         ap(t["kev"], 32, [(512, 128), (64, 8), (1, 32)]), t["xq32"][:, :], ALU.mult))
                    b.op("vector", lambda E: E.tensor_tensor(t["kk32b"][:, :],
                         ap(t["kev"], 0, [(512, 128), (64, 8), (1, 32)]), t["xk32"][:, :], ALU.mult))
                    b.op("vector", lambda E: E.tensor_tensor(t["kq16b"][:, :],
                         ap(t["kev"], 16, [(512, 128), (32, 16), (1, 16)]), t["xq16"][:, :], ALU.mult))
                    b.op("vector", lambda E: E.tensor_tensor(t["kk16b"][:, :],
                         ap(t["kev"], 0, [(512, 128), (32, 16), (1, 16)]), t["xk16"][:, :], ALU.mult))
                    dbg_dump("kev", t["kev"][:, :], (128, 512), when=(n == 0))
                    dbg_dump("vpb", t["vpb"][:, :], (128, 512), BF16, when=(n == 0))
                    dbg_dump("Pfull", t["Pfull"][:, :], (128, 512), when=(n == 0))
                    dbg_dump("expE", t["expE"][:, :], (128, 512), when=(n == 0))
                    dbg_dump("qtb", t["qtb"][:, :], (128, 512), BF16, when=(n == 0))
                    dbg_dump("rg", t["rg"][:, :], (128, 512), when=(n == 0))
                    dbg_dump("btok", t["btok"][:, :], (128, 1), when=(n == 0))
                    if astage < 5:
                        continue
                    b.op("vector", lambda E: E.tensor_copy(t["Sb"][:, :], t["Stile"][:, :]))
                    for (srcn, dstn) in (("qtb", "qtbB"), ("ke1", "ke1B"), ("ke2", "ke2B"),
                                         ("kq64b", "kq64B"), ("kk64b", "kk64B"),
                                         ("kq32b", "kq32B"), ("kk32b", "kk32B"),
                                         ("kq16b", "kq16B"), ("kk16b", "kk16B"),
                                         ("Sb", "SbB")):
                        b.dma("gpsimd", t[dstn][:, :], t[srcn][64:128, :], waits=("vector",))
                    if astage < 5.2:
                        continue
                    for p in range(4):
                        b.op("tensor", lambda E, p=p: E.transpose(
                            t["pF"][:, 128 * p:128 * (p + 1)],
                            t["khf"][:, 128 * p:128 * (p + 1)], t["sidentf"][:, :]),
                            waits=("vector", "scalar") if p == 0 else ())
                    if astage < 5.3:
                        continue
                    b.op("vector", lambda E: E.tensor_copy(t["khtm"][:, :], t["pF"][:, :512]),
                         waits=("tensor",))
                    if astage < 5.4:
                        continue
                    for p in range(4):
                        b.op("tensor", lambda E, p=p: E.transpose(
                            t["pF"][:, 128 * p:128 * (p + 1)],
                            t["vrawf"][:, 128 * p:128 * (p + 1)], t["sidentf"][:, :]),
                            waits=("vector",) if p == 0 else ())
                    if astage < 5.5:
                        continue
                    b.op("vector", lambda E: E.tensor_copy(t["vfm"][:, :], t["pF"][:, :512]),
                         waits=("tensor",))
                    if astage < 5.6:
                        continue
                    for rnd in range(2):
                        import os as _os3
                        for j in range(4):
                            h = 4 * rnd + j
                            p, hh = h // 2, h % 2
                            if _os3.environ.get("NOTP"):
                                tp = None
                            else:
                                tp = (64 * hh, 0)
                            if _os3.environ.get("BASE0"):
                                hh = 0
                                tp = None if _os3.environ.get("NOTP") else (0, 0)
                            ke2s = (t["ke2"][0:64, 128 * p:128 * (p + 1)] if hh == 0
                                    else t["ke2B"][:, 128 * p:128 * (p + 1)])
                            ke1s = (t["ke1"][0:64, 128 * p:128 * (p + 1)] if hh == 0
                                    else t["ke1B"][:, 128 * p:128 * (p + 1)])
                            b.op("tensor", lambda E, j=j, ke2s=ke2s, ke1s=ke1s: E.matmul(
                                t["pG"][:, 128 * j:128 * (j + 1)], ke2s, ke1s,
                                start=(j == 0), stop=False, skip_group_check=True),
                                waits=("vector", "gpsimd") if j == 0 else ())
                        for j in range(4 if astage >= 5.7 else 0):
                            h = 4 * rnd + j
                            p, hh = h // 2, h % 2
                            kkس = (t["kk64b"][0:64, 64 * p:64 * (p + 1)] if hh == 0
                                   else t["kk64B"][:, 64 * p:64 * (p + 1)])
                            kqس = (t["kq64b"][0:64, 64 * p:64 * (p + 1)] if hh == 0
                                   else t["kq64B"][:, 64 * p:64 * (p + 1)])
                            b.op("tensor", lambda E, j=j, a=kkس, c=kqس: E.matmul(
                                t["pG"][0:64, 128 * j + 64:128 * (j + 1)], a, c,
                                start=True, stop=False, skip_group_check=True))
                            for q2 in range(2 if astage >= 5.8 else 0):
                                a32 = (t["kk32b"][0:64, 32 * (2 * p + q2):32 * (2 * p + q2) + 32] if hh == 0
                                       else t["kk32B"][:, 32 * (2 * p + q2):32 * (2 * p + q2) + 32])
                                c32_ = (t["kq32b"][0:64, 32 * (2 * p + q2):32 * (2 * p + q2) + 32] if hh == 0
                                        else t["kq32B"][:, 32 * (2 * p + q2):32 * (2 * p + q2) + 32])
                                b.op("tensor", lambda E, j=j, q2=q2, a=a32, c=c32_: E.matmul(
                                    t["pG"][64 * q2:64 * q2 + 32,
                                            128 * j + 64 * q2 + 32:128 * j + 64 * q2 + 64],
                                    a, c, start=True, stop=False, skip_group_check=True,
                                    tile_position=(0, 64 * q2)))
                            for q in range(4 if astage >= 5.9 else 0):
                                a16 = (t["kk16b"][0:64, 16 * (4 * p + q):16 * (4 * p + q) + 16] if hh == 0
                                       else t["kk16B"][:, 16 * (4 * p + q):16 * (4 * p + q) + 16])
                                c16_ = (t["kq16b"][0:64, 16 * (4 * p + q):16 * (4 * p + q) + 16] if hh == 0
                                        else t["kq16B"][:, 16 * (4 * p + q):16 * (4 * p + q) + 16])
                                b.op("tensor", lambda E, j=j, q=q, a=a16, c=c16_: E.matmul(
                                    t["pG"][32 * q:32 * q + 16,
                                            128 * j + 32 * q + 16:128 * j + 32 * q + 32],
                                    a, c, start=True, stop=(j == 3 and q == 3), skip_group_check=True,
                                    tile_position=(0, 32 * q)))
                        for j in range(4):
                            b.op("vector", lambda E, j=j, rnd=rnd: E.select(
                                t["Amb"][:, 512 * rnd + 128 * j:512 * rnd + 128 * (j + 1)],
                                t["sutm"][:, :],
                                t["pG"][:, 128 * j:128 * (j + 1)], t["zero"][:, :]),
                                waits=("tensor",) if j == 0 else ())
                    if astage < 6:
                        continue
                    for h in range(8):
                        p, hh = h // 2, h % 2
                        Sbs = (t["Sb"][0:64, 64 * p:64 * (p + 1)] if hh == 0
                               else t["SbB"][:, 64 * p:64 * (p + 1)])
                        qts = (t["qtb"][0:64, 128 * p:128 * (p + 1)] if hh == 0
                               else t["qtbB"][:, 128 * p:128 * (p + 1)])
                        b.op("tensor", lambda E, p=p, hh=hh, a=Sbs, c=qts: E.matmul(
                            t["pH"][64 * hh:64 * (hh + 1), 128 * p:128 * (p + 1)],
                            a, c, start=True, stop=False, skip_group_check=True,
                            tile_position=(0, 64 * hh)),
                            waits=("vector", "gpsimd") if h == 0 else ())
                        b.op("tensor", lambda E, p=p, hh=hh, h=h: E.matmul(
                            t["pH"][64 * hh:64 * (hh + 1), 128 * p:128 * (p + 1)],
                            t["vpb"][:, 64 * h:64 * (h + 1)],
                            t["Amb"][:, 128 * h:128 * (h + 1)],
                            start=False, stop=True, skip_group_check=True, tile_position=(0, 64 * hh)))
                    b.op("vector", lambda E: E.tensor_tensor(t["kv"][:, :], t["kev"][:, :],
                         t["vfm"][:, :], ALU.mult))
                    b.op("vector", lambda E: E.tensor_tensor(t["bonus"][:, :], t["kv"][:, :],
                         ap(t["stf"], 0, [(4, 128), (1, 4), (0, 128)]), ALU.mult))
                    b.op("vector", lambda E: E.tensor_tensor(t["yst"][:, :], t["pH"][:, :512],
                         t["bonus"][:, :], ALU.add), waits=("tensor",))
                    b.op("vector", lambda E, n=n: E.tensor_tensor(
                        ap(t["ys"], 128 * n, [(4 * L, 128), (L, 4), (1, 128)]),
                        t["yst"][:, :], t["rg"][:, :], ALU.mult))
                    dbg_dump("Amb", t["Amb"][:, :], (128, 1024), BF16, when=(n == 0))
                    dbg_dump("ke1", t["ke1"][:, :], (128, 512), BF16, when=(n == 0))
                    dbg_dump("ke2", t["ke2"][:, :], (128, 512), BF16, when=(n == 0))
                    dbg_dump("kq64b", t["kq64b"][:, :], (128, 256), BF16, when=(n == 0))
                    dbg_dump("kk64b", t["kk64b"][:, :], (128, 256), BF16, when=(n == 0))
                    dbg_dump("yst", t["yst"][:, :], (128, 512), when=(n == 0))
                    if astage < 7:
                        continue
                    for h in range(8):
                        p, hh = h // 2, h % 2
                        b.op("tensor", lambda E, p=p, hh=hh, h=h: E.matmul(
                            t["pG"][64 * hh:64 * (hh + 1), 64 * p:64 * (p + 1)],
                            t["khtm"][:, 128 * p + 64 * hh:128 * p + 64 * (hh + 1)],
                            t["vpb"][:, 64 * h:64 * (h + 1)],
                            start=True, stop=True, skip_group_check=True, tile_position=(0, 64 * hh)),
                            waits=("vector",) if h == 0 else ())
                    for p in range(4):
                        b.op("vector", lambda E, p=p: E.scalar_tensor_tensor(
                            t["Stile"][:, 64 * p:64 * (p + 1)],
                            t["Stile"][:, 64 * p:64 * (p + 1)],
                            t["expE"][:, 128 * p + 127:128 * p + 128],
                            t["pG"][:, 64 * p:64 * (p + 1)], ALU.mult, ALU.add),
                            waits=("tensor",) if p == 0 else ())

            if debug:
                b.op("vector", lambda E: E.tensor_copy(t["Asq"][:, :256], t["Stile"][:, :]))
                dbg_dump("Sfinal", t["Asq"][:, :256], (128, 256))
            if skip_b:
                for g in range(NG):
                    for dc in range(8):
                        b.dma("gpsimd", out_h[128 * dc:128 * (dc + 1), 512 * g:512 * (g + 1)],
                              t["Asq"][:, :512], waits=("vector",))
                b.op("gpsimd", lambda E: E.memset(t["zero"][:1, :1], 0.0),
                     waits=("gpsimd",), inc_k=0)
                b.finalize(block)
                return nc
            esB = contextlib.ExitStack()
            with esB:
                esB1 = contextlib.ExitStack()
                esB1.__enter__()
                sb("w1qb", [128, 16384], BF16, esB1)
                sb("w2qb", [128, 16384], BF16, esB1)
                sb("xnqg", [128, 4096], BF16, esB1)
                sb("xnqg2", [128, 4096], BF16, esB1)
                sb("wstage2", [128, 1024], F32, esB1)
                sb("hq", [128, 8192], BF16, esB1)
                sb("y2g", [128, 1024], F32, esB1)
                psum("qA", [128, 512], F32, esB1)
                psum("qB", [128, 512], F32, esB1)
                psum("qC", [128, 512], F32, esB1)
                psum("qD", [128, 512], F32, esB1)
                sb("b1row", [1, 4096], BF16, esB1); sb("b2row", [1, 1024], BF16, esB1)
                sb("onesbf", [1, 512], BF16, esB1)
                b.op("vector", lambda E: E.memset(t["onesbf"][:, :], 1.0))
                b.dma("gpsimd", ap(t["b1row"], 0, [(4096, 1), (1, 4096)]),
                      ap(t["b1qT"], 0, [(128, 32), (1, 128)]), waits=("vector",))
                b.dma("gpsimd", ap(t["b2row"], 0, [(1024, 1), (1, 1024)]),
                      ap(t["b2qT"], 0, [(128, 8), (1, 128)]), waits=("vector",))

                xnq_gp_mark = b.mark("gpsimd")
                y2_done = {}
                for half in range(2):
                    qd_free = 0
                    wq_guard = [("tensor", b.mark("tensor"))] if half == 1 else []
                    first = True
                    wtiles = ([("w1", nt) for nt in range(16)]
                              + [("w2", (dc, qrt)) for dc in range(8) for qrt in range(2)])
                    wbuf_free = {0: 0, 1: 0}
                    ld_done = {}
                    for ti, (kind, idx) in enumerate(wtiles):
                        bi = ti % 2
                        buf = t["wstage"] if bi == 0 else t["wstage2"]
                        q = "sync" if bi == 0 else "gpsimd"
                        if kind == "w1":
                            nt = idx
                            srcap = w1_h[2048 * half + 128 * nt:2048 * half + 128 * (nt + 1), :]
                        else:
                            dc, qrt = idx
                            col = 2048 * half + 1024 * qrt
                            srcap = w2_h[128 * dc:128 * (dc + 1), col:col + 1024]
                        wa = [("tensor", wbuf_free[bi])]
                        if first:
                            wa = [("tensor", b.mark("tensor")), ("scalar", b.mark("scalar")),
                                  ("vector", b.mark("vector"))]
                        b.dma(q, buf[:, :1024], srcap, waits_abs=wa)
                        first = False
                        ld_done[ti] = (q, b.mark(q))
                        for rnd in range(2):
                            for j in range(4):
                                fc = 4 * rnd + j
                                b.op("tensor", lambda E, fc=fc, j=j, buf=buf: E.transpose(
                                    t["qD"][:, 128 * j:128 * (j + 1)],
                                    buf[:, 128 * fc:128 * (fc + 1)], t["sidentf"][:, :]),
                                    waits_abs=([ld_done[ti], ("scalar", qd_free)]
                                               if j == 0 else []))
                            if kind == "w1":
                                out = ap(t["w1qb"], 2048 * 4 * rnd + 128 * nt,
                                         [(16384, 128), (2048, 4), (1, 128)])
                            else:
                                out = ap(t["w2qb"], 1024 * (8 * qrt + 4 * rnd) + 128 * dc,
                                         [(16384, 128), (1024, 4), (1, 128)])
                            b.op("scalar", lambda E: E.activation(t["q8"][:, :512],
                                                                  t["qD"][:, :512], ACT.Copy),
                                 waits=("tensor", "vector"))
                            qd_free = b.mark("scalar")
                            b.op("vector", lambda E, out=out: E.tensor_copy(t["q8" if False else "q8"][:, :512], t["q8"][:, :512]) if False else E.tensor_copy(out, t["q8"][:, :512]),
                                 waits=("scalar",), waits_abs=wq_guard)
                            wq_guard = []
                        wbuf_free[bi] = b.mark("tensor")
                    w_ready = b.mark("vector")
                    bank_free = {"qA": 0, "qB": 0}
                    qc_free = 0
                    if half == 0:
                        y2dma_mark = {0: 0, 1: 0}
                    xnqg_rd = {0: b.mark("tensor"), 1: b.mark("tensor")}
                    pair_mm_mark = {}

                    def xq_of(g):
                        return t["xnqg" if g % 2 == 0 else "xnqg2"]

                    def emit_loads(g):
                        gcol = slice(512 * g, 512 * (g + 1))
                        for fc in range(8):
                            b.dma("sync", xq_of(g)[:, 512 * fc:512 * (fc + 1)],
                                  xnq_s[128 * fc:128 * (fc + 1), gcol],
                                  waits_abs=([("tensor", xnqg_rd[g % 2]),
                                              ("gpsimd", xnq_gp_mark)] if fc == 0 else []))
                        return b.mark("sync")

                    def emit_mm(g, pair, ld_mark):
                        for s_, bank in ((0, "qA"), (1, "qB")):
                            nch = 2 * pair + s_
                            for fc in range(8):
                                b.op("tensor", lambda E, bank=bank, nch=nch, fc=fc, g=g: E.matmul(
                                    t[bank][:, :512],
                                    t["w1qb"][:, 2048 * fc + 128 * nch:2048 * fc + 128 * (nch + 1)],
                                    xq_of(g)[:, 512 * fc:512 * (fc + 1)],
                                    start=(fc == 0), stop=(fc == 7), skip_group_check=True),
                                    waits_abs=([("sync", ld_mark), ("vector", w_ready),
                                                ("scalar", bank_free[bank])] if fc == 0 else []))
                        pair_mm_mark[(g, pair)] = b.mark("tensor")

                    ld_marks = {0: emit_loads(0)}
                    emit_mm(0, 0, ld_marks[0])
                    ffn2_end = 0
                    for g in range(NG):
                        if g + 1 < NG and (g + 1) not in ld_marks:
                            ld_marks[g + 1] = emit_loads(g + 1)
                        for pair in range(8):
                            for s_, bank in ((0, "qA"), (1, "qB")):
                                nch = 2 * pair + s_
                                ow = ([("tensor", ffn2_end)]
                                      if (pair == 0 and s_ == 0) else [])
                                ng = 16 * half + nch
                                b.op("scalar", lambda E, bank=bank, ng=ng: E.activation(
                                    t["qb16"][:, :512], t[bank][:, :512], ACT.Relu,
                                    bias=t["Ar"][:, ng:ng + 1], scale=1.0),
                                    waits=("vector",),
                                    waits_abs=([("tensor", pair_mm_mark[(g, pair)])]
                                               if s_ == 0 else []))
                                bank_free[bank] = b.mark("scalar")
                                b.op("scalar", lambda E: E.activation(
                                    t["q8"][:, :512], t["qb16"][:, :512], ACT.Copy))
                                b.op("vector", lambda E, nch=nch: E.tensor_copy(
                                    t["hq"][:, 512 * nch:512 * (nch + 1)], t["q8"][:, :512]),
                                    waits=("scalar",), waits_abs=ow)
                            if pair + 1 < 8:
                                emit_mm(g, pair + 1, ld_marks[g])
                            elif g + 1 < NG:
                                emit_mm(g + 1, 0, ld_marks[g + 1])
                        xnqg_rd[g % 2] = b.mark("tensor")
                        dbg_dump("hq", t["hq"][:, :], (128, 8192), BF16,
                                 when=(half == 0 and g == 0))
                        hq_mark = b.mark("vector")
                        for dc in range(8):
                            for ncq in range(16):
                                b.op("tensor", lambda E, dc=dc, ncq=ncq: E.matmul(
                                    t["qC"][:, :512],
                                    t["w2qb"][:, 1024 * ncq + 128 * dc:1024 * ncq + 128 * (dc + 1)],
                                    t["hq"][:, 512 * ncq:512 * (ncq + 1)],
                                    start=(ncq == 0), stop=(half == 1 and ncq == 15), skip_group_check=True),
                                    waits_abs=([("vector", hq_mark), ("scalar", qc_free)]
                                               if ncq == 0 else [])),
                            if half == 0:
                                b.op("tensor", lambda E, dc=dc: E.matmul(
                                    t["qC"][:, :512], t["b2row"][0:1, 128 * dc:128 * (dc + 1)],
                                    t["onesbf"][:, :512], start=False, stop=True, skip_group_check=True),
                                    waits=("gpsimd",) if dc == 0 and g == 0 else ())
                            slot = dc % 2
                            b.op("scalar", lambda E, slot=slot: E.activation(
                                t["y2g"][:, 512 * slot:512 * (slot + 1)], t["qC"][:, :512], ACT.Copy),
                                waits=("tensor",),
                                waits_abs=[("gpsimd", y2dma_mark[slot])])
                            qc_free = b.mark("scalar")
                            tgt = y2_s if half == 0 else y2_s1
                            b.dma("gpsimd", tgt[128 * dc:128 * (dc + 1), 512 * g:512 * (g + 1)],
                                  t["y2g"][:, 512 * slot:512 * (slot + 1)], waits=("scalar",))
                            y2dma_mark[slot] = b.mark("gpsimd")
                        if half == 1:
                            y2_done[g] = b.mark("gpsimd")
                        ffn2_end = b.mark("tensor")


                esB1.__exit__(None, None, None)
                sb("wob", [128, 4096], BF16, esB)
                sb("y2f", [128, 4096], F32, esB); sb("y2f2", [128, 4096], F32, esB)
                sb("y2h", [128, 4096], F32, esB); sb("y2h2", [128, 4096], F32, esB)
                sb("xg", [128, 4096], F32, esB); sb("xg2", [128, 4096], F32, esB)
                sb("maskg", [128, 4096], BF16, esB)
                sb("outt", [128, 512], F32, esB)
                psum("pJ", [128, 512], F32, esB)
                psum("pK", [128, 512], F32, esB)
                for vc in range(4):
                    b.dma("sync", t["wstage"][:, :1024], wo_h[128 * vc:128 * (vc + 1), :],
                          waits=("vector", "tensor"))
                    b.op("vector", lambda E, vc=vc: E.tensor_copy(
                        t["wob"][:, 1024 * vc:1024 * (vc + 1)], t["wstage"][:, :1024]),
                        waits=("sync",))

                fb_free = {0: {"v": 0, "t": 0}, 1: {"v": 0, "t": 0}}
                ld_done = {}

                def fin_bufs(g):
                    s = "" if g % 2 == 0 else "2"
                    return (t["y2f" + s], t["y2h" + s], t["xg" + s], t["maskg"])

                def fin_loads(g):
                    y2fb, y2hb, xgb, mkb = fin_bufs(g)
                    fr = fb_free[g % 2]
                    wa = [("vector", fr["v"]), ("tensor", fr["t"]),
                          ("gpsimd", y2_done[g])]
                    b.dma("sync", ap(y2fb, 0, [(4096, 128), (512, 8), (1, 512)]),
                          ap(y2_s, 512 * g, [(L, 128), (128 * L, 8), (1, 512)]),
                          waits_abs=wa)

                    b.dma("scalar", ap(y2hb, 0, [(4096, 128), (512, 8), (1, 512)]),
                          ap(y2_s1, 512 * g, [(L, 128), (128 * L, 8), (1, 512)]),
                          waits_abs=wa)
                    b.dma("scalar", xgb[:, :2048],
                          ap(x_h, 512 * g * 1024, [(1024, 128), (131072, 2), (1, 1024)]))
                    b.dma("gpsimd", xgb[:, 2048:],
                          ap(x_h, 512 * g * 1024 + 262144, [(1024, 128), (131072, 2), (1, 1024)]),
                          waits_abs=wa)
                    ld_done[g] = {"sync": b.mark("sync"), "gpsimd": b.mark("gpsimd"),
                                  "scalar": b.mark("scalar")}

                fin_loads(0)
                mk_rd = 0
                for g in range(NG):
                    if g + 1 < NG:
                        fin_loads(g + 1)
                    y2fb, y2hb, xgb, mkb = fin_bufs(g)
                    b.dma("sync", ap(mkb, 0, [(4096, 128), (512, 8), (1, 512)]),
                          ap(maskt_h, 512 * g, [(L, 128), (128 * L, 8), (1, 512)]),
                          waits_abs=[("vector", mk_rd)])
                    mk_ld = b.mark("sync")
                    gcol = slice(512 * g, 512 * (g + 1))
                    for dc in range(8):
                        b.op("vector", lambda E, dc=dc, y2fb=y2fb, y2hb=y2hb: E.tensor_tensor(
                            y2fb[:, 512 * dc:512 * (dc + 1)],
                            y2fb[:, 512 * dc:512 * (dc + 1)],
                            y2hb[:, 512 * dc:512 * (dc + 1)], ALU.add),
                            waits_abs=([("sync", ld_done[g]["sync"]),
                                        ("scalar", ld_done[g]["scalar"])] if dc == 0 else []))
                    pre_rm_v = b.mark("vector")
                    dbg_dump("y2fin", y2fb[:, :], (128, 4096), when=(g == 0))
                    pj_free = pre_rm_v
                    pk_free = pre_rm_v
                    for dc in range(8):
                        for vc in range(4):
                            b.op("tensor", lambda E, dc=dc, vc=vc, g=g: E.matmul(
                                t["pJ"][:, :512],
                                t["wob"][:, 1024 * vc + 128 * dc:1024 * vc + 128 * (dc + 1)],
                                ap(t["ys"], L * vc + 512 * g, [(4 * L, 128), (1, 512)]),
                                start=(vc == 0), stop=(vc == 3), skip_group_check=True),
                                waits_abs=([("vector", pj_free)] if vc == 0 else []))
                        for tt_ in range(4):
                            b.op("tensor", lambda E, tt_=tt_, dc=dc, xgb=xgb: E.transpose(
                                t["pK"][:, 128 * tt_:128 * (tt_ + 1)],
                                xgb[:, 1024 * tt_ + 128 * dc:1024 * tt_ + 128 * (dc + 1)],
                                t["sidentf"][:, :]),
                                waits_abs=([("vector", pk_free),
                                            ("scalar", ld_done[g]["scalar"]),
                                            ("gpsimd", ld_done[g]["gpsimd"])] if tt_ == 0 else []))
                        mm_mark = b.mark("tensor")
                        b.op("vector", lambda E, dc=dc, y2fb=y2fb: E.tensor_copy(
                            t["qb16"][:, :512], y2fb[:, 512 * dc:512 * (dc + 1)]))
                        b.op("vector", lambda E, mkb=mkb, dc=dc: E.tensor_tensor(
                            t["outt"][:, :512], t["qb16"][:, :512],
                            mkb[:, 512 * dc:512 * (dc + 1)], ALU.mult),
                            waits_abs=[("tensor", mm_mark), ("sync", mk_ld),
                                       ("gpsimd", b.mark("gpsimd"))])
                        if debug and dc == 0:
                            b.op("vector", lambda E: E.tensor_copy(t["Asq"][:, :512], t["pJ"][:, :512]))
                            dbg_dump("att0", t["Asq"][:, :512], (128, 512), when=True)
                        b.op("vector", lambda E: E.tensor_tensor(
                            t["outt"][:, :512], t["outt"][:, :512], t["pJ"][:, :512], ALU.add))
                        pj_free = b.mark("vector")
                        b.op("vector", lambda E: E.tensor_tensor(
                            t["outt"][:, :512], t["outt"][:, :512], t["pK"][:, :512], ALU.add))
                        pk_free = b.mark("vector")
                        dbg_dump("outt0", t["outt"][:, :512], (128, 512),
                                 when=(g == 0 and dc == 0))
                        if debug and g == 0 and dc == 0:
                            b.op("vector", lambda E, y2fb=y2fb: E.tensor_copy(t["At1"][:, :512], y2fb[:, :512]))
                            dbg_dump("y2r0", t["At1"][:, :512], (128, 512), when=True)
                            b.op("vector", lambda E: E.tensor_copy(t["At1"][:, :512], t["pK"][:, :512]))
                            dbg_dump("xfm0", t["At1"][:, :512], (128, 512), when=True)
                            b.op("vector", lambda E, mkb=mkb: E.tensor_copy(t["At1"][:, :512], mkb[:, :512]))
                            dbg_dump("msk0", t["At1"][:, :512], (128, 512), when=True)
                        b.dma("gpsimd", out_h[128 * dc:128 * (dc + 1), gcol],
                              t["outt"][:, :512], waits=("vector",))
                    mk_rd = b.mark("vector")
                    fb_free[g % 2] = {"v": mk_rd, "t": b.mark("tensor")}
                b.op("gpsimd", lambda E: E.memset(t["zero"][:1, :1], 0.0),
                     waits=("gpsimd",), inc_k=0)

            b.finalize(block)
    return nc


_NC_CACHE = {}


def _bf16():
    import ml_dtypes
    return ml_dtypes.bfloat16


def kernel(L0_attn_x, L0_attn_W_sel, L0_attn_dt_W, L0_attn_dt_b, L0_attn_Wk,
           L0_attn_Wv, L0_attn_Wr, L0_attn_Wg, L0_attn_Wo, L0_attn_base_decay,
           L0_attn_time_first, L0_ffn_W1, L0_ffn_b1, L0_ffn_W2, L0_ffn_b2):
    import jax
    B, L, D = L0_attn_x.shape
    if L not in _NC_CACHE:
        _NC_CACHE[L] = build_nc(L)
    nc = _NC_CACHE[L]
    jax.config.update("jax_threefry_partitionable", True)
    with jax.default_device(jax.devices("cpu")[0]):
        keep = np.asarray(jax.random.bernoulli(
            jax.random.key(42, impl="threefry2x32"), 0.8, (B, L, D)))
    maskt = np.where(keep, np.float32(1.25), np.float32(0.0)).astype(_bf16())
    maskt = np.ascontiguousarray(np.swapaxes(maskt, 1, 2))
    f32 = np.float32
    common = {
        "wsel": np.ascontiguousarray(L0_attn_W_sel[:, :96], dtype=f32),
        "dtw": np.ascontiguousarray(L0_attn_dt_W, dtype=f32),
        "dtb": np.ascontiguousarray(np.asarray(L0_attn_dt_b, dtype=f32).reshape(4, 128).T),
        "wk": np.ascontiguousarray(L0_attn_Wk, dtype=f32),
        "wv": np.ascontiguousarray(L0_attn_Wv, dtype=f32),
        "wr": np.ascontiguousarray(L0_attn_Wr, dtype=f32),
        "wg": np.ascontiguousarray(L0_attn_Wg, dtype=f32),
        "wo": np.ascontiguousarray(L0_attn_Wo, dtype=f32),
        "bd": np.ascontiguousarray(np.asarray(L0_attn_base_decay, dtype=f32).reshape(512).reshape(4, 128).T),
        "tf": np.ascontiguousarray(np.asarray(L0_attn_time_first, dtype=f32).reshape(512).reshape(4, 128).T),
        "w1": np.ascontiguousarray(L0_ffn_W1, dtype=f32),
        "b1": np.ascontiguousarray(np.asarray(L0_ffn_b1, dtype=f32).reshape(32, 128).T),
        "w2": np.ascontiguousarray(L0_ffn_W2, dtype=f32),
        "b2": np.ascontiguousarray(np.asarray(L0_ffn_b2, dtype=f32).reshape(8, 128).T),
        "identf": np.eye(128, dtype=f32),
        "identb": np.eye(128, dtype=f32).astype(_bf16()),
        "utm": np.triu(np.ones((128, 128), dtype=np.uint8)),
    }
    in_maps = []
    for bi in range(B):
        m = dict(common)
        m["x"] = np.ascontiguousarray(L0_attn_x[bi], dtype=f32)
        m["maskt"] = np.ascontiguousarray(maskt[bi])
        in_maps.append(m)
    res = run_bass_kernel_spmd(nc, in_maps, core_ids=list(range(B)))
    outs = [np.ascontiguousarray(np.asarray(r["out"]).T) for r in res.results]
    return np.stack(outs).astype(np.float32)


# version 12 (final)
# speedup vs baseline: 2.8972x; 1.0120x over previous
import contextlib
import numpy as np

import concourse.bass as bass
import concourse.mybir as mybir
from concourse.bass_utils import run_bass_kernel_spmd

F32 = mybir.dt.float32
FP8 = mybir.dt.float8e5
BF16 = mybir.dt.bfloat16
I32 = mybir.dt.int32
U8 = mybir.dt.uint8
ALU = mybir.AluOpType
ACT = mybir.ActivationFunctionType
AX = mybir.AxisListType

C2 = 256.0 / (2.0 * np.log(2.0))

FQ_THR = [(16.0, ALU.is_gt), (48.0, ALU.is_ge), (80.0, ALU.is_gt), (112.0, ALU.is_ge),
          (144.0, ALU.is_gt), (176.0, ALU.is_ge), (208.0, ALU.is_gt), (240.0, ALU.is_ge)]
COMP_THR = [(16.0, ALU.is_gt), (48.0, ALU.is_ge), (80.0, ALU.is_gt), (112.0, ALU.is_ge),
            (144.5, ALU.is_gt), (175.5, ALU.is_ge), (208.5, ALU.is_gt), (239.5, ALU.is_ge)]


def ap(tensor, offset, dims):
    return bass.AP(tensor, offset, [list(d) for d in dims])


class Builder:
    ENGINES = ("sync", "scalar", "vector", "tensor", "gpsimd")

    def __init__(self, nc, es):
        self.nc = nc
        self.ops = {e: [] for e in self.ENGINES}
        self.sems = {}
        self.counts = {}
        for e in self.ENGINES:
            self.sems[e] = es.enter_context(nc.semaphore(f"s_{e}"))
            self.counts[e] = 0

    def op(self, eng, fn, waits=(), waits_abs=(), inc_k=1):
        wl = [(w, self.counts[w]) for w in waits if self.counts[w] > 0]
        wl += [(w, c) for (w, c) in waits_abs if c > 0]
        self.ops[eng].append((fn, wl, inc_k))
        self.counts[eng] += inc_k

    def mark(self, eng):
        return self.counts[eng]

    def dma(self, eng, out_ap, in_ap, waits=(), waits_abs=(), **kw):
        self.op(eng, (lambda E, o=out_ap, i=in_ap, kw=kw: E.dma_start(o, i, **kw)),
                waits=waits, waits_abs=waits_abs, inc_k=16)

    def finalize(self, block):
        sems = self.sems

        def mk(ename):
            ops = self.ops[ename]

            def emit(E):
                for fn, wl, inck in ops:
                    for (w, v) in wl:
                        E.wait_ge(sems[w], v)
                    inst = fn(E)
                    if inck:
                        inst.then_inc(sems[ename], inck)
            return emit
        block.sync(mk("sync"))
        block.scalar(mk("scalar"))
        block.vector(mk("vector"))
        block.tensor(mk("tensor"))
        block.gpsimd(mk("gpsimd"))


def emit_quant(b, t, y_ap, out_ap, width, mode, waits_sc=(), waits_v=(), out_waits=(),
               evac=None, out_waits_abs=(), waits_sc_abs=()):
    W = width
    sq, lg = t["Asq"], t["Alg"]
    u, i256, eint, e2int = t["Au"], t["Ai256"], t["Aeint"], t["Ae2int"]
    shS, shI, stepf, invf = t["AshS"], t["AshI"], t["Astepf"], t["Ainvf"]
    q, L = t["Aq"], t["AL"]
    b.op("scalar", lambda E, y=y_ap: E.activation(sq[:, :W], y, ACT.Square),
         waits=tuple(waits_sc) + ("vector",), waits_abs=waits_sc_abs)
    if evac is not None:
        b.op("scalar", lambda E, y=y_ap: E.activation(evac, y, ACT.Copy))
        y_ap = evac
    free_mark = b.mark("scalar")
    b.op("scalar", lambda E: E.activation(lg[:, :W], sq[:, :W], ACT.Ln,
                                          bias=t["tinyb"][:, :], scale=1.0))
    if mode == "rm":
        b.op("vector", lambda E: E.tensor_scalar(u[:, :W], lg[:, :W], C2, 20480.0,
                                                 ALU.mult, ALU.add),
             waits=("scalar",) + tuple(waits_v))
        b.op("vector", lambda E: E.tensor_single_scalar(u[:, :W], u[:, :W], 12288.0, ALU.max))
        b.op("vector", lambda E: E.tensor_copy(i256[:, :W], u[:, :W]))
        b.op("vector", lambda E: E.tensor_single_scalar(eint[:, :W], i256[:, :W], 8,
                                                        ALU.arith_shift_right))
        b.op("vector", lambda E: E.tensor_single_scalar(eint[:, :W], eint[:, :W], -64, ALU.add))
        b.op("vector", lambda E: E.tensor_single_scalar(e2int[:, :W], eint[:, :W], 16, ALU.add))
        b.op("vector", lambda E: E.tensor_single_scalar(shS[:, :W], e2int[:, :W], 1,
                                                        ALU.arith_shift_right))
        b.op("vector", lambda E: E.tensor_tensor(shI[:, :W], e2int[:, :W], shS[:, :W],
                                                 ALU.subtract))
        b.op("vector", lambda E: E.tensor_tensor(shS[:, :W], t["onei"][:, :W], shS[:, :W],
                                                 ALU.logical_shift_left))
        b.op("vector", lambda E: E.tensor_tensor(shI[:, :W], t["onei"][:, :W], shI[:, :W],
                                                 ALU.logical_shift_left))
        b.op("vector", lambda E: E.tensor_copy(stepf[:, :W], shS[:, :W]))
        b.op("vector", lambda E: E.tensor_copy(q[:, :W], shI[:, :W]))
        b.op("vector", lambda E: E.tensor_tensor(stepf[:, :W], stepf[:, :W], q[:, :W],
                                                 ALU.mult))
        b.op("vector", lambda E: E.tensor_tensor(e2int[:, :W], t["thirty1"][:, :W], eint[:, :W],
                                                 ALU.subtract))
        b.op("vector", lambda E: E.tensor_single_scalar(shS[:, :W], e2int[:, :W], 1,
                                                        ALU.arith_shift_right))
        b.op("vector", lambda E: E.tensor_tensor(shI[:, :W], e2int[:, :W], shS[:, :W],
                                                 ALU.subtract))
        b.op("vector", lambda E: E.tensor_tensor(shS[:, :W], t["onei"][:, :W], shS[:, :W],
                                                 ALU.logical_shift_left))
        b.op("vector", lambda E: E.tensor_tensor(shI[:, :W], t["onei"][:, :W], shI[:, :W],
                                                 ALU.logical_shift_left))
        b.op("vector", lambda E: E.tensor_copy(invf[:, :W], shS[:, :W]))
        b.op("vector", lambda E: E.tensor_copy(q[:, :W], shI[:, :W]))
        b.op("vector", lambda E: E.tensor_tensor(invf[:, :W], invf[:, :W], q[:, :W],
                                                 ALU.mult))
    else:
        b.op("vector", lambda E: E.tensor_scalar(u[:, :W], lg[:, :W], C2, 4096.0,
                                                 ALU.mult, ALU.add),
             waits=("scalar",) + tuple(waits_v))
        b.op("vector", lambda E: E.tensor_single_scalar(u[:, :W], u[:, :W], 512.0, ALU.max))
        b.op("vector", lambda E: E.tensor_copy(i256[:, :W], u[:, :W]))
        b.op("vector", lambda E: E.tensor_single_scalar(eint[:, :W], i256[:, :W], 8,
                                                        ALU.arith_shift_right))
        b.op("vector", lambda E: E.tensor_tensor(shS[:, :W], t["onei"][:, :W], eint[:, :W],
                                                 ALU.logical_shift_left))
        b.op("vector", lambda E: E.tensor_tensor(e2int[:, :W], eint[:, :W], t["thirty1"][:, :W],
                                                 ALU.bitwise_xor))
        b.op("vector", lambda E: E.tensor_tensor(shI[:, :W], t["onei"][:, :W], e2int[:, :W],
                                                 ALU.logical_shift_left))
        b.op("vector", lambda E: E.tensor_copy(stepf[:, :W], shS[:, :W]))
        b.op("vector", lambda E: E.tensor_copy(invf[:, :W], shI[:, :W]))
    b.op("vector", lambda E, y=y_ap: E.scalar_tensor_tensor(q[:, :W], invf[:, :W], 2.0 ** -8, y,
                                                    ALU.mult, ALU.mult))
    if mode == "comp":
        (t0, o0) = COMP_THR[0]
        b.op("vector", lambda E: E.tensor_single_scalar(L[:, :W], q[:, :W], t0, o0))
        for (tv, ov) in COMP_THR[1:]:
            b.op("vector", lambda E, tv=tv, ov=ov: E.scalar_tensor_tensor(
                L[:, :W], q[:, :W], tv, L[:, :W], ov, ALU.add))
        b.op("vector", lambda E: E.scalar_tensor_tensor(
            out_ap, L[:, :W], 2.0 ** -18, stepf[:, :W], ALU.mult, ALU.mult),
            waits=out_waits, waits_abs=out_waits_abs)
        return free_mark
    nq, aq, sg = t["Anq"], t["Aaq"], t["Asg"]
    b.op("vector", lambda E: E.tensor_single_scalar(nq[:, :W], q[:, :W], -1.0, ALU.mult))
    b.op("vector", lambda E: E.tensor_tensor(aq[:, :W], q[:, :W], nq[:, :W], ALU.max))
    if mode == "fq":
        (t0, o0) = FQ_THR[0]
        b.op("vector", lambda E: E.tensor_single_scalar(L[:, :W], aq[:, :W], t0, o0))
        for (tv, ov) in FQ_THR[1:]:
            b.op("vector", lambda E, tv=tv, ov=ov: E.scalar_tensor_tensor(
                L[:, :W], aq[:, :W], tv, L[:, :W], ov, ALU.add))
        res, scl = L, 2.0 ** -18
    else:
        r, fl, b1, t1, tie = t["Ar"], t["Afl"], t["Ab1"], t["At1"], t["Atie"]
        b.op("vector", lambda E: E.tensor_single_scalar(r[:, :W], aq[:, :W], -128.0, ALU.add))
        for w in (64.0, 32.0, 16.0, 8.0, 4.0, 2.0):
            b.op("vector", lambda E, w=w: E.tensor_single_scalar(t1[:, :W], r[:, :W], w, ALU.is_ge))
            b.op("vector", lambda E, w=w: E.scalar_tensor_tensor(
                r[:, :W], t1[:, :W], -w, r[:, :W], ALU.mult, ALU.add))
        b.op("vector", lambda E: E.tensor_single_scalar(b1[:, :W], r[:, :W], 1.0, ALU.is_ge))
        b.op("vector", lambda E: E.scalar_tensor_tensor(
            r[:, :W], b1[:, :W], -1.0, r[:, :W], ALU.mult, ALU.add))
        b.op("vector", lambda E: E.tensor_tensor(fl[:, :W], aq[:, :W], r[:, :W], ALU.subtract))
        b.op("vector", lambda E: E.tensor_single_scalar(t1[:, :W], r[:, :W], 0.5, ALU.is_gt))
        b.op("vector", lambda E: E.tensor_tensor(fl[:, :W], fl[:, :W], t1[:, :W], ALU.add))
        b.op("vector", lambda E: E.tensor_single_scalar(tie[:, :W], r[:, :W], 0.5, ALU.is_equal))
        b.op("vector", lambda E: E.tensor_tensor(tie[:, :W], tie[:, :W], b1[:, :W], ALU.mult))
        b.op("vector", lambda E: E.tensor_tensor(fl[:, :W], fl[:, :W], tie[:, :W], ALU.add))
        b.op("vector", lambda E: E.tensor_single_scalar(t1[:, :W], aq[:, :W], 1.0, ALU.is_ge))
        b.op("vector", lambda E: E.tensor_tensor(fl[:, :W], fl[:, :W], t1[:, :W], ALU.mult))
        res, scl = fl, 2.0 ** -39
    b.op("vector", lambda E: E.tensor_scalar(sg[:, :W], q[:, :W], 0.0, 2.0, ALU.is_ge, ALU.mult))
    b.op("vector", lambda E: E.tensor_single_scalar(sg[:, :W], sg[:, :W], -1.0, ALU.add))
    b.op("vector", lambda E, res=res: E.tensor_tensor(res[:, :W], res[:, :W], sg[:, :W], ALU.mult))
    b.op("vector", lambda E, res=res, scl=scl: E.scalar_tensor_tensor(
        out_ap, res[:, :W], scl, stepf[:, :W], ALU.mult, ALU.mult),
        waits=out_waits, waits_abs=out_waits_abs)
    return free_mark


def build_nc(L=4096, debug=False, skip_b=False, skip_a=False, astage=99.0):
    NCH = L // 128
    NG = L // 512
    nc = bass.Bass("TRN2", target_bir_lowering=False, detect_race_conditions=False)

    x_h = nc.dram_tensor("x", [L, 1024], F32, kind="ExternalInput")
    wsel_h = nc.dram_tensor("wsel", [1024, 96], F32, kind="ExternalInput")
    dtw_h = nc.dram_tensor("dtw", [32, 512], F32, kind="ExternalInput")
    dtb_h = nc.dram_tensor("dtb", [128, 4], F32, kind="ExternalInput")
    wk_h = nc.dram_tensor("wk", [1024, 512], F32, kind="ExternalInput")
    wv_h = nc.dram_tensor("wv", [1024, 512], F32, kind="ExternalInput")
    wr_h = nc.dram_tensor("wr", [1024, 512], F32, kind="ExternalInput")
    wg_h = nc.dram_tensor("wg", [1024, 512], F32, kind="ExternalInput")
    wo_h = nc.dram_tensor("wo", [512, 1024], F32, kind="ExternalInput")
    bd_h = nc.dram_tensor("bd", [128, 4], F32, kind="ExternalInput")
    tf_h = nc.dram_tensor("tf", [128, 4], F32, kind="ExternalInput")
    w1_h = nc.dram_tensor("w1", [4096, 1024], F32, kind="ExternalInput")
    b1_h = nc.dram_tensor("b1", [128, 32], F32, kind="ExternalInput")
    w2_h = nc.dram_tensor("w2", [1024, 4096], F32, kind="ExternalInput")
    b2_h = nc.dram_tensor("b2", [128, 8], F32, kind="ExternalInput")
    maskt_h = nc.dram_tensor("maskt", [1024, L], BF16, kind="ExternalInput")
    identf_h = nc.dram_tensor("identf", [128, 128], F32, kind="ExternalInput")
    identb_h = nc.dram_tensor("identb", [128, 128], BF16, kind="ExternalInput")
    utm_h = nc.dram_tensor("utm", [128, 128], U8, kind="ExternalInput")
    out_h = nc.dram_tensor("out", [1024, L], F32, kind="ExternalOutput")
    xnq_s = nc.dram_tensor("xnq_s", [1024, L], BF16)
    y2_s = nc.dram_tensor("y2_s", [1024, L], F32)
    y2_s1 = nc.dram_tensor("y2_s1", [1024, L], F32)

    dbg = {}

    def mkdbg(name, shape, dt=F32):
        if name not in dbg:
            dbg[name] = nc.dram_tensor("dbg_" + name, list(shape), dt, kind="ExternalOutput")
        return dbg[name]

    es = contextlib.ExitStack()
    with es:
        b = Builder(nc, es)
        t = {}

        def dbg_dump(name, src_ap, shape, dt=F32, when=True):
            if debug and when:
                h = mkdbg(name, shape, dt)
                b.dma("gpsimd", h[:, :], src_ap, waits=("vector", "scalar", "tensor"))
                b.op("vector", lambda E: E.memset(t["zero"][:1, :1], 0.0), waits=("gpsimd",))

        def sb(name, shape, dt=F32, stack=es):
            t[name] = stack.enter_context(nc.sbuf_tensor(name, list(shape), dt))
            return t[name]

        def psum(name, shape, dt=F32, stack=es):
            t[name] = stack.enter_context(nc.psum_tensor(name, list(shape), dt))
            return t[name]

        sb("sidentf", [128, 128]); sb("sidentb", [128, 128], BF16)
        sb("sutm", [128, 128], U8)
        sb("zero", [128, 128], F32)
        sb("sdtb", [128, 4]); sb("sbd", [128, 4]); sb("bdneg", [128, 4]); sb("stf", [128, 4])
        sb("b1r", [128, 32]); sb("b2r", [128, 8])
        sb("b1qT", [32, 128], BF16); sb("b2qT", [8, 128], BF16)
        sb("wstage", [128, 1024], F32)
        sb("ys", [128, 4 * L], BF16)
        sb("Asq", [128, 1024])
        for nm in ("Alg", "Ar", "Afl", "At1", "Atie"):
            sb(nm, [128, 512])
        sb("minus1f", [128, 512], F32)
        sb("q8", [128, 512], FP8)
        sb("qb16", [128, 512], BF16)
        sb("eps5", [128, 1], F32)
        sb("one1", [128, 1], F32)


        with nc.Block() as block:
            esA = contextlib.ExitStack()
            with esA:
                sb("wkb", [128, 4096], BF16, esA); sb("wvb", [128, 4096], BF16, esA)
                sb("wrb", [128, 4096], BF16, esA); sb("wgb", [128, 4096], BF16, esA)
                sb("wselb", [128, 768], BF16, esA); sb("dtwb", [32, 512], BF16, esA)
                sb("xbuf0", [128, 1024], F32, esA); sb("xbuf1", [128, 1024], F32, esA)
                sb("xc", [128, 1024], F32, esA)
                sb("red", [128, 1], F32, esA); sb("mu", [128, 1], F32, esA)
                sb("var", [128, 1], F32, esA); sb("rstd", [128, 1], F32, esA)
                sb("musq", [128, 1], F32, esA)
                sb("dtev", [128, 32], F32, esA)
                sb("xn0", [128, 1024], F32, esA); sb("xn1", [128, 1024], F32, esA)
                sb("xnfmb0", [128, 1024], BF16, esA); sb("xnfmb1", [128, 1024], BF16, esA)
                sb("xnqb0", [128, 1024], FP8, esA); sb("xnqb1", [128, 1024], FP8, esA)
                sb("kev", [128, 512], F32, esA)
                sb("Pfull", [128, 512], F32, esA)
                sb("ell", [128, 512], F32, esA)
                sb("ez", [128, 512], F32, esA)
                sb("delta", [128, 512], F32, esA)
                sb("Pmid", [128, 32], F32, esA); sb("Pb64", [128, 4], F32, esA)
                sb("Pb32", [128, 8], F32, esA); sb("Pb16", [128, 16], F32, esA)
                sb("e1arg", [128, 512], F32, esA)
                sb("s64", [128, 512], F32, esA); sb("s32", [128, 512], F32, esA)
                sb("s16", [128, 512], F32, esA); sb("farg", [128, 512], F32, esA)
                sb("expE", [128, 512], F32, esA); sb("expF", [128, 512], F32, esA)
                sb("expE1", [128, 512], F32, esA); sb("expE2", [128, 512], F32, esA)
                sb("xq64", [128, 256], F32, esA); sb("xk64", [128, 256], F32, esA)
                sb("xq32", [128, 256], F32, esA); sb("xk32", [128, 256], F32, esA)
                sb("xq16", [128, 256], F32, esA); sb("xk16", [128, 256], F32, esA)
                sb("qtb", [128, 512], BF16, esA); sb("khf", [128, 512], F32, esA)
                sb("qtbB", [64, 512], BF16, esA)
                sb("ke1B", [64, 512], BF16, esA); sb("ke2B", [64, 512], BF16, esA)
                sb("kq64B", [64, 256], BF16, esA); sb("kk64B", [64, 256], BF16, esA)
                sb("kq32B", [64, 256], BF16, esA); sb("kk32B", [64, 256], BF16, esA)
                sb("kq16B", [64, 256], BF16, esA); sb("kk16B", [64, 256], BF16, esA)
                sb("SbB", [64, 256], BF16, esA)
                sb("khtm", [128, 512], BF16, esA)
                sb("ke1", [128, 512], BF16, esA); sb("ke2", [128, 512], BF16, esA)
                sb("kq64b", [128, 256], BF16, esA); sb("kk64b", [128, 256], BF16, esA)
                sb("kq32b", [128, 256], BF16, esA); sb("kk32b", [128, 256], BF16, esA)
                sb("kq16b", [128, 256], BF16, esA); sb("kk16b", [128, 256], BF16, esA)
                sb("vpb", [128, 512], BF16, esA); sb("vrawf", [128, 512], F32, esA)
                sb("vfm", [128, 512], F32, esA)
                sb("btok", [128, 1], F32, esA)
                sb("sigr", [128, 512], F32, esA); sb("sigg", [128, 512], F32, esA)
                sb("rg", [128, 512], F32, esA)
                sb("Amb", [128, 1024], BF16, esA)
                sb("kv", [128, 512], F32, esA); sb("bonus", [128, 512], F32, esA)
                sb("dtrawb", [32, 128], BF16, esA)
                sb("Stile", [128, 256], F32, esA)
                sb("Sb", [128, 256], BF16, esA)
                sb("yst", [128, 512], F32, esA)
                psum("pA", [128, 512], F32, esA)
                psum("pB", [128, 512], F32, esA)
                psum("pD", [128, 512], F32, esA)
                psum("pE", [128, 512], F32, esA)
                psum("pF", [128, 512], F32, esA)
                psum("pG", [128, 512], F32, esA)
                psum("pH", [128, 512], F32, esA)
                psum("pL", [128, 512], F32, esA)
                

                b.dma("sync", t["sidentf"][:, :], identf_h[:, :])
                b.dma("sync", t["sidentb"][:, :], identb_h[:, :])
                b.dma("sync", t["sutm"][:, :], utm_h[:, :])
                b.dma("sync", t["sdtb"][:, :], dtb_h[:, :])
                b.dma("sync", t["sbd"][:, :], bd_h[:, :])
                b.dma("sync", t["stf"][:, :], tf_h[:, :])
                b.dma("sync", t["b1r"][:, :], b1_h[:, :])
                b.dma("sync", t["b2r"][:, :], b2_h[:, :])
                b.dma("sync", t["wstage"][:32, :512], dtw_h[:, :])
                b.op("vector", lambda E: E.memset(t["zero"][:, :], 0.0), waits=("sync",))
                b.op("vector", lambda E: E.memset(t["minus1f"][:, :], -1.0))
                b.op("vector", lambda E: E.memset(t["eps5"][:, :], 1e-5))
                b.op("vector", lambda E: E.memset(t["one1"][:, :], 1.0))
                b.op("vector", lambda E: E.memset(t["Stile"][:, :], 0.0))
                b.op("vector", lambda E: E.tensor_single_scalar(t["bdneg"][:, :], t["sbd"][:, :], -1.0, ALU.mult))
                b.dma("gpsimd", t["dtwb"][:, :], dtw_h[:, :])
                for (wh, wb) in ((wk_h, "wkb"), (wv_h, "wvb"), (wr_h, "wrb"), (wg_h, "wgb")):
                    for fc in range(8):
                        b.dma("gpsimd", t[wb][:, 512 * fc:512 * (fc + 1)],
                              wh[128 * fc:128 * (fc + 1), :])
                for fc in range(8):
                    b.dma("gpsimd", t["wselb"][:, 96 * fc:96 * (fc + 1)],
                          wsel_h[128 * fc:128 * (fc + 1), :])
                wld_mark = b.mark("sync")
                wld_gp_mark = b.mark("gpsimd")
                b.op("vector", lambda E: E.tensor_copy(
                    t["At1"][:, :512], ap(t["b1r"], 0, [(32, 128), (0, 16), (1, 32)])),
                    waits=("sync",))
                b.op("vector", lambda E: E.tensor_copy(
                    t["Atie"][:, :512], ap(t["b2r"], 0, [(8, 128), (0, 64), (1, 8)])))
                b.op("vector", lambda E: E.tensor_copy(t["q8"][:, :512], t["At1"][:, :512]))
                b.op("vector", lambda E: E.tensor_copy(t["Ar"][:, :512], t["q8"][:, :512]))
                b.op("vector", lambda E: E.tensor_copy(t["q8"][:, :512], t["Atie"][:, :512]))
                b.op("vector", lambda E: E.tensor_copy(t["Afl"][:, :512], t["q8"][:, :512]))
                b.op("tensor", lambda E: E.transpose(t["pF"][:32, :128], t["Ar"][:, :32],
                                                     t["sidentf"][:, :]), waits=("vector",))
                b.op("vector", lambda E: E.tensor_copy(t["b1qT"][:, :], t["pF"][:32, :128]),
                     waits=("tensor",))
                b.op("tensor", lambda E: E.transpose(t["pF"][:8, :128], t["Afl"][:, :8],
                                                     t["sidentf"][:, :]), waits=("vector",))
                b.op("vector", lambda E: E.tensor_copy(t["b2qT"][:, :], t["pF"][:8, :128]),
                     waits=("tensor",))

                xbuf_free = {0: 0, 1: 0}
                fm_free = {0: 0, 1: 0}
                qb_free = {0: 0, 1: 0}
                ln_done = {}

                def emit_ln(n):
                    bs = n % 2
                    tok = slice(128 * n, 128 * (n + 1))
                    xb, xnn = t[f"xbuf{bs}"], t[f"xn{bs}"]
                    fm, qb = t[f"xnfmb{bs}"], t[f"xnqb{bs}"]
                    b.dma("sync", xb[:, :], x_h[tok, :],
                          waits_abs=[("vector", xbuf_free[bs])])
                    b.op("vector", lambda E: E.reduce_sum(t["red"][:, :], xb[:, :], AX.X),
                         waits=("sync",))
                    b.op("vector", lambda E: E.tensor_tensor(t["Asq"][:, :1024], xb[:, :],
                                                             xb[:, :], ALU.mult))
                    b.op("vector", lambda E: E.tensor_single_scalar(t["mu"][:, :], t["red"][:, :],
                                                                    1.0 / 1024.0, ALU.mult))
                    b.op("vector", lambda E: E.reduce_sum(t["red"][:, :], t["Asq"][:, :1024], AX.X))
                    b.op("vector", lambda E: E.tensor_tensor(t["xc"][:, :], xb[:, :],
                         ap(t["mu"], 0, [(1, 128), (0, 1024)]), ALU.subtract))
                    xbuf_free[bs] = b.mark("vector")
                    b.op("scalar", lambda E: E.activation(t["musq"][:, :], t["mu"][:, :],
                                                          ACT.Square), waits=("vector",))
                    b.op("vector", lambda E: E.scalar_tensor_tensor(t["var"][:, :], t["red"][:, :],
                         1.0 / 1024.0, t["musq"][:, :], ALU.mult, ALU.subtract),
                         waits=("scalar",))
                    b.op("scalar", lambda E: E.activation(t["var"][:, :], t["var"][:, :], ACT.Ln,
                                                          bias=t["eps5"][:, :], scale=1.0),
                         waits=("vector",))
                    b.op("scalar", lambda E: E.activation(t["Alg"][:, :512], t["Asq"][:, :512],
                                                          ACT.Copy))
                    b.op("scalar", lambda E: E.activation(t["rstd"][:, :], t["var"][:, :], ACT.Exp,
                                                          scale=-0.5))
                    b.op("vector", lambda E: E.scalar_tensor_tensor(xnn[:, :], t["xc"][:, :],
                         t["rstd"][:, :], t["xc"][:, :], ALU.mult, ALU.bypass),
                         waits=("scalar",))
                    for rnd in range(2):
                        for j in range(4):
                            fc = 4 * rnd + j
                            b.op("tensor", lambda E, fc=fc, j=j: E.transpose(
                                t["pL"][:, 128 * j:128 * (j + 1)],
                                xnn[:, 128 * fc:128 * (fc + 1)], t["sidentf"][:, :]),
                                waits=("vector", "scalar") if j == 0 else (),
                                waits_abs=[("tensor", fm_free[bs])] if rnd == 0 and j == 0 else [])
                        b.op("scalar", lambda E, rnd=rnd: E.activation(
                            fm[:, 512 * rnd:512 * (rnd + 1)], t["pL"][:, :], ACT.Copy),
                            waits=("tensor",))
                        b.op("scalar", lambda E: E.activation(t["q8"][:, :512], t["pL"][:, :],
                                                              ACT.Copy))
                        b.op("vector", lambda E, rnd=rnd: E.tensor_copy(
                            qb[:, 512 * rnd:512 * (rnd + 1)], t["q8"][:, :512]),
                            waits=("scalar",),
                            waits_abs=[("gpsimd", qb_free[bs])] if rnd == 0 else [])
                    dbg_dump("xn", xnn[:, :], (128, 1024), when=(n == 0))
                    dbg_dump("xnfmb", fm[:, :], (128, 1024), BF16, when=(n == 0))
                    for fc in range(8):
                        b.dma("gpsimd", xnq_s[128 * fc:128 * (fc + 1), tok],
                              qb[:, 128 * fc:128 * (fc + 1)], waits=("vector",))
                    qb_free[bs] = b.mark("gpsimd")
                    ln_done[n] = {"v": b.mark("vector"), "s": b.mark("scalar")}

                if not skip_a:
                    emit_ln(0)
                for n in range(NCH if not skip_a else 0):
                    bs = n % 2
                    tok = slice(128 * n, 128 * (n + 1))
                    if n + 1 < NCH:
                        emit_ln(n + 1)
                    fmb = t[f"xnfmb{bs}"]
                    if astage < 2:
                        continue
                    for fc in range(8):
                        b.op("tensor", lambda E, fc=fc, fmb=fmb: E.matmul(
                            t["pA"][:, :512], fmb[:, 128 * fc:128 * (fc + 1)],
                            t["wvb"][:, 512 * fc:512 * (fc + 1)],
                            start=(fc == 0), stop=(fc == 7), skip_group_check=True),
                            waits_abs=([("vector", ln_done[n]["v"]),
                                        ("scalar", ln_done[n]["s"]),
                                        ("sync", wld_mark),
                                        ("gpsimd", wld_gp_mark)] if fc == 0 else []))
                    for fc in range(8):
                        b.op("tensor", lambda E, fc=fc, fmb=fmb: E.matmul(
                            t["pB"][:, :96], fmb[:, 128 * fc:128 * (fc + 1)],
                            t["wselb"][:, 96 * fc:96 * (fc + 1)],
                            start=(fc == 0), stop=(fc == 7), skip_group_check=True))
                    for (wb, bank) in (("wkb", "pD"), ("wrb", "pE")):
                        for dc in range(4):
                            for fc in range(8):
                                b.op("tensor", lambda E, wb=wb, bank=bank, dc=dc, fc=fc, fmb=fmb: E.matmul(
                                    t[bank][:, 128 * dc:128 * (dc + 1)],
                                    t[wb][:, 512 * fc + 128 * dc:512 * fc + 128 * (dc + 1)],
                                    fmb[:, 128 * fc:128 * (fc + 1)],
                                    start=(fc == 0), stop=(fc == 7), skip_group_check=True))
                    b.op("vector", lambda E: E.reduce_sum(t["red"][:, :], t["pB"][:, 32:96], AX.X),
                         waits=("tensor",))
                    b.op("vector", lambda E: E.tensor_copy(t["vrawf"][:, :], t["pA"][:, :512]))
                    b.op("vector", lambda E: E.tensor_single_scalar(t["btok"][:, :], t["red"][:, :],
                                                                    1.0 / 64.0, ALU.mult))
                    b.op("vector", lambda E: E.tensor_copy(t["kev"][:, :], t["pD"][:, :512]))
                    b.op("vector", lambda E: E.scalar_tensor_tensor(t["vpb"][:, :], t["pA"][:, :512],
                         t["btok"][:, :], t["minus1f"][:, :], ALU.mult, ALU.bypass))
                    b.op("vector", lambda E: E.tensor_copy(t["dtev"][:, :32], t["pB"][:, 0:32]))
                    b.op("tensor", lambda E: E.transpose(t["pF"][:32, :128], t["dtev"][:, :32],
                                                         t["sidentf"][:, :]),
                         waits=("vector", "scalar"))
                    b.op("vector", lambda E: E.tensor_copy(t["dtrawb"][:, :], t["pF"][:32, :128]),
                         waits=("tensor",))
                    if astage < 3:
                        continue
                    b.op("scalar", lambda E: E.activation(t["sigr"][:, :], t["pE"][:, :512],
                                                          ACT.Exp, scale=-1.0), waits=("vector",))
                    b.op("scalar", lambda E: E.activation(t["sigr"][:, :], t["sigr"][:, :], ACT.Ln,
                                                          bias=t["one1"][:, :], scale=1.0))
                    b.op("scalar", lambda E: E.activation(t["sigr"][:, :], t["sigr"][:, :], ACT.Exp,
                                                          scale=-1.0))
                    for dc in range(4):
                        for fc in range(8):
                            b.op("tensor", lambda E, dc=dc, fc=fc, fmb=fmb: E.matmul(
                                t["pE"][:, 128 * dc:128 * (dc + 1)],
                                t["wgb"][:, 512 * fc + 128 * dc:512 * fc + 128 * (dc + 1)],
                                fmb[:, 128 * fc:128 * (fc + 1)],
                                start=(fc == 0), stop=(fc == 7), skip_group_check=True),
                                waits=("scalar",) if (dc == 0 and fc == 0) else ())
                    fm_free[bs] = b.mark("tensor")
                    b.op("scalar", lambda E: E.activation(t["sigg"][:, :], t["pE"][:, :512],
                                                          ACT.Exp, scale=-1.0), waits=("tensor",))
                    b.op("scalar", lambda E: E.activation(t["sigg"][:, :], t["sigg"][:, :], ACT.Ln,
                                                          bias=t["one1"][:, :], scale=1.0))
                    b.op("scalar", lambda E: E.activation(t["sigg"][:, :], t["sigg"][:, :], ACT.Exp,
                                                          scale=-1.0))
                    b.op("vector", lambda E: E.tensor_tensor(t["rg"][:, :], t["sigr"][:, :],
                                                             t["sigg"][:, :], ALU.mult),
                         waits=("scalar",))
                    for dc in range(4):
                        b.op("tensor", lambda E, dc=dc: E.matmul(
                            t["pB"][:, 128 * dc:128 * (dc + 1)],
                            t["dtwb"][:, 128 * dc:128 * (dc + 1)], t["dtrawb"][:, :],
                            start=True, stop=True, skip_group_check=True), waits=("vector",) if dc == 0 else ())
                    for c4 in range(4):
                        b.op("scalar", lambda E, c4=c4: E.activation(
                            t["ez"][:, 128 * c4:128 * (c4 + 1)], t["pB"][:, 128 * c4:128 * (c4 + 1)],
                            ACT.Exp, bias=t["sdtb"][:, c4:c4 + 1], scale=1.0),
                            waits=("tensor", "vector") if c4 == 0 else ("tensor",))
                    b.op("scalar", lambda E: E.activation(t["delta"][:, :], t["ez"][:, :], ACT.Ln,
                                                          bias=t["one1"][:, :], scale=1.0))
                    for c4 in range(4):
                        b.op("vector", lambda E, c4=c4: E.scalar_tensor_tensor(
                            t["ell"][:, 128 * c4:128 * (c4 + 1)],
                            t["delta"][:, 128 * c4:128 * (c4 + 1)],
                            t["bdneg"][:, c4:c4 + 1],
                            t["delta"][:, 128 * c4:128 * (c4 + 1)], ALU.mult, ALU.bypass),
                            waits=("scalar",) if c4 == 0 else ())
                    for c4 in range(4):
                        b.op("vector", lambda E, c4=c4: E.tensor_tensor_scan(
                            t["Pfull"][:, 128 * c4:128 * (c4 + 1)],
                            t["ell"][:, 128 * c4:128 * (c4 + 1)],
                            t["ell"][:, 128 * c4:128 * (c4 + 1)], 0.0, ALU.add, ALU.bypass))
                    for c4 in range(3):
                        b.op("vector", lambda E, c4=c4: E.scalar_tensor_tensor(
                            t["farg"][:, 128 * c4:128 * (c4 + 1)],
                            t["Pfull"][:, 128 * c4:128 * (c4 + 1)],
                            t["Pfull"][:, 128 * c4 + 127:128 * c4 + 128],
                            t["minus1f"][:, 128 * c4:128 * (c4 + 1)], ALU.subtract, ALU.mult))
                    b.op("vector", lambda E: E.tensor_copy(t["Pmid"][:, :],
                         ap(t["Pfull"], 8, [(512, 128), (16, 32)])))
                    b.op("vector", lambda E: E.tensor_copy(t["Pb64"][:, :],
                         ap(t["Pfull"], 63, [(512, 128), (128, 4)])))
                    b.op("vector", lambda E: E.tensor_copy(t["Pb32"][:, :],
                         ap(t["Pfull"], 31, [(512, 128), (64, 8)])))
                    b.op("vector", lambda E: E.tensor_copy(t["Pb16"][:, :],
                         ap(t["Pfull"], 15, [(512, 128), (32, 16)])))
                    b.op("vector", lambda E: E.tensor_copy(t["Asq"][:, :512],
                                                           t["Pfull"][:, :]))
                    b.op("vector", lambda E: E.scalar_tensor_tensor(
                        t["farg"][:, 384:512], t["Pfull"][:, 384:512],
                        t["Pfull"][:, 511:512],
                        t["minus1f"][:, 384:512], ALU.subtract, ALU.mult))
                    b.op("vector", lambda E: E.tensor_tensor(t["e1arg"][:, :], t["Pfull"][:, :],
                         ap(t["Pmid"], 0, [(32, 128), (1, 32), (0, 16)]), ALU.subtract))
                    b.op("vector", lambda E: E.tensor_tensor(t["s64"][:, :], t["Pfull"][:, :],
                         ap(t["Pb64"], 0, [(4, 128), (1, 4), (0, 128)]), ALU.subtract))
                    b.op("vector", lambda E: E.tensor_tensor(t["s32"][:, :], t["Pfull"][:, :],
                         ap(t["Pb32"], 0, [(8, 128), (1, 8), (0, 64)]), ALU.subtract))
                    b.op("vector", lambda E: E.tensor_tensor(t["s16"][:, :], t["Pfull"][:, :],
                         ap(t["Pb16"], 0, [(16, 128), (1, 16), (0, 32)]), ALU.subtract))
                    if astage < 4:
                        continue
                    b.op("scalar", lambda E: E.activation(t["expE"][:, :], t["Pfull"][:, :], ACT.Exp),
                         waits=("vector",))
                    b.op("scalar", lambda E: E.activation(t["expF"][:, :], t["farg"][:, :], ACT.Exp))
                    b.op("scalar", lambda E: E.activation(t["expE1"][:, :], t["e1arg"][:, :], ACT.Exp))
                    b.op("scalar", lambda E: E.activation(t["expE2"][:, :], t["e1arg"][:, :], ACT.Exp,
                                                          scale=-1.0))
                    b.op("scalar", lambda E: E.activation(t["xq64"][:, :],
                         ap(t["s64"], 64, [(512, 128), (128, 4), (1, 64)]), ACT.Exp))
                    b.op("scalar", lambda E: E.activation(t["xk64"][:, :],
                         ap(t["s64"], 0, [(512, 128), (128, 4), (1, 64)]), ACT.Exp, scale=-1.0))
                    b.op("scalar", lambda E: E.activation(t["xq32"][:, :],
                         ap(t["s32"], 32, [(512, 128), (64, 8), (1, 32)]), ACT.Exp))
                    b.op("scalar", lambda E: E.activation(t["xk32"][:, :],
                         ap(t["s32"], 0, [(512, 128), (64, 8), (1, 32)]), ACT.Exp, scale=-1.0))
                    b.op("scalar", lambda E: E.activation(t["xq16"][:, :],
                         ap(t["s16"], 16, [(512, 128), (32, 16), (1, 16)]), ACT.Exp))
                    b.op("scalar", lambda E: E.activation(t["xk16"][:, :],
                         ap(t["s16"], 0, [(512, 128), (32, 16), (1, 16)]), ACT.Exp, scale=-1.0))
                    b.op("vector", lambda E: E.tensor_tensor(t["qtb"][:, :], t["kev"][:, :],
                         t["expE"][:, :], ALU.mult), waits=("scalar",))
                    b.op("vector", lambda E: E.tensor_tensor(t["khf"][:, :], t["kev"][:, :],
                         t["expF"][:, :], ALU.mult))
                    b.op("vector", lambda E: E.tensor_tensor(t["ke1"][:, :], t["kev"][:, :],
                         t["expE1"][:, :], ALU.mult))
                    b.op("vector", lambda E: E.tensor_tensor(t["ke2"][:, :], t["kev"][:, :],
                         t["expE2"][:, :], ALU.mult))
                    b.op("vector", lambda E: E.tensor_tensor(t["kq64b"][:, :],
                         ap(t["kev"], 64, [(512, 128), (128, 4), (1, 64)]), t["xq64"][:, :], ALU.mult))
                    b.op("vector", lambda E: E.tensor_tensor(t["kk64b"][:, :],
                         ap(t["kev"], 0, [(512, 128), (128, 4), (1, 64)]), t["xk64"][:, :], ALU.mult))
                    b.op("vector", lambda E: E.tensor_tensor(t["kq32b"][:, :],
                         ap(t["kev"], 32, [(512, 128), (64, 8), (1, 32)]), t["xq32"][:, :], ALU.mult))
                    b.op("vector", lambda E: E.tensor_tensor(t["kk32b"][:, :],
                         ap(t["kev"], 0, [(512, 128), (64, 8), (1, 32)]), t["xk32"][:, :], ALU.mult))
                    b.op("vector", lambda E: E.tensor_tensor(t["kq16b"][:, :],
                         ap(t["kev"], 16, [(512, 128), (32, 16), (1, 16)]), t["xq16"][:, :], ALU.mult))
                    b.op("vector", lambda E: E.tensor_tensor(t["kk16b"][:, :],
                         ap(t["kev"], 0, [(512, 128), (32, 16), (1, 16)]), t["xk16"][:, :], ALU.mult))
                    dbg_dump("kev", t["kev"][:, :], (128, 512), when=(n == 0))
                    dbg_dump("vpb", t["vpb"][:, :], (128, 512), BF16, when=(n == 0))
                    dbg_dump("Pfull", t["Pfull"][:, :], (128, 512), when=(n == 0))
                    dbg_dump("expE", t["expE"][:, :], (128, 512), when=(n == 0))
                    dbg_dump("qtb", t["qtb"][:, :], (128, 512), BF16, when=(n == 0))
                    dbg_dump("rg", t["rg"][:, :], (128, 512), when=(n == 0))
                    dbg_dump("btok", t["btok"][:, :], (128, 1), when=(n == 0))
                    if astage < 5:
                        continue
                    b.op("vector", lambda E: E.tensor_copy(t["Sb"][:, :], t["Stile"][:, :]))
                    for (srcn, dstn) in (("qtb", "qtbB"), ("ke1", "ke1B"), ("ke2", "ke2B"),
                                         ("kq64b", "kq64B"), ("kk64b", "kk64B"),
                                         ("kq32b", "kq32B"), ("kk32b", "kk32B"),
                                         ("kq16b", "kq16B"), ("kk16b", "kk16B"),
                                         ("Sb", "SbB")):
                        b.dma("gpsimd", t[dstn][:, :], t[srcn][64:128, :], waits=("vector",))
                    if astage < 5.2:
                        continue
                    for p in range(4):
                        b.op("tensor", lambda E, p=p: E.transpose(
                            t["pF"][:, 128 * p:128 * (p + 1)],
                            t["khf"][:, 128 * p:128 * (p + 1)], t["sidentf"][:, :]),
                            waits=("vector", "scalar") if p == 0 else ())
                    if astage < 5.3:
                        continue
                    b.op("vector", lambda E: E.tensor_copy(t["khtm"][:, :], t["pF"][:, :512]),
                         waits=("tensor",))
                    if astage < 5.4:
                        continue
                    for p in range(4):
                        b.op("tensor", lambda E, p=p: E.transpose(
                            t["pF"][:, 128 * p:128 * (p + 1)],
                            t["vrawf"][:, 128 * p:128 * (p + 1)], t["sidentf"][:, :]),
                            waits=("vector",) if p == 0 else ())
                    if astage < 5.5:
                        continue
                    b.op("vector", lambda E: E.tensor_copy(t["vfm"][:, :], t["pF"][:, :512]),
                         waits=("tensor",))
                    if astage < 5.6:
                        continue
                    for rnd in range(2):
                        import os as _os3
                        for j in range(4):
                            h = 4 * rnd + j
                            p, hh = h // 2, h % 2
                            if _os3.environ.get("NOTP"):
                                tp = None
                            else:
                                tp = (64 * hh, 0)
                            if _os3.environ.get("BASE0"):
                                hh = 0
                                tp = None if _os3.environ.get("NOTP") else (0, 0)
                            ke2s = (t["ke2"][0:64, 128 * p:128 * (p + 1)] if hh == 0
                                    else t["ke2B"][:, 128 * p:128 * (p + 1)])
                            ke1s = (t["ke1"][0:64, 128 * p:128 * (p + 1)] if hh == 0
                                    else t["ke1B"][:, 128 * p:128 * (p + 1)])
                            b.op("tensor", lambda E, j=j, ke2s=ke2s, ke1s=ke1s: E.matmul(
                                t["pG"][:, 128 * j:128 * (j + 1)], ke2s, ke1s,
                                start=(j == 0), stop=False, skip_group_check=True),
                                waits=("vector", "gpsimd") if j == 0 else ())
                        for j in range(4 if astage >= 5.7 else 0):
                            h = 4 * rnd + j
                            p, hh = h // 2, h % 2
                            kkس = (t["kk64b"][0:64, 64 * p:64 * (p + 1)] if hh == 0
                                   else t["kk64B"][:, 64 * p:64 * (p + 1)])
                            kqس = (t["kq64b"][0:64, 64 * p:64 * (p + 1)] if hh == 0
                                   else t["kq64B"][:, 64 * p:64 * (p + 1)])
                            b.op("tensor", lambda E, j=j, a=kkس, c=kqس: E.matmul(
                                t["pG"][0:64, 128 * j + 64:128 * (j + 1)], a, c,
                                start=True, stop=False, skip_group_check=True))
                            for q2 in range(2 if astage >= 5.8 else 0):
                                a32 = (t["kk32b"][0:64, 32 * (2 * p + q2):32 * (2 * p + q2) + 32] if hh == 0
                                       else t["kk32B"][:, 32 * (2 * p + q2):32 * (2 * p + q2) + 32])
                                c32_ = (t["kq32b"][0:64, 32 * (2 * p + q2):32 * (2 * p + q2) + 32] if hh == 0
                                        else t["kq32B"][:, 32 * (2 * p + q2):32 * (2 * p + q2) + 32])
                                b.op("tensor", lambda E, j=j, q2=q2, a=a32, c=c32_: E.matmul(
                                    t["pG"][64 * q2:64 * q2 + 32,
                                            128 * j + 64 * q2 + 32:128 * j + 64 * q2 + 64],
                                    a, c, start=True, stop=False, skip_group_check=True,
                                    tile_position=(0, 64 * q2)))
                            for q in range(4 if astage >= 5.9 else 0):
                                a16 = (t["kk16b"][0:64, 16 * (4 * p + q):16 * (4 * p + q) + 16] if hh == 0
                                       else t["kk16B"][:, 16 * (4 * p + q):16 * (4 * p + q) + 16])
                                c16_ = (t["kq16b"][0:64, 16 * (4 * p + q):16 * (4 * p + q) + 16] if hh == 0
                                        else t["kq16B"][:, 16 * (4 * p + q):16 * (4 * p + q) + 16])
                                b.op("tensor", lambda E, j=j, q=q, a=a16, c=c16_: E.matmul(
                                    t["pG"][32 * q:32 * q + 16,
                                            128 * j + 32 * q + 16:128 * j + 32 * q + 32],
                                    a, c, start=True, stop=(j == 3 and q == 3), skip_group_check=True,
                                    tile_position=(0, 32 * q)))
                        for j in range(4):
                            b.op("vector", lambda E, j=j, rnd=rnd: E.select(
                                t["Amb"][:, 512 * rnd + 128 * j:512 * rnd + 128 * (j + 1)],
                                t["sutm"][:, :],
                                t["pG"][:, 128 * j:128 * (j + 1)], t["zero"][:, :]),
                                waits=("tensor",) if j == 0 else ())
                    if astage < 6:
                        continue
                    for h in range(8):
                        p, hh = h // 2, h % 2
                        Sbs = (t["Sb"][0:64, 64 * p:64 * (p + 1)] if hh == 0
                               else t["SbB"][:, 64 * p:64 * (p + 1)])
                        qts = (t["qtb"][0:64, 128 * p:128 * (p + 1)] if hh == 0
                               else t["qtbB"][:, 128 * p:128 * (p + 1)])
                        b.op("tensor", lambda E, p=p, hh=hh, a=Sbs, c=qts: E.matmul(
                            t["pH"][64 * hh:64 * (hh + 1), 128 * p:128 * (p + 1)],
                            a, c, start=True, stop=False, skip_group_check=True,
                            tile_position=(0, 64 * hh)),
                            waits=("vector", "gpsimd") if h == 0 else ())
                        b.op("tensor", lambda E, p=p, hh=hh, h=h: E.matmul(
                            t["pH"][64 * hh:64 * (hh + 1), 128 * p:128 * (p + 1)],
                            t["vpb"][:, 64 * h:64 * (h + 1)],
                            t["Amb"][:, 128 * h:128 * (h + 1)],
                            start=False, stop=True, skip_group_check=True, tile_position=(0, 64 * hh)))
                    b.op("vector", lambda E: E.tensor_tensor(t["kv"][:, :], t["kev"][:, :],
                         t["vfm"][:, :], ALU.mult))
                    b.op("vector", lambda E: E.tensor_tensor(t["bonus"][:, :], t["kv"][:, :],
                         ap(t["stf"], 0, [(4, 128), (1, 4), (0, 128)]), ALU.mult))
                    b.op("vector", lambda E: E.tensor_tensor(t["yst"][:, :], t["pH"][:, :512],
                         t["bonus"][:, :], ALU.add), waits=("tensor",))
                    b.op("vector", lambda E, n=n: E.tensor_tensor(
                        ap(t["ys"], 128 * n, [(4 * L, 128), (L, 4), (1, 128)]),
                        t["yst"][:, :], t["rg"][:, :], ALU.mult))
                    dbg_dump("Amb", t["Amb"][:, :], (128, 1024), BF16, when=(n == 0))
                    dbg_dump("ke1", t["ke1"][:, :], (128, 512), BF16, when=(n == 0))
                    dbg_dump("ke2", t["ke2"][:, :], (128, 512), BF16, when=(n == 0))
                    dbg_dump("kq64b", t["kq64b"][:, :], (128, 256), BF16, when=(n == 0))
                    dbg_dump("kk64b", t["kk64b"][:, :], (128, 256), BF16, when=(n == 0))
                    dbg_dump("yst", t["yst"][:, :], (128, 512), when=(n == 0))
                    if astage < 7:
                        continue
                    for h in range(8):
                        p, hh = h // 2, h % 2
                        b.op("tensor", lambda E, p=p, hh=hh, h=h: E.matmul(
                            t["pG"][64 * hh:64 * (hh + 1), 64 * p:64 * (p + 1)],
                            t["khtm"][:, 128 * p + 64 * hh:128 * p + 64 * (hh + 1)],
                            t["vpb"][:, 64 * h:64 * (h + 1)],
                            start=True, stop=True, skip_group_check=True, tile_position=(0, 64 * hh)),
                            waits=("vector",) if h == 0 else ())
                    for p in range(4):
                        b.op("vector", lambda E, p=p: E.scalar_tensor_tensor(
                            t["Stile"][:, 64 * p:64 * (p + 1)],
                            t["Stile"][:, 64 * p:64 * (p + 1)],
                            t["expE"][:, 128 * p + 127:128 * p + 128],
                            t["pG"][:, 64 * p:64 * (p + 1)], ALU.mult, ALU.add),
                            waits=("tensor",) if p == 0 else ())

            if debug:
                b.op("vector", lambda E: E.tensor_copy(t["Asq"][:, :256], t["Stile"][:, :]))
                dbg_dump("Sfinal", t["Asq"][:, :256], (128, 256))
            if skip_b:
                for g in range(NG):
                    for dc in range(8):
                        b.dma("gpsimd", out_h[128 * dc:128 * (dc + 1), 512 * g:512 * (g + 1)],
                              t["Asq"][:, :512], waits=("vector",))
                b.op("gpsimd", lambda E: E.memset(t["zero"][:1, :1], 0.0),
                     waits=("gpsimd",), inc_k=0)
                b.finalize(block)
                return nc
            esB = contextlib.ExitStack()
            with esB:
                esB1 = contextlib.ExitStack()
                esB1.__enter__()
                sb("w1qb", [128, 16384], BF16, esB1)
                sb("w2qb", [128, 16384], BF16, esB1)
                sb("xnqg", [128, 4096], BF16, esB1)
                sb("xnqg2", [128, 4096], BF16, esB1)
                sb("wstage2", [128, 1024], F32, esB1)
                sb("hq", [128, 8192], BF16, esB1)
                sb("y2g", [128, 1024], F32, esB1)
                psum("qA", [128, 512], F32, esB1)
                psum("qB", [128, 512], F32, esB1)
                psum("qC", [128, 512], F32, esB1)
                psum("qD", [128, 512], F32, esB1)
                sb("b1row", [1, 4096], BF16, esB1); sb("b2row", [1, 1024], BF16, esB1)
                sb("onesbf", [1, 512], BF16, esB1)
                b.op("vector", lambda E: E.memset(t["onesbf"][:, :], 1.0))
                b.dma("gpsimd", ap(t["b1row"], 0, [(4096, 1), (1, 4096)]),
                      ap(t["b1qT"], 0, [(128, 32), (1, 128)]), waits=("vector",))
                b.dma("gpsimd", ap(t["b2row"], 0, [(1024, 1), (1, 1024)]),
                      ap(t["b2qT"], 0, [(128, 8), (1, 128)]), waits=("vector",))

                xnq_gp_mark = b.mark("gpsimd")
                y2_done = {}
                for half in range(2):
                    wfree = {"qD": 0, "qC": b.mark("scalar")}
                    wsi = 0
                    wq_guard = [("tensor", b.mark("tensor"))] if half == 1 else []
                    first = True
                    wtiles = ([("w1", nt) for nt in range(16)]
                              + [("w2", (dc, qrt)) for dc in range(8) for qrt in range(2)])
                    wbuf_free = {0: 0, 1: 0}
                    ld_done = {}
                    for ti, (kind, idx) in enumerate(wtiles):
                        bi = ti % 2
                        buf = t["wstage"] if bi == 0 else t["wstage2"]
                        q = "sync" if bi == 0 else "gpsimd"
                        if kind == "w1":
                            nt = idx
                            srcap = w1_h[2048 * half + 128 * nt:2048 * half + 128 * (nt + 1), :]
                        else:
                            dc, qrt = idx
                            col = 2048 * half + 1024 * qrt
                            srcap = w2_h[128 * dc:128 * (dc + 1), col:col + 1024]
                        wa = [("tensor", wbuf_free[bi])]
                        if first:
                            wa = [("tensor", b.mark("tensor")), ("scalar", b.mark("scalar")),
                                  ("vector", b.mark("vector"))]
                        b.dma(q, buf[:, :1024], srcap, waits_abs=wa)
                        first = False
                        ld_done[ti] = (q, b.mark(q))
                        for rnd in range(2):
                            bank = "qD" if wsi % 2 == 0 else "qC"
                            wsi += 1
                            for j in range(4):
                                fc = 4 * rnd + j
                                b.op("tensor", lambda E, fc=fc, j=j, buf=buf, bank=bank: E.transpose(
                                    t[bank][:, 128 * j:128 * (j + 1)],
                                    buf[:, 128 * fc:128 * (fc + 1)], t["sidentf"][:, :]),
                                    waits_abs=([ld_done[ti], ("scalar", wfree[bank])]
                                               if j == 0 else []))
                            if kind == "w1":
                                out = ap(t["w1qb"], 2048 * 4 * rnd + 128 * nt,
                                         [(16384, 128), (2048, 4), (1, 128)])
                            else:
                                out = ap(t["w2qb"], 1024 * (8 * qrt + 4 * rnd) + 128 * dc,
                                         [(16384, 128), (1024, 4), (1, 128)])
                            b.op("scalar", lambda E, bank=bank: E.activation(t["q8"][:, :512],
                                                                  t[bank][:, :512], ACT.Copy),
                                 waits=("tensor", "vector"))
                            wfree[bank] = b.mark("scalar")
                            b.op("vector", lambda E, out=out: E.tensor_copy(out, t["q8"][:, :512]),
                                 waits=("scalar",), waits_abs=wq_guard)
                            wq_guard = []
                        wbuf_free[bi] = b.mark("tensor")
                    w_ready = b.mark("vector")
                    bank_free = {"qA": 0, "qB": 0}
                    qc_free = 0
                    if half == 0:
                        y2dma_mark = {0: 0, 1: 0}
                    xnqg_rd = {0: b.mark("tensor"), 1: b.mark("tensor")}
                    pair_mm_mark = {}

                    def xq_of(g):
                        return t["xnqg" if g % 2 == 0 else "xnqg2"]

                    def emit_loads(g):
                        gcol = slice(512 * g, 512 * (g + 1))
                        for fc in range(8):
                            b.dma("sync", xq_of(g)[:, 512 * fc:512 * (fc + 1)],
                                  xnq_s[128 * fc:128 * (fc + 1), gcol],
                                  waits_abs=([("tensor", xnqg_rd[g % 2]),
                                              ("gpsimd", xnq_gp_mark)] if fc == 0 else []))
                        return b.mark("sync")

                    def emit_mm(g, pair, ld_mark):
                        for s_, bank in ((0, "qA"), (1, "qB")):
                            nch = 2 * pair + s_
                            for fc in range(8):
                                b.op("tensor", lambda E, bank=bank, nch=nch, fc=fc, g=g: E.matmul(
                                    t[bank][:, :512],
                                    t["w1qb"][:, 2048 * fc + 128 * nch:2048 * fc + 128 * (nch + 1)],
                                    xq_of(g)[:, 512 * fc:512 * (fc + 1)],
                                    start=(fc == 0), stop=(fc == 7), skip_group_check=True),
                                    waits_abs=([("sync", ld_mark), ("vector", w_ready),
                                                ("scalar", bank_free[bank])] if fc == 0 else []))
                        pair_mm_mark[(g, pair)] = b.mark("tensor")

                    ld_marks = {0: emit_loads(0)}
                    emit_mm(0, 0, ld_marks[0])
                    ffn2_end = 0
                    for g in range(NG):
                        if g + 1 < NG and (g + 1) not in ld_marks:
                            ld_marks[g + 1] = emit_loads(g + 1)
                        for pair in range(8):
                            for s_, bank in ((0, "qA"), (1, "qB")):
                                nch = 2 * pair + s_
                                ow = ([("tensor", ffn2_end)]
                                      if (pair == 0 and s_ == 0) else [])
                                ng = 16 * half + nch
                                b.op("scalar", lambda E, bank=bank, ng=ng: E.activation(
                                    t["qb16"][:, :512], t[bank][:, :512], ACT.Relu,
                                    bias=t["Ar"][:, ng:ng + 1], scale=1.0),
                                    waits=("vector",),
                                    waits_abs=([("tensor", pair_mm_mark[(g, pair)])]
                                               if s_ == 0 else []))
                                bank_free[bank] = b.mark("scalar")
                                b.op("scalar", lambda E: E.activation(
                                    t["q8"][:, :512], t["qb16"][:, :512], ACT.Copy))
                                b.op("vector", lambda E, nch=nch: E.tensor_copy(
                                    t["hq"][:, 512 * nch:512 * (nch + 1)], t["q8"][:, :512]),
                                    waits=("scalar",), waits_abs=ow)
                            if pair + 1 < 8:
                                emit_mm(g, pair + 1, ld_marks[g])
                            elif g + 1 < NG:
                                emit_mm(g + 1, 0, ld_marks[g + 1])
                        xnqg_rd[g % 2] = b.mark("tensor")
                        dbg_dump("hq", t["hq"][:, :], (128, 8192), BF16,
                                 when=(half == 0 and g == 0))
                        hq_mark = b.mark("vector")
                        for dc in range(8):
                            for ncq in range(16):
                                b.op("tensor", lambda E, dc=dc, ncq=ncq: E.matmul(
                                    t["qC"][:, :512],
                                    t["w2qb"][:, 1024 * ncq + 128 * dc:1024 * ncq + 128 * (dc + 1)],
                                    t["hq"][:, 512 * ncq:512 * (ncq + 1)],
                                    start=(ncq == 0), stop=(half == 1 and ncq == 15), skip_group_check=True),
                                    waits_abs=([("vector", hq_mark), ("scalar", qc_free)]
                                               if ncq == 0 else [])),
                            if half == 0:
                                b.op("tensor", lambda E, dc=dc: E.matmul(
                                    t["qC"][:, :512], t["b2row"][0:1, 128 * dc:128 * (dc + 1)],
                                    t["onesbf"][:, :512], start=False, stop=True, skip_group_check=True),
                                    waits=("gpsimd",) if dc == 0 and g == 0 else ())
                            slot = dc % 2
                            b.op("scalar", lambda E, slot=slot: E.activation(
                                t["y2g"][:, 512 * slot:512 * (slot + 1)], t["qC"][:, :512], ACT.Copy),
                                waits=("tensor",),
                                waits_abs=[("gpsimd", y2dma_mark[slot])])
                            qc_free = b.mark("scalar")
                            tgt = y2_s if half == 0 else y2_s1
                            b.dma("gpsimd", tgt[128 * dc:128 * (dc + 1), 512 * g:512 * (g + 1)],
                                  t["y2g"][:, 512 * slot:512 * (slot + 1)], waits=("scalar",))
                            y2dma_mark[slot] = b.mark("gpsimd")
                        if half == 1:
                            y2_done[g] = b.mark("gpsimd")
                        ffn2_end = b.mark("tensor")


                esB1.__exit__(None, None, None)
                sb("wob", [128, 4096], BF16, esB)
                sb("y2f", [128, 4096], F32, esB); sb("y2f2", [128, 4096], F32, esB)
                sb("y2h", [128, 4096], F32, esB); sb("y2h2", [128, 4096], F32, esB)
                sb("xg", [128, 4096], F32, esB); sb("xg2", [128, 4096], F32, esB)
                sb("maskg", [128, 4096], BF16, esB)
                sb("outt", [128, 512], F32, esB)
                psum("pJ", [128, 512], F32, esB)
                psum("pK", [128, 512], F32, esB)
                for vc in range(4):
                    b.dma("sync", t["wstage"][:, :1024], wo_h[128 * vc:128 * (vc + 1), :],
                          waits=("vector", "tensor"))
                    b.op("vector", lambda E, vc=vc: E.tensor_copy(
                        t["wob"][:, 1024 * vc:1024 * (vc + 1)], t["wstage"][:, :1024]),
                        waits=("sync",))

                fb_free = {0: {"v": 0, "t": 0}, 1: {"v": 0, "t": 0}}
                ld_done = {}

                def fin_bufs(g):
                    s = "" if g % 2 == 0 else "2"
                    return (t["y2f" + s], t["y2h" + s], t["xg" + s], t["maskg"])

                def fin_loads(g):
                    y2fb, y2hb, xgb, mkb = fin_bufs(g)
                    fr = fb_free[g % 2]
                    wa = [("vector", fr["v"]), ("tensor", fr["t"]),
                          ("gpsimd", y2_done[g])]
                    b.dma("sync", ap(y2fb, 0, [(4096, 128), (512, 8), (1, 512)]),
                          ap(y2_s, 512 * g, [(L, 128), (128 * L, 8), (1, 512)]),
                          waits_abs=wa)

                    b.dma("scalar", ap(y2hb, 0, [(4096, 128), (512, 8), (1, 512)]),
                          ap(y2_s1, 512 * g, [(L, 128), (128 * L, 8), (1, 512)]),
                          waits_abs=wa)
                    b.dma("scalar", xgb[:, :2048],
                          ap(x_h, 512 * g * 1024, [(1024, 128), (131072, 2), (1, 1024)]))
                    b.dma("gpsimd", xgb[:, 2048:],
                          ap(x_h, 512 * g * 1024 + 262144, [(1024, 128), (131072, 2), (1, 1024)]),
                          waits_abs=wa)
                    ld_done[g] = {"sync": b.mark("sync"), "gpsimd": b.mark("gpsimd"),
                                  "scalar": b.mark("scalar")}

                fin_loads(0)
                mk_rd = 0
                for g in range(NG):
                    if g + 1 < NG:
                        fin_loads(g + 1)
                    y2fb, y2hb, xgb, mkb = fin_bufs(g)
                    b.dma("sync", ap(mkb, 0, [(4096, 128), (512, 8), (1, 512)]),
                          ap(maskt_h, 512 * g, [(L, 128), (128 * L, 8), (1, 512)]),
                          waits_abs=[("vector", mk_rd)])
                    mk_ld = b.mark("sync")
                    gcol = slice(512 * g, 512 * (g + 1))
                    for dc in range(8):
                        b.op("vector", lambda E, dc=dc, y2fb=y2fb, y2hb=y2hb: E.tensor_tensor(
                            y2fb[:, 512 * dc:512 * (dc + 1)],
                            y2fb[:, 512 * dc:512 * (dc + 1)],
                            y2hb[:, 512 * dc:512 * (dc + 1)], ALU.add),
                            waits_abs=([("sync", ld_done[g]["sync"]),
                                        ("scalar", ld_done[g]["scalar"])] if dc == 0 else []))
                    pre_rm_v = b.mark("vector")
                    dbg_dump("y2fin", y2fb[:, :], (128, 4096), when=(g == 0))
                    pj_free = pre_rm_v
                    pk_free = pre_rm_v
                    for dc in range(8):
                        for vc in range(4):
                            b.op("tensor", lambda E, dc=dc, vc=vc, g=g: E.matmul(
                                t["pJ"][:, :512],
                                t["wob"][:, 1024 * vc + 128 * dc:1024 * vc + 128 * (dc + 1)],
                                ap(t["ys"], L * vc + 512 * g, [(4 * L, 128), (1, 512)]),
                                start=(vc == 0), stop=(vc == 3), skip_group_check=True),
                                waits_abs=([("vector", pj_free)] if vc == 0 else []))
                        for tt_ in range(4):
                            b.op("tensor", lambda E, tt_=tt_, dc=dc, xgb=xgb: E.transpose(
                                t["pK"][:, 128 * tt_:128 * (tt_ + 1)],
                                xgb[:, 1024 * tt_ + 128 * dc:1024 * tt_ + 128 * (dc + 1)],
                                t["sidentf"][:, :]),
                                waits_abs=([("vector", pk_free),
                                            ("scalar", ld_done[g]["scalar"]),
                                            ("gpsimd", ld_done[g]["gpsimd"])] if tt_ == 0 else []))
                        mm_mark = b.mark("tensor")
                        b.op("vector", lambda E, dc=dc, y2fb=y2fb: E.tensor_copy(
                            t["qb16"][:, :512], y2fb[:, 512 * dc:512 * (dc + 1)]))
                        b.op("vector", lambda E, mkb=mkb, dc=dc: E.tensor_tensor(
                            t["outt"][:, :512], t["qb16"][:, :512],
                            mkb[:, 512 * dc:512 * (dc + 1)], ALU.mult),
                            waits_abs=[("tensor", mm_mark), ("sync", mk_ld),
                                       ("gpsimd", b.mark("gpsimd"))])
                        if debug and dc == 0:
                            b.op("vector", lambda E: E.tensor_copy(t["Asq"][:, :512], t["pJ"][:, :512]))
                            dbg_dump("att0", t["Asq"][:, :512], (128, 512), when=True)
                        b.op("vector", lambda E: E.tensor_tensor(
                            t["outt"][:, :512], t["outt"][:, :512], t["pJ"][:, :512], ALU.add))
                        pj_free = b.mark("vector")
                        b.op("vector", lambda E: E.tensor_tensor(
                            t["outt"][:, :512], t["outt"][:, :512], t["pK"][:, :512], ALU.add))
                        pk_free = b.mark("vector")
                        dbg_dump("outt0", t["outt"][:, :512], (128, 512),
                                 when=(g == 0 and dc == 0))
                        if debug and g == 0 and dc == 0:
                            b.op("vector", lambda E, y2fb=y2fb: E.tensor_copy(t["At1"][:, :512], y2fb[:, :512]))
                            dbg_dump("y2r0", t["At1"][:, :512], (128, 512), when=True)
                            b.op("vector", lambda E: E.tensor_copy(t["At1"][:, :512], t["pK"][:, :512]))
                            dbg_dump("xfm0", t["At1"][:, :512], (128, 512), when=True)
                            b.op("vector", lambda E, mkb=mkb: E.tensor_copy(t["At1"][:, :512], mkb[:, :512]))
                            dbg_dump("msk0", t["At1"][:, :512], (128, 512), when=True)
                        b.dma("gpsimd", out_h[128 * dc:128 * (dc + 1), gcol],
                              t["outt"][:, :512], waits=("vector",))
                    mk_rd = b.mark("vector")
                    fb_free[g % 2] = {"v": mk_rd, "t": b.mark("tensor")}
                b.op("gpsimd", lambda E: E.memset(t["zero"][:1, :1], 0.0),
                     waits=("gpsimd",), inc_k=0)

            b.finalize(block)
    return nc


_NC_CACHE = {}


def _bf16():
    import ml_dtypes
    return ml_dtypes.bfloat16


def kernel(L0_attn_x, L0_attn_W_sel, L0_attn_dt_W, L0_attn_dt_b, L0_attn_Wk,
           L0_attn_Wv, L0_attn_Wr, L0_attn_Wg, L0_attn_Wo, L0_attn_base_decay,
           L0_attn_time_first, L0_ffn_W1, L0_ffn_b1, L0_ffn_W2, L0_ffn_b2):
    import jax
    B, L, D = L0_attn_x.shape
    if L not in _NC_CACHE:
        _NC_CACHE[L] = build_nc(L)
    nc = _NC_CACHE[L]
    jax.config.update("jax_threefry_partitionable", True)
    with jax.default_device(jax.devices("cpu")[0]):
        keep = np.asarray(jax.random.bernoulli(
            jax.random.key(42, impl="threefry2x32"), 0.8, (B, L, D)))
    maskt = np.where(keep, np.float32(1.25), np.float32(0.0)).astype(_bf16())
    maskt = np.ascontiguousarray(np.swapaxes(maskt, 1, 2))
    f32 = np.float32
    common = {
        "wsel": np.ascontiguousarray(L0_attn_W_sel[:, :96], dtype=f32),
        "dtw": np.ascontiguousarray(L0_attn_dt_W, dtype=f32),
        "dtb": np.ascontiguousarray(np.asarray(L0_attn_dt_b, dtype=f32).reshape(4, 128).T),
        "wk": np.ascontiguousarray(L0_attn_Wk, dtype=f32),
        "wv": np.ascontiguousarray(L0_attn_Wv, dtype=f32),
        "wr": np.ascontiguousarray(L0_attn_Wr, dtype=f32),
        "wg": np.ascontiguousarray(L0_attn_Wg, dtype=f32),
        "wo": np.ascontiguousarray(L0_attn_Wo, dtype=f32),
        "bd": np.ascontiguousarray(np.asarray(L0_attn_base_decay, dtype=f32).reshape(512).reshape(4, 128).T),
        "tf": np.ascontiguousarray(np.asarray(L0_attn_time_first, dtype=f32).reshape(512).reshape(4, 128).T),
        "w1": np.ascontiguousarray(L0_ffn_W1, dtype=f32),
        "b1": np.ascontiguousarray(np.asarray(L0_ffn_b1, dtype=f32).reshape(32, 128).T),
        "w2": np.ascontiguousarray(L0_ffn_W2, dtype=f32),
        "b2": np.ascontiguousarray(np.asarray(L0_ffn_b2, dtype=f32).reshape(8, 128).T),
        "identf": np.eye(128, dtype=f32),
        "identb": np.eye(128, dtype=f32).astype(_bf16()),
        "utm": np.triu(np.ones((128, 128), dtype=np.uint8)),
    }
    in_maps = []
    for bi in range(B):
        m = dict(common)
        m["x"] = np.ascontiguousarray(L0_attn_x[bi], dtype=f32)
        m["maskt"] = np.ascontiguousarray(maskt[bi])
        in_maps.append(m)
    res = run_bass_kernel_spmd(nc, in_maps, core_ids=list(range(B)))
    outs = [np.ascontiguousarray(np.asarray(r["out"]).T) for r in res.results]
    return np.stack(outs).astype(np.float32)
